# version 05
# baseline (speedup 1.0000x reference)
import contextlib
import numpy as np
import concourse.bass as bass
import concourse.mybir as mybir
from concourse.bass_utils import run_bass_kernel_spmd

F32 = mybir.dt.float32
BF16 = mybir.dt.bfloat16
FP8 = mybir.dt.float8e4
AL = mybir.AluOpType
AF = mybir.ActivationFunctionType
AX = mybir.AxisListType.X

MAGIC = 3.0 * 2.0**22
LN2 = float(np.log(2.0))
NCORE = 8
T = 8192
TL = T // NCORE
D = 2048
H = 16
DH = 128
HL = H // NCORE
CH = 64
NCHK = T // CH
NCHKL = TL // CH
I0 = 8192
HID1 = 8192
SQD = float(np.sqrt(DH))
NTC = TL // 128
GATE_NORM = 16.0
CONST_VALS = [-2.0 * float(np.log(2.0)), 2.0 * float(np.log(2.0)),
              4.0 * float(np.log(2.0)), -4.0 * float(np.log(2.0))]

CFG = {"phases": ("l0a", "l0f", "l1a", "l1f"), "taps": ()}


class Sched:
    def __init__(self, nc):
        self.nc = nc
        self.steps = []
        self.cnt = {k: 0 for k in ("t", "v", "s", "g", "q", "gd", "sd", "cc")}
        self.sems = {}
        self.hiw = {}

    def make_sems(self, stack):
        for k in self.cnt:
            self.sems[k] = stack.enter_context(self.nc.semaphore("sem_" + k))

    def all(self):
        return {k: v for k, v in self.cnt.items() if v > 0}

    def emit_for(self, ek_engine, eng):
        for ek, fn, waits in self.steps:
            if ek != ek_engine:
                continue
            for sk, val in waits.items():
                hk = (ek, sk)
                if self.hiw.get(hk, 0) >= val:
                    continue
                self.hiw[hk] = val
                eng.wait_ge(self.sems[sk], val)
            fn(eng)


class KB:
    def __init__(self, cfg):
        self.cfg = cfg
        self.nc = bass.Bass("TRN2", target_bir_lowering=False, debug=False)
        self.S = Sched(self.nc)
        self.stack = contextlib.ExitStack()
        self._declare_dram()
        self._declare_sbuf()
        self.S.make_sems(self.stack)
        self.taps = []
        self._dynreg = {}

    def _declare_dram(self):
        nc = self.nc

        def din(name, shape, dt=F32):
            return nc.dram_tensor(name, list(shape), dt, kind="ExternalInput").ap()

        def mk(name, shape, dt):
            return nc.dram_tensor(name, list(shape), dt, kind="Internal").ap()

        self.x_in = din("x", (TL, D))
        self.y_out = nc.dram_tensor("y", [TL, D], F32, kind="ExternalOutput").ap()
        w = {}
        for n, shape in [("l0_wq", (D, D)), ("l0_wk", (D, D)), ("l0_wv", (D, D)),
                         ("l0_wo", (D, D)), ("l0_wg", (I0, D)), ("l0_wu", (I0, D)),
                         ("l0_wd", (D, I0)), ("l1_wq", (D, D)), ("l1_wk", (D, D)),
                         ("l1_wv", (D, D)), ("l1_wg", (D, D)), ("l1_wo", (D, D)),
                         ("l1_b", (5, D)), ("l1_wnorm", (1, D)),
                         ("l1_wgv", (2 * HID1, D)), ("l1_wout", (D, HID1))]:
            w[n] = din(n, shape)
        self.w = w
        c = {}
        for k, shape in [("eye", (128, 128)), ("ones", (128, 128)),
                         ("msk", (64, 5 * 512)), ("bmask", (128, 4 * 512)),
                         ("rsin", (TL, 64)), ("rcos", (TL, 64)),
                         ("sel", (128, NCORE))]:
            c[k] = din("c_" + k, shape)
        self.c = c
        self.w0_in = mk("w0_in", (4 * 256, D), BF16)
        self.w0_out = mk("w0_out", (NCORE * 4 * 256, D), BF16)
        self.w1_in = mk("w1_in", (2 * 1024, D), BF16)
        self.w1_out = mk("w1_out", (2 * I0, D), BF16)
        self.w1b_in = mk("w1b_in", (256, I0), BF16)
        self.w1b_out = mk("w1b_out", (D, I0), BF16)
        self.w2_in = mk("w2_in", (5 * 256, D), BF16)
        self.w2_out = mk("w2_out", (5 * D, D), BF16)
        self.w3_in = mk("w3_in", (D, D), BF16)
        self.w3_out = mk("w3_out", (2 * HID1, D), BF16)
        self.w3b_in = mk("w3b_in", (256, HID1), BF16)
        self.w3b_out = mk("w3b_out", (D, HID1), BF16)
        self.sc_in = mk("sc_in", (1, 8), F32)
        self.sc_out = mk("sc_out", (1, 8), F32)
        self.a0_in = mk("a0_in", (3 * D, TL), BF16)
        self.a0_out = mk("a0_out", (NCORE * 3 * D, TL), BF16)
        self.a1_in = mk("a1_in", (HL * DH, T), BF16)
        self.a1_out = mk("a1_out", (D, T), BF16)
        self.s1_in = mk("s1_in", (H * DH, 136), F32)
        self.s1_out = mk("s1_out", (NCORE * H * DH, 136), F32)
        self.xstream = mk("xstream", (TL, D), F32)
        self.u_spill = mk("u_spill", (TL, HID1), BF16)
        self.qkvg_sp = mk("qkvg_sp", (4 * TL, D), BF16)
        self.brow_sp = mk("brow_sp", (1, 6 * 2048), BF16)
        self.st_w0 = [mk("st_w0_%d" % i, (256, D), F32) for i in range(4)]
        self.st_wg = mk("st_wg", (1024, D), F32)
        self.st_wu = mk("st_wu", (1024, D), F32)
        self.st_wd = mk("st_wd", (256, I0), F32)
        self.st_wgv = mk("st_wgv", (2048, D), F32)
        self.st_wout = mk("st_wout", (256, I0), F32)
        self.hstage = mk("hstage", (128, NCORE * 3 * 1024), BF16)
        self.a1st = mk("a1st", (1, D, TL), BF16)
        self.mrow_sp = mk("mrow_sp", (HL, T), BF16)
        self.mk = mk

    def _declare_sbuf(self):
        nc, st = self.nc, self.stack

        def sbuf(name, shape, dt):
            return st.enter_context(nc.sbuf_tensor(name, list(shape), dt))

        def psum(name, shape):
            return st.enter_context(nc.psum_tensor(name, list(shape), F32))

        self.AF = sbuf("A_f32", (128, 16384), F32)
        self.AB = sbuf("A_bf", (128, 24576), BF16)
        self.SBB = sbuf("s_bf", (128, 2048), BF16)
        self.WS0 = sbuf("ws0", (128, 8192), BF16)
        self.WS1 = sbuf("ws1", (128, 8192), BF16)
        self.T0 = sbuf("t0", (128, 2048), F32)
        self.T1 = sbuf("t1", (128, 2048), F32)
        self.ROPE = sbuf("ropebuf", (128, 128), F32)
        self.EYEB = sbuf("eyeb", (128, 128), BF16)
        self.ONEF = sbuf("onef", (128, 128), F32)
        self.ONEB = sbuf("oneb", (128, 128), BF16)
        self.MSK = sbuf("mskS", (64, 5 * 512), BF16)
        self.BMASK = sbuf("bmaskb", (128, 4 * 512), BF16)
        self.SEL = sbuf("selS", (128, NCORE), F32)
        self.SM0 = sbuf("sm0", (128, 128), F32)
        self.SM1 = sbuf("sm1", (128, 128), F32)
        self.SM2 = sbuf("sm2", (128, 128), F32)
        self.SM3 = sbuf("sm3", (128, 128), F32)
        self.CONSTC = sbuf("constc", (128, 16), F32)
        for ci, val in enumerate(CONST_VALS):
            self.nc.const_aps.register(F32, val, self.CONSTC[:, 3 + ci:4 + ci])
        self.ROWV = sbuf("rowv", (1, 1024), F32)
        self.BIASR = sbuf("biasrow", (1, 2048), BF16)
        self.WNBB = sbuf("wnbb", (128, 2048), BF16)
        self.FQ = sbuf("fq", (128, 1024), FP8)
        self.P = [psum("p%d" % i, (128, 512)) for i in range(6)]
        self.PT = self.stack.enter_context(self.nc.psum_tensor("pt0", [128, 512], BF16))
        self.PT2 = self.stack.enter_context(self.nc.psum_tensor("pt1", [128, 512], BF16))

    def dma(self, qk, out, in_, after=None, **kw):
        S = self.S
        ek = {"q": "sync", "gd": "g", "sd": "s"}[qk]
        S.cnt[qk] += 16
        sem = S.sems[qk]

        def fn(eng, out=out, in_=in_, kw=kw):
            eng.dma_start(out=out, in_=in_, **kw).then_inc(sem, 16)
        S.steps.append((ek, fn, {k: v for k, v in (after or {}).items() if v > 0}))
        return S.cnt[qk]

    def dmaT(self, qk, out, in_, after=None):
        S = self.S
        ek = {"q": "sync", "gd": "g", "sd": "s"}[qk]
        S.cnt[qk] += 16
        sem = S.sems[qk]

        def fn(eng, out=out, in_=in_):
            eng.dma_start_transpose(out=out, in_=in_).then_inc(sem, 16)
        S.steps.append((ek, fn, {k: v for k, v in (after or {}).items() if v > 0}))
        return S.cnt[qk]

    def _dyn_off(self, eng, mult):
        key = (id(eng), mult)
        if key not in self._dynreg:
            pid = self.nc.partition_id()
            self._dynreg[key] = eng.compute_val(pid * mult)
        return self._dynreg[key]

    def dma_dyn(self, qk, out, src_tensor, mult, roff, nrows, cols, after=None):
        S = self.S
        ek = {"q": "sync", "gd": "g", "sd": "s"}[qk]
        S.cnt[qk] += 16
        sem = S.sems[qk]

        def fn(eng, out=out):
            off = self._dyn_off(eng, mult)
            nr = src_tensor.shape[0]
            base = src_tensor[roff:nr, cols] if roff else src_tensor[:, cols]
            eng.dma_start(out=out, in_=base[bass.ds(off, nrows), :]).then_inc(sem, 16)
        S.steps.append((ek, fn, {k: v for k, v in (after or {}).items() if v > 0}))
        return S.cnt[qk]

    def step(self, ek, after=None):
        S = self.S
        S.cnt[ek] += 1
        sem = S.sems[ek]

        def deco(fn):
            def wrapped(eng, fn=fn):
                last = fn(eng)
                assert last is not None
                last.then_inc(sem, 1)
            S.steps.append((ek, wrapped, {k: v for k, v in (after or {}).items() if v > 0}))
        return deco

    def cc(self, kind, op, ins, outs, after=None):
        S = self.S
        S.cnt["cc"] += 1
        sem = S.sems["cc"]

        def fn(eng, ins=ins, outs=outs):
            eng.collective_compute(
                kind, op, replica_groups=[list(range(NCORE))],
                ins=[a.opt() for a in ins], outs=[a.opt() for a in outs],
            ).then_inc(sem, 1)
        S.steps.append(("g", fn, {k: v for k, v in (after or {}).items() if v > 0}))
        return S.cnt["cc"]

    def tap(self, name, shape, dt, src_fn):
        if name not in self.cfg.get("taps", ()):
            return
        t = self.nc.dram_tensor("tap_" + name, list(shape), dt, kind="ExternalOutput").ap()
        self.dma("gd", t[:, :], src_fn(), after=self.S.all())

    def plan_setup(self):
        c, S = self.c, self.S
        self.dma("q", self.T1[:, 0:128], c["eye"][:, :])
        self.dma("q", self.ONEF[:, :], c["ones"][:, :])
        self.dma("q", self.AF[0:64, 0:2560], c["msk"][:, :])
        self.dma("q", self.SEL[:, :], c["sel"][:, :])
        self.dma("q", self.T0[:, 0:2048], c["bmask"][:, :])
        tq = self.dma("q", self.xstream[:, :], self.x_in[:, :])

        @self.step("v", after={"q": tq})
        def _(v):
            v.tensor_copy(out=self.EYEB[:, :], in_=self.T1[:, 0:128])
            v.tensor_copy(out=self.ONEB[:, :], in_=self.ONEF[:, :])
            v.tensor_copy(out=self.MSK[:, :], in_=self.AF[0:64, 0:2560])
            v.tensor_copy(out=self.BMASK[:, :], in_=self.T0[:, 0:2048])
            v.memset(self.CONSTC[:, 0:1], 2.0)
            v.memset(self.CONSTC[:, 1:2], 4.0)
            v.memset(self.CONSTC[:, 2:3], 0.0)
            for ci, val in enumerate(CONST_VALS):
                v.memset(self.CONSTC[:, 3 + ci:4 + ci], val)
            return v.drain()
        for mb in range(5):
            tq = self.dma("q", self.AF[0:1, mb * 2048:(mb + 1) * 2048],
                          self.w["l1_b"][mb:mb + 1, :],
                          after={"v": S.cnt["v"]} if mb == 0 else None)
        tq = self.dma("q", self.AF[0:1, 10240:12288], self.w["l1_wnorm"][:, :])
        for mth in range(6):
            @self.step("v", after={"q": tq} if mth == 0 else {"gd": S.cnt["gd"]})
            def _(v, mth=mth):
                return v.tensor_copy(out=self.BIASR[0:1, 0:2048],
                                     in_=self.AF[0:1, mth * 2048:(mth + 1) * 2048])
            self.dma("gd", self.brow_sp[0:1, mth * 2048:(mth + 1) * 2048],
                     self.BIASR[0:1, 0:2048], after={"v": S.cnt["v"]})

    def plan_weight_prep(self):
        S = self.S
        for i, wn in enumerate(["l0_wq", "l0_wk", "l0_wv", "l0_wo"]):
            self.dma_dyn("q", self.st_w0[i][:, :], self.w[wn], 256, 0, 256, slice(0, D))
        self.dma_dyn("q", self.st_wg[:, :], self.w["l0_wg"], 1024, 0, 1024, slice(0, D))
        self.dma_dyn("q", self.st_wu[:, :], self.w["l0_wu"], 1024, 0, 1024, slice(0, D))
        self.dma_dyn("q", self.st_wd[:, :], self.w["l0_wd"], 256, 0, 256, slice(0, I0))
        self.dma_dyn("q", self.st_wgv[:, :], self.w["l1_wgv"], 2048, 0, 2048, slice(0, D))
        t_stage = self.dma_dyn("q", self.st_wout[:, :], self.w["l1_wout"], 256, 0, 256,
                               slice(0, I0))
        self.t_stage = t_stage
        for i, wn in enumerate(["l0_wq", "l0_wk", "l0_wv", "l0_wo"]):
            for half in range(2):
                tq = self.dma("q", self.T0[:, 0:2048],
                              self.st_w0[i][half * 128:(half + 1) * 128, :],
                              after={"v": S.cnt["v"], "gd": S.cnt["gd"]})

                @self.step("v", after={"q": tq})
                def _(v):
                    return v.tensor_copy(out=self.SBB[:, 0:2048], in_=self.T0[:, 0:2048])
                r0 = i * 256 + half * 128
                self.dma("gd", self.w0_in[r0:r0 + 128, :], self.SBB[:, 0:2048],
                         after={"v": S.cnt["v"]})
        self.t_w0 = self.cc("AllGather", AL.bypass, [self.w0_in], [self.w0_out],
                            after={"gd": S.cnt["gd"]})

        jobs = []
        for i, wn in enumerate(["l0_wg", "l0_wu"]):
            for r in range(8):
                jobs.append((wn, 1024, r * 128, slice(0, 2048),
                             self.w1_in, i * 1024 + r * 128, slice(0, 2048)))
        for r in range(2):
            for p in range(4):
                jobs.append(("l0_wd", 256, r * 128, slice(p * 2048, (p + 1) * 2048),
                             self.w1b_in, r * 128, slice(p * 2048, (p + 1) * 2048)))
        stmap = {"l0_wg": self.st_wg, "l0_wu": self.st_wu, "l0_wd": self.st_wd,
                 "l1_wgv": self.st_wgv, "l1_wout": self.st_wout}
        for (wn, srows, roff, csl, dest, drow, dsl) in jobs:
            tq = self.dma("q", self.AF[:, 0:2048], stmap[wn][roff:roff + 128, csl],
                          after={"v": S.cnt["v"], "s": S.cnt["s"], "gd": S.cnt["gd"]})
            self._emit_e2m1(after={"q": tq})
            self.dma("gd", dest[drow:drow + 128, dsl], self.SBB[:, 0:2048],
                     after={"v": S.cnt["v"]})
        self.t_w1 = self.cc("AllGather", AL.bypass, [self.w1_in], [self.w1_out],
                            after={"gd": S.cnt["gd"]})
        self.t_w1b = self.cc("AllGather", AL.bypass, [self.w1b_in], [self.w1b_out],
                             after={"gd": S.cnt["gd"]})

        self._plan_l1w_prep()
        self._plan_ternary()

    def _emit_e2m1(self, after):
        nb = 64
        x3 = self.AF[:, 0:2048].rearrange("p (n b) -> p n b", b=32)

        @self.step("v", after=after)
        def _(v):
            v.tensor_reduce(out=self.SM0[:, 0:nb], in_=x3, axis=AX, op=AL.max,
                            apply_absolute_value=True)
            v.drain()
            v.tensor_scalar(out=self.SM0[:, 0:nb], in0=self.SM0[:, 0:nb],
                            scalar1=1e-30, scalar2=None, op0=AL.max)
            return v.drain()

        @self.step("s", after={"v": self.S.cnt["v"]})
        def _(s):
            return s.activation(out=self.SM1[:, 0:nb], in_=self.SM0[:, 0:nb], func=AF.Ln)

        @self.step("v", after={"s": self.S.cnt["s"]})
        def _(v):
            v.tensor_scalar(out=self.SM1[:, 0:nb], in0=self.SM1[:, 0:nb],
                            scalar1=1.0 / LN2, scalar2=-0.5, op0=AL.mult, op1=AL.add)
            v.drain()
            v.tensor_scalar(out=self.SM1[:, 0:nb], in0=self.SM1[:, 0:nb],
                            scalar1=MAGIC, scalar2=MAGIC, op0=AL.add, op1=AL.subtract)
            return v.drain()

        @self.step("s", after={"v": self.S.cnt["v"]})
        def _(s):
            s.activation(out=self.SM0[:, 0:nb], in_=self.SM1[:, 0:nb], func=AF.Exp,
                         scale=LN2, bias=-2.0 * LN2)
            return s.activation(out=self.SM2[:, 0:nb], in_=self.SM1[:, 0:nb], func=AF.Exp,
                                scale=-LN2, bias=2.0 * LN2)

        @self.step("v", after={"s": self.S.cnt["s"]})
        def _(v):
            v.drain()
            ibs = self.SM2[:, 0:nb].broadcast_to((128, nb, 32))
            bs = self.SM0[:, 0:nb].broadcast_to((128, nb, 32))
            yy = self.T1[:, 0:2048]
            y3 = yy.rearrange("p (n b) -> p n b", b=32)
            a = self.T0[:, 0:2048]
            t2 = self.AF[:, 4096:6144]
            t4 = self.AF[:, 6144:8192]
            c2 = self.CONSTC[:, 0:1].broadcast_to((128, 2048))
            c4 = self.CONSTC[:, 1:2].broadcast_to((128, 2048))
            v.tensor_tensor(out=y3, in0=x3, in1=ibs, op=AL.mult)
            v.drain()
            v.tensor_scalar(out=a, in0=yy, scalar1=-1.0, scalar2=None, op0=AL.mult)
            v.drain()
            v.tensor_tensor(out=a, in0=a, in1=yy, op=AL.max)
            v.drain()
            v.tensor_tensor(out=t2, in0=a, in1=c2, op=AL.is_ge)
            v.drain()
            v.tensor_tensor(out=t4, in0=a, in1=c4, op=AL.is_ge)
            v.drain()
            v.tensor_scalar(out=a, in0=t4, scalar1=-0.5, scalar2=2.0, op0=AL.mult, op1=AL.add)
            v.drain()
            v.scalar_tensor_tensor(out=a, in0=t2, scalar=-1.0, in1=a, op0=AL.mult, op1=AL.add)
            v.drain()
            v.tensor_tensor(out=yy, in0=yy, in1=a, op=AL.mult)
            v.drain()
            v.tensor_scalar(out=yy, in0=yy, scalar1=MAGIC, scalar2=MAGIC,
                            op0=AL.add, op1=AL.subtract)
            v.drain()
            v.tensor_scalar(out=a, in0=t2, scalar1=0.5, scalar2=0.5, op0=AL.mult, op1=AL.add)
            v.drain()
            v.tensor_tensor(out=a, in0=a, in1=t4, op=AL.add)
            v.drain()
            v.tensor_tensor(out=yy, in0=yy, in1=a, op=AL.mult)
            v.drain()
            v.tensor_scalar(out=yy, in0=yy, scalar1=6.0, scalar2=-6.0, op0=AL.min, op1=AL.max)
            v.drain()
            o3 = self.SBB[:, 0:2048].rearrange("p (n b) -> p n b", b=32)
            return v.tensor_tensor(out=o3, in0=y3, in1=bs, op=AL.mult)

    def _plan_l1w_prep(self):
        S = self.S
        self.l1wst = {}
        for i, wn in enumerate(["l1_wq", "l1_wk", "l1_wv", "l1_wg", "l1_wo"]):
            st = self.mk("l1wst_" + wn, (1, 2048, 256), F32)
            self.l1wst[wn] = st
            Sq = self.S
            Sq.cnt["q"] += 16
            sem = Sq.sems["q"]

            def stfn(eng, wn=wn, st=st, sem=sem):
                p1 = self._dyn_off(eng, 1)
                view = self.w[wn].rearrange("r (n c) -> n r c", c=256)
                eng.dma_start(out=st[:, :, :], in_=view[bass.ds(p1, 1), :, :]).then_inc(sem, 16)
            Sq.steps.append(("sync", stfn, {}))
        t_stage = S.cnt["q"]
        for i, wn in enumerate(["l1_wq", "l1_wk", "l1_wv", "l1_wg", "l1_wo"]):
            perm = wn in ("l1_wq", "l1_wk")
            for rt in range(16):
                ek_sem = S.sems["q"]
                S.cnt["q"] += 32 if perm else 16
                waits = {"v": S.cnt["v"], "t": S.cnt["t"]}

                def ldfn(eng, wn=wn, rt=rt, perm=perm, sem=ek_sem):
                    st = self.l1wst[wn]
                    if perm:
                        eng.dma_start(out=self.T0[0:64, 0:256],
                                      in_=st[0, rt * 128 + 0: rt * 128 + 128: 2, :]).then_inc(sem, 16)
                        eng.dma_start(out=self.T0[64:128, 0:256],
                                      in_=st[0, rt * 128 + 1: rt * 128 + 128: 2, :]).then_inc(sem, 16)
                    else:
                        eng.dma_start(out=self.T0[:, 0:256],
                                      in_=st[0, rt * 128: rt * 128 + 128, :]).then_inc(sem, 16)
                S.steps.append(("sync", ldfn, waits))
                tq = S.cnt["q"]

                @self.step("v", after={"q": tq})
                def _(v):
                    return v.tensor_copy(out=self.SBB[:, 0:256], in_=self.T0[:, 0:256])

                @self.step("t", after={"v": S.cnt["v"]})
                def _(t):
                    t.transpose(out=self.PT[:, 0:128], in_=self.SBB[:, 0:128],
                                identity=self.EYEB[:, :])
                    return t.transpose(out=self.PT[:, 128:256], in_=self.SBB[:, 128:256],
                                       identity=self.EYEB[:, :])

                @self.step("v", after={"t": S.cnt["t"]})
                def _(v):
                    return v.tensor_copy(out=self.SBB[:, 256:512], in_=self.PT[:, 0:256])
                tv = S.cnt["v"]
                self.dma("gd", self.w2_in[i * 256: i * 256 + 128, rt * 128: rt * 128 + 128],
                         self.SBB[:, 256:384], after={"v": tv})
                self.dma("gd", self.w2_in[i * 256 + 128: i * 256 + 256, rt * 128: rt * 128 + 128],
                         self.SBB[:, 384:512], after={"v": tv})
        self.t_w2 = self.cc("AllGather", AL.bypass, [self.w2_in], [self.w2_out],
                            after={"gd": S.cnt["gd"]})

    def _plan_ternary(self):
        S = self.S

        @self.step("v", after={"v": S.cnt["v"]})
        def _(v):
            v.memset(self.SM3[:, 0:2], 0.0)
            return v.drain()
        jobs = [("l1_wgv", 2048, r * 128, slice(0, 2048), 0, self.w3_in, r * 128, slice(0, 2048))
                for r in range(16)]
        for r in range(2):
            for p in range(4):
                jobs.append(("l1_wout", 256, r * 128, slice(p * 2048, (p + 1) * 2048), 1,
                             self.w3b_in, r * 128, slice(p * 2048, (p + 1) * 2048)))
        stmap = {"l1_wgv": self.st_wgv, "l1_wout": self.st_wout}
        for (wn, srows, roff, csl, col, dest, drow, dsl) in jobs:
            tq = self.dma("q", self.AF[:, 0:2048], stmap[wn][roff:roff + 128, csl],
                          after={"v": S.cnt["v"]})

            @self.step("v", after={"q": tq})
            def _(v, col=col):
                v.tensor_reduce(out=self.SM1[:, 4:5], in_=self.AF[:, 0:2048], axis=AX,
                                op=AL.add, apply_absolute_value=True)
                v.drain()
                v.tensor_tensor(out=self.SM3[:, col:col + 1], in0=self.SM3[:, col:col + 1],
                                in1=self.SM1[:, 4:5], op=AL.add)
                return v.drain()

        @self.step("v", after={"v": S.cnt["v"]})
        def _(v):
            return v.tensor_copy(out=self.SBB[:, 0:2], in_=self.SM3[:, 0:2])

        @self.step("t", after={"v": S.cnt["v"]})
        def _(t):
            return t.matmul(out=self.P[0][0:1, 0:2], lhsT=self.ONEB[:, 0:1],
                            rhs=self.SBB[:, 0:2], start=True, stop=True)

        @self.step("v", after={"t": S.cnt["t"]})
        def _(v):
            return v.tensor_copy(out=self.ROWV[0:1, 0:2], in_=self.P[0][0:1, 0:2])
        self.dma("gd", self.sc_in[:, 0:2], self.ROWV[0:1, 0:2], after={"v": S.cnt["v"]})
        t_ar = self.cc("AllReduce", AL.add, [self.sc_in], [self.sc_out],
                       after={"gd": S.cnt["gd"]})
        tq = self.dma("q", self.ROWV[0:1, 4:6], self.sc_out[:, 0:2], after={"cc": t_ar})

        @self.step("v", after={"q": tq})
        def _(v):
            v.tensor_scalar(out=self.ROWV[0:1, 6:7], in0=self.ROWV[0:1, 4:5],
                            scalar1=1.0 / (2.0 * HID1 * D), scalar2=1e-5,
                            op0=AL.mult, op1=AL.max)
            v.tensor_scalar(out=self.ROWV[0:1, 7:8], in0=self.ROWV[0:1, 5:6],
                            scalar1=1.0 / (1.0 * D * HID1), scalar2=1e-5,
                            op0=AL.mult, op1=AL.max)
            v.drain()
            v.reciprocal(out=self.ROWV[0:1, 8:9], in_=self.ROWV[0:1, 6:7])
            v.drain()
            v.reciprocal(out=self.ROWV[0:1, 9:10], in_=self.ROWV[0:1, 7:8])
            return v.drain()

        @self.step("t", after={"v": S.cnt["v"]})
        def _(t):
            return t.matmul(out=self.P[0][:, 0:4], lhsT=self.ONEF[0:1, :],
                            rhs=self.ROWV[0:1, 6:10], start=True, stop=True)

        @self.step("v", after={"t": S.cnt["t"]})
        def _(v):
            v.tensor_copy(out=self.SM3[:, 4:6], in_=self.P[0][:, 0:2])
            v.tensor_copy(out=self.SM3[:, 6:8], in_=self.P[0][:, 2:4])
            return v.drain()
        for (wn, srows, roff, csl, col, dest, drow, dsl) in jobs:
            tq = self.dma("q", self.AF[:, 0:2048], stmap[wn][roff:roff + 128, csl],
                          after={"v": S.cnt["v"], "gd": S.cnt["gd"]})

            @self.step("v", after={"q": tq})
            def _(v, col=col):
                yy = self.T1[:, 0:2048]
                v.tensor_scalar(out=yy, in0=self.AF[:, 0:2048],
                                scalar1=self.SM3[:, 6 + col:7 + col], scalar2=MAGIC,
                                op0=AL.mult, op1=AL.add)
                v.tensor_scalar(out=yy, in0=yy, scalar1=MAGIC, scalar2=1.0,
                                op0=AL.subtract, op1=AL.min)
                return v.tensor_scalar(out=self.SBB[:, 0:2048], in0=yy, scalar1=-1.0,
                                       scalar2=None, op0=AL.max)
            self.dma("gd", dest[drow:drow + 128, dsl], self.SBB[:, 0:2048],
                     after={"v": S.cnt["v"]})
        self.t_w3 = self.cc("AllGather", AL.bypass, [self.w3_in], [self.w3_out],
                            after={"gd": S.cnt["gd"]})
        self.t_w3b = self.cc("AllGather", AL.bypass, [self.w3b_in], [self.w3b_out],
                             after={"gd": S.cnt["gd"]})

    def _emit_mxact(self, x_ap, nb, out_bf, fq_cols, after):
        ncol = nb * 32
        x3 = x_ap.rearrange("p (n b) -> p n b", b=32)
        cb = slice(64, 64 + nb)

        @self.step("v", after=after)
        def _(v):
            v.tensor_reduce(out=self.SM0[:, cb], in_=x3, axis=AX, op=AL.max,
                            apply_absolute_value=True)
            v.drain()
            v.tensor_scalar(out=self.SM0[:, cb], in0=self.SM0[:, cb],
                            scalar1=1e-30, scalar2=None, op0=AL.max)
            return v.drain()

        @self.step("s", after={"v": self.S.cnt["v"]})
        def _(s):
            return s.activation(out=self.SM1[:, cb], in_=self.SM0[:, cb], func=AF.Ln)

        @self.step("v", after={"s": self.S.cnt["s"]})
        def _(v):
            v.tensor_scalar(out=self.SM1[:, cb], in0=self.SM1[:, cb],
                            scalar1=1.0 / LN2, scalar2=-0.5, op0=AL.mult, op1=AL.add)
            v.drain()
            v.tensor_scalar(out=self.SM1[:, cb], in0=self.SM1[:, cb],
                            scalar1=MAGIC, scalar2=MAGIC, op0=AL.add, op1=AL.subtract)
            return v.drain()

        @self.step("s", after={"v": self.S.cnt["v"]})
        def _(s):
            s.activation(out=self.SM0[:, cb], in_=self.SM1[:, cb], func=AF.Exp,
                         scale=LN2, bias=4.0 * LN2)
            return s.activation(out=self.SM2[:, cb], in_=self.SM1[:, cb], func=AF.Exp,
                                scale=-LN2, bias=-4.0 * LN2)

        @self.step("v", after={"s": self.S.cnt["s"]})
        def _(v):
            v.drain()
            ibs = self.SM2[:, cb].broadcast_to((128, nb, 32))
            bs = self.SM0[:, cb].broadcast_to((128, nb, 32))
            fq = self.FQ[:, fq_cols]
            f3 = fq.rearrange("p (n b) -> p n b", b=32)
            v.tensor_tensor(out=f3, in0=x3, in1=ibs, op=AL.mult)
            v.drain()
            v.tensor_scalar(out=fq, in0=fq, scalar1=0.1171875, scalar2=-0.1171875,
                            op0=AL.min, op1=AL.max)
            v.drain()
            o3 = out_bf.rearrange("p (n b) -> p n b", b=32)
            return v.tensor_tensor(out=o3, in0=f3, in1=bs, op=AL.mult)

    def trans_tiles(self, pairs, after):
        S = self.S
        n = len(pairs)
        for i0 in range(0, n, 2):
            grp = pairs[i0:i0 + 2]

            @self.step("t", after={"v": S.cnt["v"], "t": S.cnt["t"]} if i0 else dict(after))
            def _(t, grp=grp, i0=i0):
                last = None
                for j, (srcap, _d) in enumerate(grp):
                    pp = self.PT if j == 0 else self.PT2
                    last = t.transpose(out=pp[:, 0:128], in_=srcap,
                                       identity=self.EYEB[:, :])
                return last

            @self.step("v", after={"t": S.cnt["t"]})
            def _(v, grp=grp):
                last = None
                for j, (_s, dstap) in enumerate(grp):
                    pp = self.PT if j == 0 else self.PT2
                    last = v.tensor_copy(out=dstap, in_=pp[:, 0:128])
                return last
        return S.cnt["v"]

    def ln_write(self, t1_ap, dst_hbm_rows, after, extra_scale=None):
        S = self.S

        @self.step("v", after=after)
        def _(v):
            v.tensor_reduce(out=self.SM1[:, 8:9], in_=t1_ap, axis=AX, op=AL.add)
            v.tensor_tensor(out=self.T0[:, 0:D], in0=t1_ap, in1=t1_ap, op=AL.mult)
            v.tensor_reduce(out=self.SM1[:, 9:10], in_=self.T0[:, 0:D], axis=AX, op=AL.add)
            v.drain()
            v.tensor_scalar(out=self.SM1[:, 8:9], in0=self.SM1[:, 8:9],
                            scalar1=1.0 / D, scalar2=None, op0=AL.mult)
            v.tensor_scalar(out=self.SM1[:, 9:10], in0=self.SM1[:, 9:10],
                            scalar1=1.0 / D, scalar2=None, op0=AL.mult)
            v.drain()
            v.tensor_tensor(out=self.SM1[:, 10:11], in0=self.SM1[:, 8:9],
                            in1=self.SM1[:, 8:9], op=AL.mult)
            v.drain()
            v.tensor_tensor(out=self.SM1[:, 10:11], in0=self.SM1[:, 9:10],
                            in1=self.SM1[:, 10:11], op=AL.subtract)
            v.drain()
            v.tensor_scalar(out=self.SM1[:, 10:11], in0=self.SM1[:, 10:11],
                            scalar1=1e-5, scalar2=None, op0=AL.add)
            return v.drain()

        @self.step("s", after={"v": S.cnt["v"]})
        def _(s):
            return s.activation(out=self.SM1[:, 11:12], in_=self.SM1[:, 10:11], func=AF.Sqrt)

        @self.step("v", after={"s": S.cnt["s"]})
        def _(v):
            v.reciprocal(out=self.SM1[:, 12:13], in_=self.SM1[:, 11:12])
            v.drain()
            return v.tensor_scalar(out=self.T1[:, 0:D], in0=t1_ap,
                                   scalar1=self.SM1[:, 8:9], scalar2=self.SM1[:, 12:13],
                                   op0=AL.subtract, op1=AL.mult)
        tq = self.dma("q", dst_hbm_rows, self.T1[:, 0:D], after={"v": S.cnt["v"]})
        return tq

    def plan_l0_ffn(self):
        S = self.S
        start = S.all()
        for tc in range(NTC):
            tq = self.dma("q", self.T0[:, 0:D], self.xstream[tc * 128:(tc + 1) * 128, :],
                          after=start if tc == 0 else {"v": S.cnt["v"]})

            @self.step("v", after={"q": tq})
            def _(v):
                return v.tensor_copy(out=self.AF[:, 0:D], in_=self.T0[:, 0:D])
            self._emit_mxact(self.AF[:, 0:1024], 32, self.SBB[:, 0:1024], slice(0, 1024),
                             after={"v": S.cnt["v"]})
            self._emit_mxact(self.AF[:, 1024:2048], 32, self.SBB[:, 1024:2048], slice(0, 1024),
                             after={"v": S.cnt["v"]})
            pairs = [(self.SBB[:, dc * 128:(dc + 1) * 128],
                      self.AB[:, dc * 1024 + tc * 128: dc * 1024 + tc * 128 + 128])
                     for dc in range(16)]
            self.trans_tiles(pairs, after={"v": S.cnt["v"]})
        t_xq = S.all()

        WST = self.WS0
        for os in range(8):
            tw = 0
            for dc in range(16):
                for j in range(2):
                    buf = self.WS0 if dc < 8 else self.WS1
                    col = (dc % 8) * 1024 + j * 512
                    tw = self.dmaT("q", buf[:, col:col + 512],
                                   self.w1_out[os * 2048 + j * 512: os * 2048 + (j + 1) * 512,
                                               dc * 128:(dc + 1) * 128],
                                   after={"t": S.cnt["t"], "v": S.cnt["v"]} if (dc == 0 and j == 0) else None)
            for tc in range(NTC):
                @self.step("t", after={"q": tw} if tc == 0 else {"s": S.cnt["s"]})
                def _(t, tc=tc):
                    last = None
                    for half in range(2):
                        pp = self.P[half]
                        for dc in range(16):
                            buf = self.WS0 if dc < 8 else self.WS1
                            col = (dc % 8) * 1024 + half * 512
                            last = t.matmul(out=pp[:, :],
                                            lhsT=self.AB[:, dc * 1024 + tc * 128: dc * 1024 + tc * 128 + 128],
                                            rhs=buf[:, col:col + 512],
                                            start=(dc == 0), stop=(dc == 15))
                    return last

                @self.step("s", after={"t": S.cnt["t"]})
                def _(s, tc=tc):
                    s.activation(out=self.AB[:, 16384 + tc * 1024: 16384 + tc * 1024 + 512],
                                 in_=self.P[0][:, :], func=AF.Silu)
                    return s.activation(out=self.AB[:, 16384 + tc * 1024 + 512: 16384 + (tc + 1) * 1024],
                                        in_=self.P[1][:, :], func=AF.Silu)
            tw = 0
            for dc in range(16):
                for j in range(2):
                    buf = self.WS0 if dc < 8 else self.WS1
                    col = (dc % 8) * 1024 + j * 512
                    tw = self.dmaT("q", buf[:, col:col + 512],
                                   self.w1_out[os * 2048 + 1024 + j * 512: os * 2048 + 1024 + (j + 1) * 512,
                                               dc * 128:(dc + 1) * 128],
                                   after={"t": S.cnt["t"], "s": S.cnt["s"]} if (dc == 0 and j == 0) else None)
            for tc in range(NTC):
                @self.step("t", after={"q": tw} if tc == 0 else {"v": S.cnt["v"]})
                def _(t, tc=tc):
                    last = None
                    for half in range(2):
                        pp = self.P[half]
                        for dc in range(16):
                            buf = self.WS0 if dc < 8 else self.WS1
                            col = (dc % 8) * 1024 + half * 512
                            last = t.matmul(out=pp[:, :],
                                            lhsT=self.AB[:, dc * 1024 + tc * 128: dc * 1024 + tc * 128 + 128],
                                            rhs=buf[:, col:col + 512],
                                            start=(dc == 0), stop=(dc == 15))
                    return last

                @self.step("v", after={"t": S.cnt["t"]})
                def _(v, tc=tc):
                    sg = self.AB[:, 16384 + tc * 1024: 16384 + (tc + 1) * 1024]
                    v.tensor_tensor(out=self.T1[:, 0:512], in0=self.P[0][:, :],
                                    in1=sg[:, 0:512], op=AL.mult)
                    return v.tensor_tensor(out=self.T1[:, 512:1024], in0=self.P[1][:, :],
                                           in1=sg[:, 512:1024], op=AL.mult)
                self._emit_mxact(self.T1[:, 0:1024], 32, self.SBB[:, 0:1024],
                                 slice(0, 1024), after={"v": S.cnt["v"]})
                pairs = [(self.SBB[:, i * 128:(i + 1) * 128],
                          self.AB[:, 16384 + tc * 1024 + i * 128: 16384 + tc * 1024 + (i + 1) * 128])
                         for i in range(8)]
                self.trans_tiles(pairs, after={"v": S.cnt["v"]})
            tw = 0
            for ou in range(8):
                for dsl in range(4):
                    tw = self.dmaT("q", self.WS0[:, ou * 1024 + dsl * 512 - (ou // 2) * 0: ou * 1024 + dsl * 512 + 512]
                                   if False else self.WS0[:, (ou % 4) * 2048 + dsl * 512:(ou % 4) * 2048 + dsl * 512 + 512]
                                   if ou < 4 else self.WS1[:, (ou % 4) * 2048 + dsl * 512:(ou % 4) * 2048 + dsl * 512 + 512],
                                   self.w1b_out[dsl * 512:(dsl + 1) * 512,
                                                os * 1024 + ou * 128: os * 1024 + (ou + 1) * 128],
                                   after={"t": S.cnt["t"]} if (ou == 0 and dsl == 0) else None)
            for tc in range(NTC):
                @self.step("t", after={"q": tw} if tc == 0 else {"v": S.cnt["v"]})
                def _(t, tc=tc):
                    last = None
                    for dsl in range(4):
                        pp = self.P[2 + dsl]
                        for ou in range(8):
                            buf = self.WS0 if ou < 4 else self.WS1
                            col = (ou % 4) * 2048 + dsl * 512
                            last = t.matmul(out=pp[:, :],
                                            lhsT=self.AB[:, 16384 + tc * 1024 + ou * 128: 16384 + tc * 1024 + (ou + 1) * 128],
                                            rhs=buf[:, col:col + 512],
                                            start=(ou == 0), stop=(ou == 7))
                    return last

                @self.step("v", after={"t": S.cnt["t"]})
                def _(v, tc=tc, os=os):
                    last = None
                    for dsl in range(4):
                        dst = self.AF[:, tc * 2048 + dsl * 512: tc * 2048 + (dsl + 1) * 512]
                        if os == 0:
                            last = v.tensor_copy(out=dst, in_=self.P[2 + dsl][:, :])
                        else:
                            last = v.tensor_tensor(out=dst, in0=dst, in1=self.P[2 + dsl][:, :],
                                                   op=AL.add)
                    return last

        for tc in range(NTC):
            tq = self.dma("q", self.T0[:, 0:D], self.xstream[tc * 128:(tc + 1) * 128, :],
                          after={"v": S.cnt["v"]})

            @self.step("v", after={"q": tq})
            def _(v, tc=tc):
                return v.tensor_tensor(out=self.T1[:, 0:D], in0=self.AF[:, tc * 2048:(tc + 1) * 2048],
                                       in1=self.T0[:, 0:D], op=AL.add)

            @self.step("v", after={"v": S.cnt["v"]})
            def _(v):
                return v.tensor_copy(out=self.AF[:, 0:D], in_=self.T1[:, 0:D])
            self.ln_write(self.AF[:, 0:D], self.xstream[tc * 128:(tc + 1) * 128, :],
                          after={"v": S.cnt["v"]})

    def plan_l1_ffn(self, last_block=True):
        S = self.S
        start = S.all()
        tb = self.dma("q", self.BIASR[0:1, 0:2048],
                      self.brow_sp[0:1, 5 * 2048:6 * 2048], after=start)

        @self.step("t", after={"q": tb})
        def _(t):
            last = None
            for j in range(4):
                last = t.matmul(out=self.P[j][:, :], lhsT=self.ONEB[0:1, :],
                                rhs=self.BIASR[0:1, j * 512:(j + 1) * 512],
                                start=True, stop=True)
            return last

        @self.step("v", after={"t": S.cnt["t"]})
        def _(v):
            last = None
            for j in range(4):
                last = v.tensor_copy(out=self.WNBB[:, j * 512:(j + 1) * 512],
                                     in_=self.P[j][:, :])
            return last

        for tc in range(NTC):
            tq = self.dma("q", self.T0[:, 0:D], self.xstream[tc * 128:(tc + 1) * 128, :],
                          after={"v": S.cnt["v"]})

            @self.step("v", after={"q": tq})
            def _(v, tc=tc):
                v.tensor_tensor(out=self.T1[:, 0:D], in0=self.T0[:, 0:D],
                                in1=self.T0[:, 0:D], op=AL.mult)
                v.tensor_reduce(out=self.SM1[:, 0:1], in_=self.T1[:, 0:D], axis=AX, op=AL.add)
                v.drain()
                v.tensor_scalar(out=self.SM1[:, 0:1], in0=self.SM1[:, 0:1],
                                scalar1=1.0 / D, scalar2=1e-5, op0=AL.mult, op1=AL.add)
                return v.drain()

            @self.step("s", after={"v": S.cnt["v"]})
            def _(s):
                return s.activation(out=self.SM1[:, 1:2], in_=self.SM1[:, 0:1], func=AF.Sqrt)

            @self.step("v", after={"s": S.cnt["s"]})
            def _(v, tc=tc):
                v.reciprocal(out=self.SM1[:, 2:3], in_=self.SM1[:, 1:2])
                v.drain()
                v.tensor_scalar(out=self.T1[:, 0:D], in0=self.T0[:, 0:D],
                                scalar1=self.SM1[:, 2:3], scalar2=None, op0=AL.mult)
                v.tensor_tensor(out=self.T1[:, 0:D], in0=self.T1[:, 0:D],
                                in1=self.WNBB[:, :], op=AL.mult)
                v.tensor_reduce(out=self.SM1[:, 3:4], in_=self.T1[:, 0:D], axis=AX,
                                op=AL.max, apply_absolute_value=True)
                v.drain()
                v.tensor_scalar(out=self.SM1[:, 3:4], in0=self.SM1[:, 3:4],
                                scalar1=1e-5, scalar2=None, op0=AL.max)
                v.drain()
                v.reciprocal(out=self.SM1[:, 4:5], in_=self.SM1[:, 3:4])
                v.drain()
                v.tensor_scalar(out=self.SM1[:, 5:6], in0=self.SM1[:, 4:5],
                                scalar1=127.0, scalar2=None, op0=AL.mult)
                v.tensor_scalar(out=self.SM1[:, 6:7], in0=self.SM1[:, 3:4],
                                scalar1=1.0 / 127.0, scalar2=None, op0=AL.mult)
                v.drain()
                v.tensor_tensor(out=self.SM2[:, 16 + tc:17 + tc], in0=self.SM1[:, 6:7],
                                in1=self.SM3[:, 4:5], op=AL.mult)
                v.tensor_copy(out=self.SM2[:, 24 + tc:25 + tc], in_=self.SM1[:, 6:7])
                v.drain()
                v.tensor_scalar(out=self.T0[:, 0:D], in0=self.T1[:, 0:D],
                                scalar1=self.SM1[:, 5:6], scalar2=MAGIC,
                                op0=AL.mult, op1=AL.add)
                v.tensor_scalar(out=self.T0[:, 0:D], in0=self.T0[:, 0:D],
                                scalar1=MAGIC, scalar2=127.0, op0=AL.subtract, op1=AL.min)
                return v.tensor_scalar(out=self.SBB[:, 0:D], in0=self.T0[:, 0:D],
                                       scalar1=-128.0, scalar2=None, op0=AL.max)
            pairs = [(self.SBB[:, dc * 128:(dc + 1) * 128],
                      self.AB[:, dc * 1024 + tc * 128: dc * 1024 + tc * 128 + 128])
                     for dc in range(16)]
            self.trans_tiles(pairs, after={"v": S.cnt["v"]})

        @self.step("v", after={"v": S.cnt["v"]})
        def _(v):
            v.memset(self.SM2[:, 32:40], 0.0)
            return v.drain()
        for os in range(16):
            tw = 0
            for dc in range(16):
                tw = self.dmaT("q", self.WS0[:, dc * 512:(dc + 1) * 512],
                               self.w3_out[os * 512:(os + 1) * 512, dc * 128:(dc + 1) * 128],
                               after={"t": S.cnt["t"]} if dc == 0 else None)
                tw = self.dmaT("q", self.WS1[:, dc * 512:(dc + 1) * 512],
                               self.w3_out[HID1 + os * 512: HID1 + (os + 1) * 512,
                                           dc * 128:(dc + 1) * 128])
            for tc in range(NTC):
                @self.step("t", after={"q": tw} if tc == 0 else {"v": S.cnt["v"], "s": S.cnt["s"]})
                def _(t, tc=tc):
                    last = None
                    for dc in range(16):
                        t.matmul(out=self.P[0][:, :],
                                 lhsT=self.AB[:, dc * 1024 + tc * 128: dc * 1024 + tc * 128 + 128],
                                 rhs=self.WS0[:, dc * 512:(dc + 1) * 512],
                                 start=(dc == 0), stop=(dc == 15))
                        last = t.matmul(out=self.P[1][:, :],
                                        lhsT=self.AB[:, dc * 1024 + tc * 128: dc * 1024 + tc * 128 + 128],
                                        rhs=self.WS1[:, dc * 512:(dc + 1) * 512],
                                        start=(dc == 0), stop=(dc == 15))
                    return last

                @self.step("v", after={"t": S.cnt["t"]})
                def _(v, tc=tc):
                    v.tensor_scalar(out=self.T0[:, 0:512], in0=self.P[0][:, :],
                                    scalar1=self.SM2[:, 16 + tc:17 + tc], scalar2=None,
                                    op0=AL.mult)
                    return v.tensor_scalar(out=self.T1[:, 0:512], in0=self.P[1][:, :],
                                           scalar1=self.SM2[:, 16 + tc:17 + tc], scalar2=None,
                                           op0=AL.mult)

                @self.step("s", after={"v": S.cnt["v"]})
                def _(s):
                    return s.activation(out=self.T1[:, 512:1024], in_=self.T1[:, 0:512],
                                        func=AF.Silu)

                @self.step("v", after={"s": S.cnt["s"]})
                def _(v, tc=tc):
                    v.tensor_tensor(out=self.SBB[:, 0:512], in0=self.T0[:, 0:512],
                                    in1=self.T1[:, 512:1024], op=AL.mult)
                    v.drain()
                    v.tensor_reduce(out=self.SM1[:, 7:8], in_=self.SBB[:, 0:512], axis=AX,
                                    op=AL.max, apply_absolute_value=True)
                    v.drain()
                    return v.tensor_tensor(out=self.SM2[:, 32 + tc:33 + tc],
                                           in0=self.SM2[:, 32 + tc:33 + tc],
                                           in1=self.SM1[:, 7:8], op=AL.max)
                self.dma("gd", self.u_spill[tc * 128:(tc + 1) * 128, os * 512:(os + 1) * 512],
                         self.SBB[:, 0:512], after={"v": S.cnt["v"]})

        @self.step("v", after={"v": S.cnt["v"]})
        def _(v):
            v.tensor_scalar(out=self.SM2[:, 32:40], in0=self.SM2[:, 32:40],
                            scalar1=1e-5, scalar2=None, op0=AL.max)
            v.drain()
            v.reciprocal(out=self.SM2[:, 40:48], in_=self.SM2[:, 32:40])
            v.drain()
            v.tensor_scalar(out=self.SM2[:, 40:48], in0=self.SM2[:, 40:48],
                            scalar1=127.0, scalar2=None, op0=AL.mult)
            v.tensor_scalar(out=self.SM2[:, 48:56], in0=self.SM2[:, 32:40],
                            scalar1=1.0 / 127.0, scalar2=None, op0=AL.mult)
            v.drain()
            last = None
            for tc in range(NTC):
                last = v.tensor_tensor(out=self.SM2[:, 56 + tc:57 + tc],
                                       in0=self.SM2[:, 48 + tc:49 + tc],
                                       in1=self.SM3[:, 5:6], op=AL.mult)
            return last

        for og in range(8):
            tw = 0
            for ou in range(8):
                for dsl in range(4):
                    buf = self.WS0 if ou < 4 else self.WS1
                    col = (ou % 4) * 2048 + dsl * 512
                    tw = self.dmaT("q", buf[:, col:col + 512],
                                   self.w3b_out[dsl * 512:(dsl + 1) * 512,
                                                og * 1024 + ou * 128: og * 1024 + (ou + 1) * 128],
                                   after={"t": S.cnt["t"]} if (ou == 0 and dsl == 0) else None)
            for tc in range(NTC):
                tq = self.dma("q", self.SBB[:, 0:1024],
                              self.u_spill[tc * 128:(tc + 1) * 128, og * 1024:(og + 1) * 1024],
                              after={"v": S.cnt["v"]})

                @self.step("v", after={"q": tq})
                def _(v, tc=tc):
                    v.tensor_scalar(out=self.T0[:, 0:1024], in0=self.SBB[:, 0:1024],
                                    scalar1=self.SM2[:, 40 + tc:41 + tc], scalar2=MAGIC,
                                    op0=AL.mult, op1=AL.add)
                    v.tensor_scalar(out=self.T0[:, 0:1024], in0=self.T0[:, 0:1024],
                                    scalar1=MAGIC, scalar2=127.0, op0=AL.subtract, op1=AL.min)
                    return v.tensor_scalar(out=self.SBB[:, 1024:2048], in0=self.T0[:, 0:1024],
                                           scalar1=-128.0, scalar2=None, op0=AL.max)
                pairs = [(self.SBB[:, 1024 + i * 128: 1024 + (i + 1) * 128],
                          self.AB[:, 16384 + i * 128: 16384 + (i + 1) * 128])
                         for i in range(8)]
                self.trans_tiles(pairs, after={"v": S.cnt["v"]})

                @self.step("t", after={"v": S.cnt["v"]})
                def _(t, tc=tc):
                    last = None
                    for dsl in range(4):
                        for ou in range(8):
                            buf = self.WS0 if ou < 4 else self.WS1
                            col = (ou % 4) * 2048 + dsl * 512
                            last = t.matmul(out=self.P[2 + dsl][:, :],
                                            lhsT=self.AB[:, 16384 + ou * 128: 16384 + (ou + 1) * 128],
                                            rhs=buf[:, col:col + 512],
                                            start=(ou == 0), stop=(ou == 7))
                    return last

                @self.step("v", after={"t": S.cnt["t"]})
                def _(v, tc=tc, og=og):
                    last = None
                    for dsl in range(4):
                        dst = self.AF[:, tc * 2048 + dsl * 512: tc * 2048 + (dsl + 1) * 512]
                        if og == 0:
                            last = v.tensor_copy(out=dst, in_=self.P[2 + dsl][:, :])
                        else:
                            last = v.tensor_tensor(out=dst, in0=dst,
                                                   in1=self.P[2 + dsl][:, :], op=AL.add)
                    return last

        for tc in range(NTC):
            tq = self.dma("q", self.T0[:, 0:D], self.xstream[tc * 128:(tc + 1) * 128, :],
                          after={"v": S.cnt["v"]})

            @self.step("v", after={"q": tq})
            def _(v, tc=tc):
                v.tensor_scalar(out=self.T1[:, 0:D], in0=self.AF[:, tc * 2048:(tc + 1) * 2048],
                                scalar1=self.SM2[:, 56 + tc:57 + tc], scalar2=None,
                                op0=AL.mult)
                v.drain()
                v.scalar_tensor_tensor(out=self.T1[:, 0:D], in0=self.T0[:, 0:D],
                                       scalar=2.0, in1=self.T1[:, 0:D],
                                       op0=AL.mult, op1=AL.add)
                v.drain()
                return v.tensor_copy(out=self.AF[:, 0:D], in_=self.T1[:, 0:D])
            dst = self.y_out[tc * 128:(tc + 1) * 128, :] if last_block \
                else self.xstream[tc * 128:(tc + 1) * 128, :]
            self.ln_write(self.AF[:, 0:D], dst, after={"v": S.cnt["v"]})

    def plan_output_wait(self):
        S = self.S

        @self.step("g", after=S.all())
        def _(g):
            return g.nop()

    def finalize(self):
        nc, S = self.nc, self.S
        with nc.Block() as block:
            @block.sync
            def _(sync):
                S.emit_for("sync", sync)

            @block.tensor
            def _(t):
                S.emit_for("t", t)

            @block.vector
            def _(v):
                S.emit_for("v", v)

            @block.scalar
            def _(s):
                S.emit_for("s", s)

            @block.gpsimd
            def _(g):
                S.emit_for("g", g)
        self.stack.close()
        return nc


    def dmaT_dyn(self, qk, out, src_tensor, mult, roff, nrows, cols, after=None):
        S = self.S
        ek = {"q": "sync", "gd": "g", "sd": "s"}[qk]
        S.cnt[qk] += 16
        sem = S.sems[qk]

        def fn(eng, out=out):
            off = self._dyn_off(eng, mult)
            nr = src_tensor.shape[0]
            base = src_tensor[roff:nr, cols] if roff else src_tensor[:, cols]
            eng.dma_start_transpose(out=out, in_=base[bass.ds(off, nrows), :]).then_inc(sem, 16)
        S.steps.append((ek, fn, {k: v for k, v in (after or {}).items() if v > 0}))
        return S.cnt[qk]

    def w2row(self, m, kc):
        return (kc // 2) * 1280 + m * 256 + (kc % 2) * 128

    def plan_l1_attn(self):
        S = self.S
        start = S.all()
        for tc in range(NTC):
            tq = self.dma("q", self.T0[:, 0:D], self.xstream[tc * 128:(tc + 1) * 128, :],
                          after=start if tc == 0 else {"t": S.cnt["t"]})

            @self.step("v", after={"q": tq, "v": S.cnt["v"]})
            def _(v):
                return v.tensor_copy(out=self.SBB[:, 0:D], in_=self.T0[:, 0:D])
            pairs = [(self.SBB[:, dc * 128:(dc + 1) * 128],
                      self.AB[:, dc * 1024 + tc * 128: dc * 1024 + tc * 128 + 128])
                     for dc in range(16)]
            self.trans_tiles(pairs, after={"v": S.cnt["v"]})

        for m in range(4):
            for dhalf in range(2):
                tw = 0
                for dsl_in in range(2):
                    dsl = dhalf * 2 + dsl_in
                    buf = self.WS0 if dsl_in == 0 else self.WS1
                    for kc in range(16):
                        tw = self.dma("q", buf[:, kc * 512:(kc + 1) * 512],
                                      self.w2_out[self.w2row(m, kc): self.w2row(m, kc) + 128,
                                                  dsl * 512:(dsl + 1) * 512],
                                      after={"t": S.cnt["t"], "v": S.cnt["v"], "s": S.cnt["s"]}
                                      if (dsl_in == 0 and kc == 0) else None)
                tb = self.dma("q", self.BIASR[0:1, 0:1024],
                              self.brow_sp[0:1, m * 2048 + dhalf * 1024: m * 2048 + (dhalf + 1) * 1024],
                              after={"t": S.cnt["t"]})
                for tc in range(NTC):
                    @self.step("t", after={"q": max(tw, tb)} if tc == 0 else {"v": S.cnt["v"], "s": S.cnt["s"]})
                    def _(t, m=m, tc=tc, dhalf=dhalf):
                        last = None
                        for dsl_in in range(2):
                            dsl = dhalf * 2 + dsl_in
                            buf = self.WS0 if dsl_in == 0 else self.WS1
                            pp = self.P[dsl_in]
                            for kc in range(16):
                                t.matmul(out=pp[:, :],
                                         lhsT=self.AB[:, kc * 1024 + tc * 128: kc * 1024 + tc * 128 + 128],
                                         rhs=buf[:, kc * 512:(kc + 1) * 512],
                                         start=(kc == 0), stop=False, skip_group_check=True)
                            last = t.matmul(out=pp[:, :], lhsT=self.ONEB[0:1, :],
                                            rhs=self.BIASR[0:1, dsl_in * 512:(dsl_in + 1) * 512],
                                            start=False, stop=True, skip_group_check=True)
                        return last
                    if m <= 1:
                        tq = self.dma("q", self.ROPE[:, 0:64],
                                      self.c["rsin"][tc * 128:(tc + 1) * 128, :],
                                      after={"v": S.cnt["v"]})
                        tq = self.dma("q", self.ROPE[:, 64:128],
                                      self.c["rcos"][tc * 128:(tc + 1) * 128, :])

                        @self.step("v", after={"t": S.cnt["t"], "q": tq})
                        def _(v):
                            v.tensor_copy(out=self.T0[:, 0:512], in_=self.P[0][:, :])
                            v.tensor_copy(out=self.T0[:, 512:1024], in_=self.P[1][:, :])
                            v.drain()
                            r3 = lambda ap: ap.rearrange("p (h x) -> p h x", x=128)
                            x1 = r3(self.T0[:, 0:1024])[:, :, 0:64]
                            x2 = r3(self.T0[:, 0:1024])[:, :, 64:128]
                            o1 = r3(self.T1[:, 0:1024])[:, :, 0:64]
                            o2 = r3(self.T1[:, 0:1024])[:, :, 64:128]
                            a1 = r3(self.AF[:, 0:1024])[:, :, 0:64]
                            a2 = r3(self.AF[:, 0:1024])[:, :, 64:128]
                            sin = self.ROPE[:, 0:64].rearrange("p (o x) -> p o x", o=1).broadcast_to((128, 8, 64))
                            cos = self.ROPE[:, 64:128].rearrange("p (o x) -> p o x", o=1).broadcast_to((128, 8, 64))
                            v.tensor_tensor(out=o1, in0=x1, in1=cos, op=AL.mult)
                            v.tensor_tensor(out=o2, in0=x2, in1=cos, op=AL.mult)
                            v.tensor_tensor(out=a1, in0=x2, in1=sin, op=AL.mult)
                            v.tensor_tensor(out=a2, in0=x1, in1=sin, op=AL.mult)
                            v.drain()
                            v.tensor_tensor(out=o1, in0=o1, in1=a1, op=AL.subtract)
                            v.tensor_tensor(out=o2, in0=o2, in1=a2, op=AL.add)
                            v.drain()
                            v.tensor_scalar(out=self.T0[:, 0:1024], in0=self.T1[:, 0:1024],
                                            scalar1=0.0, scalar2=None, op0=AL.min)
                            return v.tensor_scalar(out=self.AF[:, 0:1024], in0=self.T1[:, 0:1024],
                                                   scalar1=0.0, scalar2=None, op0=AL.max)

                        @self.step("s", after={"v": S.cnt["v"]})
                        def _(s):
                            return s.activation(out=self.T0[:, 0:1024], in_=self.T0[:, 0:1024],
                                                func=AF.Exp)

                        @self.step("v", after={"s": S.cnt["s"]})
                        def _(v):
                            v.tensor_tensor(out=self.T0[:, 0:1024], in0=self.T0[:, 0:1024],
                                            in1=self.AF[:, 0:1024], op=AL.add)
                            v.drain()
                            return v.tensor_copy(out=self.SBB[:, 0:1024], in_=self.T0[:, 0:1024])
                    elif m == 2:
                        @self.step("v", after={"t": S.cnt["t"]})
                        def _(v):
                            v.tensor_copy(out=self.SBB[:, 0:512], in_=self.P[0][:, :])
                            return v.tensor_copy(out=self.SBB[:, 512:1024], in_=self.P[1][:, :])
                    else:
                        @self.step("s", after={"t": S.cnt["t"], "v": S.cnt["v"]})
                        def _(s):
                            s.activation(out=self.SBB[:, 0:512], in_=self.P[0][:, :],
                                         func=AF.Sigmoid)
                            return s.activation(out=self.SBB[:, 512:1024], in_=self.P[1][:, :],
                                                func=AF.Sigmoid)
                    self.dma("gd", self.qkvg_sp[m * TL + tc * 128: m * TL + (tc + 1) * 128,
                                                dhalf * 1024:(dhalf + 1) * 1024],
                             self.SBB[:, 0:1024], after={"v": S.cnt["v"], "s": S.cnt["s"]})

        tgd = S.cnt["gd"]
        for c in range(NCHKL):
            tq = self.dma("q", self.SBB[0:64, 0:D],
                          self.qkvg_sp[TL + c * 64: TL + (c + 1) * 64, :],
                          after={"gd": tgd} if c == 0 else {"t": S.cnt["t"]})
            tq = self.dma("q", self.WS0[0:64, 0:D],
                          self.qkvg_sp[2 * TL + c * 64: 2 * TL + (c + 1) * 64, :])

            @self.step("t", after={"q": tq})
            def _(t, c=c):
                last = None
                for h in range(H):
                    t.matmul(out=self.P[h // 4][:, (h % 4) * 128:(h % 4) * 128 + 128],
                             lhsT=self.SBB[0:64, h * 128:(h + 1) * 128],
                             rhs=self.WS0[0:64, h * 128:(h + 1) * 128],
                             start=(c == 0 and h % 4 == 0), stop=(c == NCHKL - 1),
                             skip_group_check=True)
                    last = t.matmul(out=self.P[4][:, h:h + 1],
                                    lhsT=self.SBB[0:64, h * 128:(h + 1) * 128],
                                    rhs=self.ONEB[0:64, 0:1],
                                    start=(c == 0 and h == 0), stop=(c == NCHKL - 1),
                                    skip_group_check=True)
                return last

        @self.step("v", after={"t": S.cnt["t"]})
        def _(v):
            last = None
            for h in range(H):
                last = v.tensor_copy(out=self.AF[:, h * 136: h * 136 + 128],
                                     in_=self.P[h // 4][:, (h % 4) * 128:(h % 4) * 128 + 128])
                last = v.tensor_copy(out=self.AF[:, h * 136 + 128: h * 136 + 129],
                                     in_=self.P[4][:, h:h + 1])
            return last
        tv = S.cnt["v"]
        self.dma("gd", self.s1_in[:, :].rearrange("(h p) c -> p h c", p=128),
                 self.AF[:, 0:H * 136].rearrange("p (h c) -> p h c", c=136), after={"v": tv})
        t_s1 = self.cc("AllGather", AL.bypass, [self.s1_in], [self.s1_out],
                       after={"gd": S.cnt["gd"]})

        @self.step("v", after={"cc": t_s1})
        def _(v):
            v.memset(self.AF[:, 0:H * 136], 0.0)
            return v.drain()
        for b in range(NCORE):
            tq = self.dma("q", self.T0[:, 0:1088].rearrange("p (h c) -> p h c", c=136),
                          self.s1_out[b * 2048: b * 2048 + 1024, 0:136]
                          .rearrange("(h p) c -> p h c", p=128),
                          after={"v": S.cnt["v"]})
            tq = self.dma("q", self.T1[:, 0:1088].rearrange("p (h c) -> p h c", c=136),
                          self.s1_out[b * 2048 + 1024: b * 2048 + 2048, 0:136]
                          .rearrange("(h p) c -> p h c", p=128))

            @self.step("v", after={"q": tq})
            def _(v, b=b):
                v.scalar_tensor_tensor(out=self.AF[:, 0:8 * 136], in0=self.T0[:, 0:1088],
                                       scalar=self.SEL[:, b:b + 1], in1=self.AF[:, 0:8 * 136],
                                       op0=AL.mult, op1=AL.add)
                v.scalar_tensor_tensor(out=self.AF[:, 8 * 136:16 * 136], in0=self.T1[:, 0:1088],
                                       scalar=self.SEL[:, b:b + 1], in1=self.AF[:, 8 * 136:16 * 136],
                                       op0=AL.mult, op1=AL.add)
                return v.drain()

        for c in range(NCHKL):
            tq = self.dma("q", self.WS0[0:64, 0:D],
                          self.qkvg_sp[0 * TL + c * 64: 0 * TL + (c + 1) * 64, :],
                          after={"v": S.cnt["v"], "t": S.cnt["t"]})
            tq = self.dma("q", self.WS0[0:64, 2048:2048 + D],
                          self.qkvg_sp[1 * TL + c * 64: 1 * TL + (c + 1) * 64, :])
            tq = self.dma("q", self.WS0[0:64, 4096:4096 + D],
                          self.qkvg_sp[2 * TL + c * 64: 2 * TL + (c + 1) * 64, :])
            tq = self.dma("q", self.WS0[0:64, 6144:6144 + D],
                          self.qkvg_sp[3 * TL + c * 64: 3 * TL + (c + 1) * 64, :])

            @self.step("t", after={"q": tq})
            def _(t):
                last = None
                for h in range(H):
                    t.matmul(out=self.P[h // 4][:, (h % 4) * 128:(h % 4) * 128 + 128],
                             lhsT=self.WS0[0:64, 2048 + h * 128: 2048 + (h + 1) * 128],
                             rhs=self.WS0[0:64, 4096 + h * 128: 4096 + (h + 1) * 128],
                             start=(h % 4 == 0), stop=(h % 4 == 3), skip_group_check=True)
                    last = t.matmul(out=self.P[4][:, h:h + 1],
                                    lhsT=self.WS0[0:64, 2048 + h * 128: 2048 + (h + 1) * 128],
                                    rhs=self.ONEB[0:64, 0:1],
                                    start=(h == 0), stop=(h == H - 1), skip_group_check=True)
                return last

            @self.step("v", after={"t": S.cnt["t"]})
            def _(v):
                acc3 = self.AF[:, 0:16 * 136].rearrange("p (h c) -> p h c", c=136)
                for bk in range(4):
                    acc = acc3[:, bk * 4:(bk + 1) * 4, 0:128]
                    pin = self.P[bk][:, :].rearrange("p (h c) -> p h c", c=128)
                    v.tensor_tensor(out=acc, in0=acc, in1=pin, op=AL.add)
                accz = acc3[:, :, 128:129]
                pz = self.P[4][:, 0:16].rearrange("p (h c) -> p h c", c=1)
                v.tensor_tensor(out=accz, in0=accz, in1=pz, op=AL.add)
                v.drain()
                return v.tensor_copy(out=self.AB[:, 0:16 * 136], in_=self.AF[:, 0:16 * 136])

            @self.step("t", after={"v": S.cnt["v"]})
            def _(t):
                last = None
                for h in range(H):
                    pp = self.PT if h < 8 else self.PT2
                    last = t.transpose(out=pp[:, (h % 8) * 64:(h % 8) * 64 + 64],
                                       in_=self.WS0[0:64, h * 128:(h + 1) * 128],
                                       identity=self.EYEB[0:64, 0:64])
                return last

            @self.step("v", after={"t": S.cnt["t"]})
            def _(v):
                v.tensor_copy(out=self.SBB[:, 0:512], in_=self.PT[:, :])
                return v.tensor_copy(out=self.SBB[:, 512:1024], in_=self.PT2[:, :])

            @self.step("t", after={"v": S.cnt["v"]})
            def _(t):
                last = None
                for h in range(H):
                    t.matmul(out=self.P[h // 4][0:64, (h % 4) * 128:(h % 4) * 128 + 128],
                             lhsT=self.SBB[:, h * 64:(h + 1) * 64],
                             rhs=self.AB[:, h * 136: h * 136 + 128],
                             start=(h % 4 == 0), stop=(h % 4 == 3), skip_group_check=True)
                    last = t.matmul(out=self.P[5][0:64, h:h + 1],
                                    lhsT=self.SBB[:, h * 64:(h + 1) * 64],
                                    rhs=self.AB[:, h * 136 + 128: h * 136 + 129],
                                    start=(h == 0), stop=(h == H - 1), skip_group_check=True)
                return last

            @self.step("v", after={"t": S.cnt["t"]})
            def _(v, c=c):
                v.tensor_scalar(out=self.SM0[0:64, 0:16], in0=self.P[5][0:64, 0:16],
                                scalar1=1e-6, scalar2=None, op0=AL.add)
                v.drain()
                v.reciprocal(out=self.SM0[0:64, 16:32], in_=self.SM0[0:64, 0:16])
                v.drain()
                last = None
                for h in range(H):
                    last = v.scalar_tensor_tensor(
                        out=self.SBB[0:64, h * 128:(h + 1) * 128],
                        in0=self.P[h // 4][0:64, (h % 4) * 128:(h % 4) * 128 + 128],
                        scalar=self.SM0[0:64, 16 + h:17 + h],
                        in1=self.WS0[0:64, 6144 + h * 128: 6144 + (h + 1) * 128],
                        op0=AL.mult, op1=AL.mult)
                return last
            self.dma("gd", self.qkvg_sp[0 * TL + c * 64: 0 * TL + (c + 1) * 64, :],
                     self.SBB[0:64, 0:D], after={"v": S.cnt["v"]})

        tgd = S.cnt["gd"]
        for tc in range(NTC):
            tq = self.dma("q", self.SBB[:, 0:D],
                          self.qkvg_sp[tc * 128:(tc + 1) * 128, :],
                          after={"gd": tgd, "v": S.cnt["v"], "t": S.cnt["t"]})

            @self.step("v", after={"q": tq})
            def _(v):
                return v.engine_nop()
            pairs = [(self.SBB[:, dc * 128:(dc + 1) * 128],
                      self.AB[:, dc * 1024 + tc * 128: dc * 1024 + tc * 128 + 128])
                     for dc in range(16)]
            self.trans_tiles(pairs, after={"v": S.cnt["v"]})
        for dsl in range(4):
            tw = 0
            for kc in range(16):
                tw = self.dma("q", self.WS1[:, kc * 512:(kc + 1) * 512],
                              self.w2_out[self.w2row(4, kc): self.w2row(4, kc) + 128,
                                          dsl * 512:(dsl + 1) * 512],
                              after={"t": S.cnt["t"]} if kc == 0 else None)
            tb = self.dma("q", self.BIASR[0:1, 0:512],
                          self.brow_sp[0:1, 4 * 2048 + dsl * 512: 4 * 2048 + (dsl + 1) * 512],
                          after={"t": S.cnt["t"]})
            for tc in range(NTC):
                @self.step("t", after={"q": max(tw, tb)} if tc == 0 else {"v": S.cnt["v"]})
                def _(t, tc=tc, dsl=dsl):
                    for kc in range(16):
                        t.matmul(out=self.P[tc % 2][:, :],
                                 lhsT=self.AB[:, kc * 1024 + tc * 128: kc * 1024 + tc * 128 + 128],
                                 rhs=self.WS1[:, kc * 512:(kc + 1) * 512],
                                 start=(kc == 0), stop=False, skip_group_check=True)
                    return t.matmul(out=self.P[tc % 2][:, :], lhsT=self.ONEB[0:1, :],
                                    rhs=self.BIASR[0:1, 0:512],
                                    start=False, stop=True, skip_group_check=True)

                @self.step("v", after={"t": S.cnt["t"]})
                def _(v, tc=tc, dsl=dsl):
                    return v.tensor_copy(out=self.AF[:, tc * 2048 + dsl * 512: tc * 2048 + (dsl + 1) * 512],
                                         in_=self.P[tc % 2][:, :])
        for tc in range(NTC):
            tq = self.dma("q", self.T0[:, 0:D], self.xstream[tc * 128:(tc + 1) * 128, :],
                          after={"v": S.cnt["v"]})

            @self.step("v", after={"q": tq})
            def _(v, tc=tc):
                v.tensor_tensor(out=self.T1[:, 0:D], in0=self.AF[:, tc * 2048:(tc + 1) * 2048],
                                in1=self.T0[:, 0:D], op=AL.add)
                v.drain()
                return v.tensor_copy(out=self.AF[:, 0:D], in_=self.T1[:, 0:D])
            self.ln_write(self.AF[:, 0:D], self.xstream[tc * 128:(tc + 1) * 128, :],
                          after={"v": S.cnt["v"]})

    def w0row(self, m, kc):
        return (kc // 2) * 1024 + m * 256 + (kc % 2) * 128

    def plan_l0_attn(self):
        S = self.S
        start = S.all()
        self.olin_sp = self.mk("olin_sp", (HL * 128, T), BF16)
        for tc in range(NTC):
            tq = self.dma("q", self.T0[:, 0:D], self.xstream[tc * 128:(tc + 1) * 128, :],
                          after=start if tc == 0 else {"t": S.cnt["t"]})

            @self.step("v", after={"q": tq, "v": S.cnt["v"]})
            def _(v):
                return v.tensor_copy(out=self.SBB[:, 0:D], in_=self.T0[:, 0:D])
            pairs = [(self.SBB[:, dc * 128:(dc + 1) * 128],
                      self.AB[:, dc * 1024 + tc * 128: dc * 1024 + tc * 128 + 128])
                     for dc in range(16)]
            self.trans_tiles(pairs, after={"v": S.cnt["v"]})
        for m in range(3):
            wn = ["l0_wq", "l0_wk", "l0_wv"][m]
            for dc in range(16):
                tw = 0
                for kc in range(16):
                    tw = self.dma("q", self.WS0[:, kc * 128:(kc + 1) * 128],
                                  self.w0_out[self.w0row(m, kc): self.w0row(m, kc) + 128,
                                              dc * 128:(dc + 1) * 128],
                                  after={"t": S.cnt["t"], "cc": self.t_w0} if kc == 0 else None)

                @self.step("t", after={"q": tw, "v": S.cnt["v"]})
                def _(t):
                    last = None
                    for half in range(2):
                        for kc in range(16):
                            last = t.matmul(out=self.P[half][:, :],
                                            lhsT=self.WS0[:, kc * 128:(kc + 1) * 128],
                                            rhs=self.AB[:, kc * 1024 + half * 512: kc * 1024 + (half + 1) * 512],
                                            start=(kc == 0), stop=(kc == 15),
                                            skip_group_check=True)
                    return last

                @self.step("v", after={"t": S.cnt["t"]})
                def _(v):
                    v.tensor_copy(out=self.SBB[:, 0:512], in_=self.P[0][:, :])
                    return v.tensor_copy(out=self.SBB[:, 512:1024], in_=self.P[1][:, :])
                self.dma("gd", self.a0_in[m * 2048 + dc * 128: m * 2048 + (dc + 1) * 128, :],
                         self.SBB[:, 0:1024], after={"v": S.cnt["v"]})
        t_a0 = self.cc("AllGather", AL.bypass, [self.a0_in], [self.a0_out],
                       after={"gd": S.cnt["gd"]})

        for hh in range(HL):
            self._l0_head(hh, t_a0)

        t_a1 = self.cc("AllGather", AL.bypass, [self.a1_in], [self.a1_out],
                       after={"gd": S.cnt["gd"]})

        for dc in range(16):
            self.dma_dyn("q", self.AB[:, dc * 1024:(dc + 1) * 1024], None,
                         None, 0, None, after={"cc": t_a1, "t": S.cnt["t"], "v": S.cnt["v"]}) \
                if False else None
        Sq = self.S
        Sq.cnt["q"] += 16
        semq = Sq.sems["q"]

        def a1fn(eng, sem=semq):
            off = self._dyn_off(eng, 1)
            view = self.a1_out.rearrange("d (blk t) -> blk d t", t=TL)
            eng.dma_start(out=self.a1st[:, :, :],
                          in_=view[bass.ds(off, 1), :, :]).then_inc(sem, 16)
        Sq.steps.append(("sync", a1fn,
                         {k: v for k, v in dict(cc=t_a1, t=S.cnt["t"], v=S.cnt["v"]).items() if v > 0}))
        t_a1st = Sq.cnt["q"]
        for dc in range(16):
            self.dma("q", self.AB[:, dc * 1024:(dc + 1) * 1024],
                     self.a1st[0, dc * 128:(dc + 1) * 128, :],
                     after={"q": t_a1st} if dc == 0 else None)
        t_ot = S.cnt["q"]
        for dsl in range(4):
            tw = 0
            for kc in range(16):
                tw = self.dma("q", self.WS1[:, kc * 512:(kc + 1) * 512],
                              self.w0_out[self.w0row(3, kc): self.w0row(3, kc) + 128,
                                          dsl * 512:(dsl + 1) * 512],
                              after={"t": S.cnt["t"], "q": t_ot} if kc == 0 else None)
            for tc in range(NTC):
                @self.step("t", after={"q": tw} if tc == 0 else {"v": S.cnt["v"]})
                def _(t, tc=tc):
                    last = None
                    for kc in range(16):
                        last = t.matmul(out=self.P[tc % 2][:, :],
                                        lhsT=self.AB[:, kc * 1024 + tc * 128: kc * 1024 + tc * 128 + 128],
                                        rhs=self.WS1[:, kc * 512:(kc + 1) * 512],
                                        start=(kc == 0), stop=(kc == 15),
                                        skip_group_check=True)
                    return last

                @self.step("v", after={"t": S.cnt["t"]})
                def _(v, tc=tc, dsl=dsl):
                    return v.tensor_copy(out=self.AF[:, tc * 2048 + dsl * 512: tc * 2048 + (dsl + 1) * 512],
                                         in_=self.P[tc % 2][:, :])
        for tc in range(NTC):
            tq = self.dma("q", self.T0[:, 0:D], self.xstream[tc * 128:(tc + 1) * 128, :],
                          after={"v": S.cnt["v"]})

            @self.step("v", after={"q": tq})
            def _(v, tc=tc):
                v.tensor_tensor(out=self.T1[:, 0:D], in0=self.AF[:, tc * 2048:(tc + 1) * 2048],
                                in1=self.T0[:, 0:D], op=AL.add)
                v.drain()
                return v.tensor_copy(out=self.AF[:, 0:D], in_=self.T1[:, 0:D])
            self.ln_write(self.AF[:, 0:D], self.xstream[tc * 128:(tc + 1) * 128, :],
                          after={"v": S.cnt["v"]})

    def _l0_head(self, hh, t_a0):
        S = self.S
        Sq = self.S
        Sq.cnt["q"] += 16
        semq = Sq.sems["q"]

        def stfn(eng, hh=hh, sem=semq):
            off = self._dyn_off(eng, 256)
            view = self.a0_out.rearrange("(b m r) t -> r b m t", m=3, r=2048)
            base = view[hh * 128: 2048, :, :, :]
            dst = self.hstage[:, :].rearrange("r (b m t) -> r b m t", m=3, t=TL)
            eng.dma_start(out=dst, in_=base[bass.ds(off, 128), :, :, :]).then_inc(sem, 16)
        Sq.steps.append(("sync", stfn,
                         {k: v for k, v in dict(cc=t_a0, t=S.cnt["t"], v=S.cnt["v"],
                                                s=S.cnt["s"], gd=S.cnt["gd"]).items() if v > 0}))
        t_hst = Sq.cnt["q"]
        hs = self.hstage
        tq = 0
        for b in range(NCORE):
            tq = self.dma("q", self.AB[:, 0 + b * 1024: 0 + (b + 1) * 1024],
                          hs[:, b * 3072 + 1024: b * 3072 + 2048],
                          after={"q": t_hst} if b == 0 else None)
            tq = self.dma("q", self.AB[:, 8192 + b * 1024: 8192 + (b + 1) * 1024],
                          hs[:, b * 3072 + 0: b * 3072 + 1024])
            for sub in range(8):
                jt = b * 8 + sub
                tq = self.dmaT("q", self.AB[:, 16384 + jt * 128: 16384 + (jt + 1) * 128],
                               hs[:, b * 3072 + 2048 + sub * 128: b * 3072 + 2048 + (sub + 1) * 128])
        t_hload = tq

        @self.step("v", after={"q": t_hload})
        def _(v):
            v.memset(self.AF[:, 0:128], 0.0)
            v.memset(self.SBB[:, 1024:1152], 0.0)
            return v.drain()

        kh = self.AB[:, 0:8192]
        qh = self.AB[:, 8192:16384]
        W1 = self.WS1
        KSF, QSF, KBF, BT, KST, VB, KBT = 0, 512, 1024, 1536, 2560, 3584, 4608
        MOFF, MTOFF, XOFF, ZOFF, WOFF = 5632, 6144, 6656, 7168, 7680
        UOFF, WTN, ATN = BT, KBF, MTOFF
        for b in range(16):
            t0c = b * 512
            @self.step("s", after={"v": S.cnt["v"], "t": S.cnt["t"]})
            def _(s, t0c=t0c):
                return s.activation(out=self.T0[:, 0:512], in_=kh[:, t0c:t0c + 512], func=AF.Exp)

            @self.step("t", after={"s": S.cnt["s"]})
            def _(t):
                return t.matmul(out=self.P[2][0:1, :], lhsT=self.ONEF[:, 0:1],
                                rhs=self.T0[:, 0:512], start=True, stop=True)

            @self.step("v", after={"t": S.cnt["t"]})
            def _(v):
                v.reciprocal(out=self.ROWV[0:1, 0:512], in_=self.P[2][0:1, :])
                return v.drain()

            @self.step("t", after={"v": S.cnt["v"]})
            def _(t):
                return t.matmul(out=self.P[3][:, :], lhsT=self.ONEF[0:1, :],
                                rhs=self.ROWV[0:1, 0:512], start=True, stop=True)

            @self.step("v", after={"t": S.cnt["t"]})
            def _(v):
                return v.tensor_tensor(out=W1[:, KSF:KSF + 512], in0=self.T0[:, 0:512],
                                       in1=self.P[3][:, :], op=AL.mult)
            @self.step("s", after={"v": S.cnt["v"]})
            def _(s, t0c=t0c):
                return s.activation(out=self.T0[:, 0:512], in_=qh[:, t0c:t0c + 512], func=AF.Exp)

            @self.step("t", after={"s": S.cnt["s"]})
            def _(t):
                return t.matmul(out=self.P[2][0:1, :], lhsT=self.ONEF[:, 0:1],
                                rhs=self.T0[:, 0:512], start=True, stop=True)

            @self.step("v", after={"t": S.cnt["t"]})
            def _(v):
                v.reciprocal(out=self.ROWV[0:1, 0:512], in_=self.P[2][0:1, :])
                v.drain()
                v.tensor_scalar(out=self.ROWV[0:1, 0:512], in0=self.ROWV[0:1, 0:512],
                                scalar1=1.0 / SQD, scalar2=None, op0=AL.mult)
                return v.drain()

            @self.step("t", after={"v": S.cnt["v"]})
            def _(t):
                return t.matmul(out=self.P[3][:, :], lhsT=self.ONEF[0:1, :],
                                rhs=self.ROWV[0:1, 0:512], start=True, stop=True)

            @self.step("v", after={"t": S.cnt["t"]})
            def _(v):
                return v.tensor_tensor(out=W1[:, QSF:QSF + 512], in0=self.T0[:, 0:512],
                                       in1=self.P[3][:, :], op=AL.mult)
            @self.step("s", after={"v": S.cnt["v"]})
            def _(s, t0c=t0c):
                return s.activation(out=self.T1[:, 0:512], in_=kh[:, t0c:t0c + 512],
                                    func=AF.Exp, scale=-1.0)

            @self.step("v", after={"s": S.cnt["s"]})
            def _(v):
                return v.tensor_scalar(out=self.T1[:, 0:512], in0=self.T1[:, 0:512],
                                       scalar1=1.0, scalar2=None, op0=AL.add)

            @self.step("s", after={"v": S.cnt["v"]})
            def _(s):
                return s.activation(out=self.T1[:, 0:512], in_=self.T1[:, 0:512],
                                    func=AF.Ln)

            @self.step("v", after={"s": S.cnt["s"]})
            def _(v):
                v.scalar_tensor_tensor(out=W1[:, KBF:KBF + 512], in0=self.T1[:, 0:512],
                                       scalar=-1.0 / GATE_NORM, in1=W1[:, KSF:KSF + 512],
                                       op0=AL.mult, op1=AL.mult)
                return v.tensor_scalar(out=self.SBB[:, 0:512], in0=self.T1[:, 0:512],
                                       scalar1=-1.0 / GATE_NORM, scalar2=None, op0=AL.mult)
            @self.step("t", after={"v": S.cnt["v"]})
            def _(t):
                last = None
                for i in range(8):
                    pp = self.PT if i < 4 else self.PT2
                    last = t.transpose(out=pp[0:64, (i % 4) * 128:(i % 4) * 128 + 128],
                                       in_=self.SBB[:, i * 64:(i + 1) * 64],
                                       identity=self.EYEB[:, :])
                return last

            @self.step("v", after={"t": S.cnt["t"]})
            def _(v):
                v.tensor_copy(out=W1[0:64, BT:BT + 512], in_=self.PT[0:64, :])
                return v.tensor_copy(out=W1[0:64, BT + 512:BT + 1024], in_=self.PT2[0:64, :])

            @self.step("t", after={"v": S.cnt["v"]})
            def _(t):
                last = None
                for i in range(8):
                    pp = self.PT if i < 4 else self.PT2
                    last = t.transpose(out=pp[0:64, (i % 4) * 128:(i % 4) * 128 + 128],
                                       in_=W1[:, KSF + i * 64: KSF + (i + 1) * 64],
                                       identity=self.EYEB[:, :])
                return last

            @self.step("v", after={"t": S.cnt["t"]})
            def _(v):
                v.tensor_copy(out=W1[0:64, KST:KST + 512], in_=self.PT[0:64, :])
                return v.tensor_copy(out=W1[0:64, KST + 512:KST + 1024], in_=self.PT2[0:64, :])
            blk0 = t0c // 1024
            coff = t0c % 1024
            tqv = self.dma("q", self.SBB[:, 1536:2048],
                           self.hstage[:, blk0 * 3072 + 2048 + coff:
                                       blk0 * 3072 + 2048 + coff + 512],
                           after={"v": S.cnt["v"]})

            @self.step("t", after={"q": tqv, "v": S.cnt["v"]})
            def _(t):
                last = None
                for i in range(8):
                    pp = self.PT if i < 4 else self.PT2
                    last = t.transpose(out=pp[0:64, (i % 4) * 128:(i % 4) * 128 + 128],
                                       in_=self.SBB[:, 1536 + i * 64: 1536 + (i + 1) * 64],
                                       identity=self.EYEB[:, :])
                return last

            @self.step("v", after={"t": S.cnt["t"]})
            def _(v):
                v.tensor_copy(out=W1[0:64, VB:VB + 512], in_=self.PT[0:64, :])
                v.tensor_copy(out=W1[0:64, VB + 512:VB + 1024], in_=self.PT2[0:64, :])
                v.drain()
                v.tensor_tensor(out=W1[0:64, VB:VB + 1024], in0=W1[0:64, VB:VB + 1024],
                                in1=W1[0:64, BT:BT + 1024], op=AL.mult)
                return v.tensor_tensor(out=W1[0:64, KBT:KBT + 1024], in0=W1[0:64, KST:KST + 1024],
                                       in1=W1[0:64, BT:BT + 1024], op=AL.mult)
            @self.step("t", after={"v": S.cnt["v"]})
            def _(t):
                last = None
                for i in range(8):
                    t.matmul(out=self.P[0][0:64, i * 64:(i + 1) * 64],
                             lhsT=W1[:, KBF + i * 64: KBF + (i + 1) * 64],
                             rhs=W1[:, KSF + i * 64: KSF + (i + 1) * 64],
                             start=(i == 0), stop=(i == 7), skip_group_check=True)
                    last = t.matmul(out=self.P[1][0:64, i * 64:(i + 1) * 64],
                                    lhsT=W1[:, KSF + i * 64: KSF + (i + 1) * 64],
                                    rhs=W1[:, KBF + i * 64: KBF + (i + 1) * 64],
                                    start=(i == 0), stop=(i == 7), skip_group_check=True)
                return last

            @self.step("v", after={"t": S.cnt["t"]})
            def _(v):
                v.tensor_tensor(out=self.T0[0:64, 0:512], in0=self.P[0][0:64, :],
                                in1=self.MSK[:, 0:512], op=AL.mult)
                v.tensor_tensor(out=self.T1[0:64, 0:512], in0=self.P[1][0:64, :],
                                in1=self.MSK[:, 512:1024], op=AL.mult)
                v.drain()
                v.tensor_tensor(out=W1[0:64, MOFF:MOFF + 512], in0=self.T0[0:64, 0:512],
                                in1=self.MSK[:, 3 * 512:4 * 512], op=AL.add)
                v.tensor_tensor(out=W1[0:64, MTOFF:MTOFF + 512], in0=self.T1[0:64, 0:512],
                                in1=self.MSK[:, 3 * 512:4 * 512], op=AL.add)
                v.drain()
                v.scalar_tensor_tensor(out=W1[0:64, XOFF:XOFF + 512], in0=W1[0:64, MTOFF:MTOFF + 512],
                                       scalar=-1.0, in1=self.MSK[:, 4 * 512:5 * 512],
                                       op0=AL.mult, op1=AL.add)
                v.scalar_tensor_tensor(out=W1[0:64, ZOFF:ZOFF + 512], in0=W1[0:64, MOFF:MOFF + 512],
                                       scalar=-1.0, in1=self.MSK[:, 4 * 512:5 * 512],
                                       op0=AL.mult, op1=AL.add)
                return v.drain()
            for it in range(5):
                @self.step("t", after={"v": S.cnt["v"]})
                def _(t):
                    last = None
                    for i in range(8):
                        sl = slice(i * 64, (i + 1) * 64)
                        t.matmul(out=self.P[0][0:64, sl],
                                 lhsT=W1[0:64, MOFF + i * 64: MOFF + (i + 1) * 64],
                                 rhs=W1[0:64, XOFF + i * 64: XOFF + (i + 1) * 64],
                                 start=(i == 0), stop=(i == 7), skip_group_check=True)
                        last = t.matmul(out=self.P[1][0:64, sl],
                                        lhsT=W1[0:64, XOFF + i * 64: XOFF + (i + 1) * 64],
                                        rhs=W1[0:64, MOFF + i * 64: MOFF + (i + 1) * 64],
                                        start=(i == 0), stop=(i == 7), skip_group_check=True)
                    return last

                @self.step("v", after={"t": S.cnt["t"]})
                def _(v):
                    v.scalar_tensor_tensor(out=W1[0:64, WOFF:WOFF + 512], in0=self.P[0][0:64, :],
                                           scalar=-1.0, in1=self.MSK[:, 4 * 512:5 * 512],
                                           op0=AL.mult, op1=AL.add)
                    v.scalar_tensor_tensor(out=self.SBB[0:64, 0:512], in0=self.P[1][0:64, :],
                                           scalar=-1.0, in1=self.MSK[:, 4 * 512:5 * 512],
                                           op0=AL.mult, op1=AL.add)
                    return v.drain()

                @self.step("t", after={"v": S.cnt["v"]})
                def _(t):
                    last = None
                    for i in range(8):
                        sl = slice(i * 64, (i + 1) * 64)
                        t.matmul(out=self.P[0][0:64, sl],
                                 lhsT=W1[0:64, ZOFF + i * 64: ZOFF + (i + 1) * 64],
                                 rhs=W1[0:64, WOFF + i * 64: WOFF + (i + 1) * 64],
                                 start=(i == 0), stop=(i == 7), skip_group_check=True)
                        last = t.matmul(out=self.P[1][0:64, sl],
                                        lhsT=W1[0:64, WOFF + i * 64: WOFF + (i + 1) * 64],
                                        rhs=self.SBB[0:64, sl] if False else W1[0:64, ZOFF + i * 64: ZOFF + (i + 1) * 64],
                                        start=(i == 0), stop=(i == 7), skip_group_check=True)
                    return last

                @self.step("v", after={"t": S.cnt["t"]})
                def _(v):
                    v.tensor_copy(out=W1[0:64, XOFF:XOFF + 512], in_=self.P[0][0:64, :])
                    v.tensor_copy(out=W1[0:64, ZOFF:ZOFF + 512], in_=self.P[1][0:64, :])
                    return v.drain()
            @self.step("t", after={"v": S.cnt["v"]})
            def _(t):
                last = None
                for i in range(8):
                    t.matmul(out=self.P[i // 4][0:64, (i % 4) * 128:(i % 4) * 128 + 128],
                             lhsT=W1[0:64, XOFF + i * 64: XOFF + (i + 1) * 64],
                             rhs=W1[0:64, VB + i * 128: VB + (i + 1) * 128],
                             start=(i % 4 == 0), stop=(i % 4 == 3), skip_group_check=True)
                    t.matmul(out=self.P[2][:, i * 64:(i + 1) * 64],
                             lhsT=W1[0:64, KBT + i * 128: KBT + (i + 1) * 128],
                             rhs=W1[0:64, XOFF + i * 64: XOFF + (i + 1) * 64],
                             start=(i == 0), stop=(i == 7), skip_group_check=True)
                    last = t.matmul(out=self.P[3][0:64, i * 64:(i + 1) * 64],
                                    lhsT=W1[:, KSF + i * 64: KSF + (i + 1) * 64],
                                    rhs=W1[:, QSF + i * 64: QSF + (i + 1) * 64],
                                    start=(i == 0), stop=(i == 7), skip_group_check=True)
                return last

            @self.step("v", after={"t": S.cnt["t"]})
            def _(v):
                v.tensor_copy(out=W1[0:64, UOFF:UOFF + 512], in_=self.P[0][0:64, :])
                v.tensor_copy(out=W1[0:64, UOFF + 512:UOFF + 1024], in_=self.P[1][0:64, :])
                v.tensor_scalar(out=W1[:, WTN:WTN + 512], in0=self.P[2][:, :],
                                scalar1=-1.0, scalar2=None, op0=AL.mult)
                v.tensor_tensor(out=W1[0:64, ATN:ATN + 512], in0=self.P[3][0:64, :],
                                in1=self.MSK[:, 2 * 512:3 * 512], op=AL.mult)
                return v.drain()
            for i in range(8):
                c = b * 8 + i

                @self.step("t", after={"v": S.cnt["v"]})
                def _(t, i=i):
                    return t.matmul(out=self.P[4][0:64, 0:128],
                                    lhsT=W1[:, WTN + i * 64: WTN + (i + 1) * 64],
                                    rhs=self.SBB[:, 1024:1152],
                                    start=True, stop=True, skip_group_check=True)

                @self.step("v", after={"t": S.cnt["t"]})
                def _(v, i=i):
                    return v.tensor_tensor(out=W1[0:64, UOFF + i * 128: UOFF + (i + 1) * 128],
                                           in0=self.P[4][0:64, 0:128],
                                           in1=W1[0:64, UOFF + i * 128: UOFF + (i + 1) * 128],
                                           op=AL.add)

                @self.step("t", after={"v": S.cnt["v"]})
                def _(t, i=i):
                    t.matmul(out=self.P[5][0:64, 0:128],
                             lhsT=W1[:, QSF + i * 64: QSF + (i + 1) * 64],
                             rhs=self.SBB[:, 1024:1152],
                             start=True, stop=False, skip_group_check=True)
                    t.matmul(out=self.P[5][0:64, 0:128],
                             lhsT=W1[0:64, ATN + i * 64: ATN + (i + 1) * 64],
                             rhs=W1[0:64, UOFF + i * 128: UOFF + (i + 1) * 128],
                             start=False, stop=True, skip_group_check=True)
                    return t.matmul(out=self.P[4][:, 0:128],
                                    lhsT=W1[0:64, KST + i * 64 * 2 - i * 64: KST + i * 128 + 128][0:64, 0:128]
                                    if False else W1[0:64, KST + i * 128: KST + (i + 1) * 128],
                                    rhs=W1[0:64, UOFF + i * 128: UOFF + (i + 1) * 128],
                                    start=True, stop=True, skip_group_check=True)

                @self.step("v", after={"t": S.cnt["t"]})
                def _(v):
                    v.tensor_tensor(out=self.AF[:, 0:128], in0=self.AF[:, 0:128],
                                    in1=self.P[4][:, 0:128], op=AL.add)
                    v.drain()
                    v.tensor_copy(out=self.SBB[:, 1024:1152], in_=self.AF[:, 0:128])
                    v.tensor_copy(out=self.SBB[0:64, 1280:1408], in_=self.P[5][0:64, 0:128])
                    return v.drain()

                @self.step("t", after={"v": S.cnt["v"]})
                def _(t):
                    return t.transpose(out=self.PT[:, 0:64], in_=self.SBB[0:64, 1280:1408],
                                       identity=self.EYEB[0:64, 0:64])

                @self.step("v", after={"t": S.cnt["t"]})
                def _(v):
                    return v.tensor_scalar(out=self.SBB[:, 1408:1472], in0=self.PT[:, 0:64],
                                           scalar1=0.5, scalar2=None, op0=AL.mult)
                self.dma("gd", self.olin_sp[hh * 128:(hh + 1) * 128, c * 64:(c + 1) * 64],
                         self.SBB[:, 1408:1472], after={"v": S.cnt["v"]})

        @self.step("v", after={"v": S.cnt["v"], "gd": S.cnt["gd"]})
        def _(v):
            v.memset(self.ROWV[0:1, 520:521], 0.0)
            return v.drain()

        @self.step("s", after={"v": S.cnt["v"]})
        def _(s):
            return s.activation(out=self.WS0[:, :], in_=kh, func=AF.Square)
        for w in range(4):
            @self.step("t", after={"s": S.cnt["s"], "v": S.cnt["v"]})
            def _(t, w=w):
                last = None
                for j in range(4):
                    last = t.matmul(out=self.P[j][0:1, :], lhsT=self.ONEB[:, 0:1],
                                    rhs=self.WS0[:, (w * 4 + j) * 512:(w * 4 + j + 1) * 512],
                                    start=True, stop=True, skip_group_check=True)
                return last

            @self.step("v", after={"t": S.cnt["t"]})
            def _(v, w=w):
                for j in range(4):
                    v.tensor_copy(out=self.T0[0:1, j * 512:(j + 1) * 512], in_=self.P[j][0:1, :])
                v.drain()
                v.tensor_reduce(out=self.ROWV[0:1, 521:522], in_=self.T0[0:1, 0:2048],
                                axis=AX, op=AL.max)
                v.drain()
                v.tensor_tensor(out=self.ROWV[0:1, 520:521], in0=self.ROWV[0:1, 520:521],
                                in1=self.ROWV[0:1, 521:522], op=AL.max)
                return v.drain()

        @self.step("s", after={"v": S.cnt["v"]})
        def _(s):
            return s.activation(out=self.WS0[:, :], in_=qh, func=AF.Square)
        for w in range(4):
            @self.step("t", after={"s": S.cnt["s"], "v": S.cnt["v"]})
            def _(t, w=w):
                last = None
                for j in range(4):
                    last = t.matmul(out=self.P[j][0:1, :], lhsT=self.ONEB[:, 0:1],
                                    rhs=self.WS0[:, (w * 4 + j) * 512:(w * 4 + j + 1) * 512],
                                    start=True, stop=True, skip_group_check=True)
                return last

            @self.step("v", after={"t": S.cnt["t"]})
            def _(v, w=w):
                for j in range(4):
                    v.tensor_copy(out=self.T0[0:1, j * 512:(j + 1) * 512], in_=self.P[j][0:1, :])
                v.drain()
                return v.tensor_scalar(out=self.T1[0:1, 0:2048], in0=self.T0[0:1, 0:2048],
                                       scalar1=self.ROWV[0:1, 520:521], scalar2=None, op0=AL.mult)

            @self.step("s", after={"v": S.cnt["v"]})
            def _(s):
                return s.activation(out=self.T0[0:1, 0:2048], in_=self.T1[0:1, 0:2048],
                                    func=AF.Sqrt)

            @self.step("v", after={"s": S.cnt["s"]})
            def _(v, w=w):
                return v.tensor_scalar(out=self.SBB[0:1, 0:2048],
                                       in0=self.T0[0:1, 0:2048],
                                       scalar1=-1.0, scalar2=None, op0=AL.mult)
            self.dma("gd", self.mrow_sp[hh:hh + 1, w * 2048:(w + 1) * 2048],
                     self.SBB[0:1, 0:2048], after={"v": S.cnt["v"]})

        for qg in range(16):
            njt = 4 * (qg + 1)
            tbm = self.dma("q", self.BIASR[0:1, 1024:1536],
                           self.mrow_sp[hh:hh + 1, qg * 512:(qg + 1) * 512],
                           after={"t": S.cnt["t"], "gd": S.cnt["gd"]})
            for jt in range(njt):
                @self.step("t", after={"s": S.cnt["s"], "v": S.cnt["v"], "q": tbm})
                def _(t, jt=jt, qg=qg):
                    t.matmul(out=self.P[0][:, :], lhsT=kh[:, jt * 128:(jt + 1) * 128],
                             rhs=qh[:, qg * 512:(qg + 1) * 512],
                             start=True, stop=False, skip_group_check=True)
                    return t.matmul(out=self.P[0][:, :], lhsT=self.ONEB[0:1, :],
                                    rhs=self.BIASR[0:1, 1024:1536],
                                    start=False, stop=True, skip_group_check=True)

                @self.step("s", after={"t": S.cnt["t"]})
                def _(s):
                    return s.activation(out=self.SBB[:, 0:512], in_=self.P[0][:, :],
                                        func=AF.Exp, scale=1.0 / SQD)
                if jt >= 4 * qg:
                    @self.step("v", after={"s": S.cnt["s"]})
                    def _(v, r=jt - 4 * qg):
                        return v.tensor_tensor(out=self.SBB[:, 0:512], in0=self.SBB[:, 0:512],
                                               in1=self.BMASK[:, r * 512:(r + 1) * 512],
                                               op=AL.mult)

                @self.step("t", after={"s": S.cnt["s"], "v": S.cnt["v"]})
                def _(t, jt=jt, njt=njt):
                    t.matmul(out=self.P[2][0:1, :], lhsT=self.ONEB[:, 0:1],
                             rhs=self.SBB[:, 0:512],
                             start=(jt == 0), stop=(jt == njt - 1), skip_group_check=True)
                    return t.matmul(out=self.P[1][:, :],
                                    lhsT=self.AB[:, 16384 + jt * 128: 16384 + (jt + 1) * 128],
                                    rhs=self.SBB[:, 0:512],
                                    start=(jt == 0), stop=(jt == njt - 1), skip_group_check=True)
            @self.step("v", after={"t": S.cnt["t"]})
            def _(v):
                v.reciprocal(out=self.ROWV[0:1, 0:512], in_=self.P[2][0:1, :])
                return v.drain()

            @self.step("t", after={"v": S.cnt["v"]})
            def _(t):
                return t.matmul(out=self.P[3][:, :], lhsT=self.ONEF[0:1, :],
                                rhs=self.ROWV[0:1, 0:512], start=True, stop=True)
            tq = self.dma("q", self.SBB[:, 512:1024],
                          self.olin_sp[hh * 128:(hh + 1) * 128, qg * 512:(qg + 1) * 512],
                          after={"v": S.cnt["v"]})

            @self.step("v", after={"t": S.cnt["t"], "q": tq})
            def _(v):
                v.tensor_copy(out=self.T0[:, 0:512], in_=self.P[1][:, :])
                v.drain()
                v.scalar_tensor_tensor(out=self.T1[:, 0:512], in0=self.T0[:, 0:512],
                                       scalar=0.5, in1=self.P[3][:, :],
                                       op0=AL.mult, op1=AL.mult)
                v.drain()
                return v.tensor_tensor(out=self.SBB[:, 1024:1536], in0=self.T1[:, 0:512],
                                       in1=self.SBB[:, 512:1024], op=AL.add)
            self.dma("gd", self.a1_in[hh * 128:(hh + 1) * 128, qg * 512:(qg + 1) * 512],
                     self.SBB[:, 1024:1536], after={"v": S.cnt["v"]})


def host_constants(core):
    c = {}
    c["c_eye"] = np.eye(128, dtype=np.float32)
    c["c_ones"] = np.ones((128, 128), dtype=np.float32)
    sl = np.tril(np.ones((64, 64), np.float32), -1)
    su = np.triu(np.ones((64, 64), np.float32), 1)
    iu = np.triu(np.ones((64, 64), np.float32), 0)
    e64 = np.eye(64, dtype=np.float32)
    c["c_msk"] = np.concatenate([np.tile(sl, (1, 8)), np.tile(su, (1, 8)),
                                 np.tile(iu, (1, 8)), np.tile(e64, (1, 8)),
                                 np.tile(2 * e64, (1, 8))], axis=1)
    j = np.arange(128)[:, None]
    q = np.arange(512)[None, :]
    bm = [(j + r * 128 <= q).astype(np.float32) for r in range(4)]
    c["c_bmask"] = np.concatenate(bm, axis=1)
    pos = np.arange(TL, dtype=np.float64) + core * TL
    inv = 1.0 / (10000.0 ** (np.arange(0, DH, 2, dtype=np.float64) / DH))
    ang = pos[:, None] * inv[None, :]
    c["c_rsin"] = np.sin(ang).astype(np.float32)
    c["c_rcos"] = np.cos(ang).astype(np.float32)
    sel = np.zeros((128, NCORE), np.float32)
    sel[:, :core] = 1.0
    c["c_sel"] = sel
    return c


def qk_perm():
    p = []
    for h in range(H):
        base = h * DH
        p.extend(range(base, base + DH, 2))
        p.extend(range(base + 1, base + DH, 2))
    return np.array(p, dtype=np.int64)


def host_inputs(core, inputs):
    m = {}
    m["x"] = inputs["L0_attn_x"][0, core * TL:(core + 1) * TL, :]
    m["l0_wq"] = inputs["L0_attn_Wq"]
    m["l0_wk"] = inputs["L0_attn_Wk"]
    m["l0_wv"] = inputs["L0_attn_Wv"]
    m["l0_wo"] = inputs["L0_attn_Wo"]
    m["l0_wg"] = inputs["L0_ffn_w_gate"]
    m["l0_wu"] = inputs["L0_ffn_w_up"]
    m["l0_wd"] = inputs["L0_ffn_w_down"]
    m["l1_wq"] = inputs["L1_attn_Wq"]
    m["l1_wk"] = inputs["L1_attn_Wk"]
    m["l1_wv"] = inputs["L1_attn_Wv"]
    m["l1_wg"] = inputs["L1_attn_Wg"]
    m["l1_wo"] = inputs["L1_attn_Wo"]
    perm = qk_perm()
    b = np.stack([np.asarray(inputs["L1_attn_bq"])[perm],
                  np.asarray(inputs["L1_attn_bk"])[perm],
                  np.asarray(inputs["L1_attn_bv"]),
                  np.asarray(inputs["L1_attn_bg"]),
                  np.asarray(inputs["L1_attn_bo"])], axis=0).astype(np.float32)
    m["l1_b"] = b
    m["l1_wnorm"] = np.asarray(inputs["L1_ffn_w_norm"]).reshape(1, D)
    m["l1_wgv"] = inputs["L1_ffn_w_gv"]
    m["l1_wout"] = inputs["L1_ffn_w_out"]
    m.update(host_constants(core))
    return {k: np.ascontiguousarray(np.asarray(v, dtype=np.float32)) for k, v in m.items()}


def build_full(cfg=None):
    cfg = cfg or CFG
    kb = KB(cfg)
    kb.plan_setup()
    ph = cfg.get("phases", ("l0a", "l0f", "l1a", "l1f"))
    need_prep = bool({"l0a", "l0f", "l1a", "l1f"} & set(ph))
    if need_prep:
        kb.plan_weight_prep()
    if "l0a" in ph:
        kb.plan_l0_attn()
    if "l0f" in ph:
        kb.plan_l0_ffn()
    if "l1a" in ph:
        kb.plan_l1_attn()
    if "l1f" in ph:
        kb.plan_l1_ffn(last_block=True)
    else:
        for tc in range(NTC):
            tq = kb.dma("q", kb.T0[:, 0:D], kb.xstream[tc * 128:(tc + 1) * 128, :],
                        after=kb.S.all() if tc == 0 else {"v": kb.S.cnt["v"]})

            @kb.step("v", after={"q": tq})
            def _(v):
                return v.tensor_copy(out=kb.T1[:, 0:D], in_=kb.T0[:, 0:D])
            kb.dma("q", kb.y_out[tc * 128:(tc + 1) * 128, :], kb.T1[:, 0:D],
                   after={"v": kb.S.cnt["v"]})
    kb.plan_output_wait()
    nc = kb.finalize()
    return nc


def kernel(**inputs):
    nc = build_full(CFG)
    in_maps = [host_inputs(c, inputs) for c in range(NCORE)]
    res = run_bass_kernel_spmd(nc, in_maps, core_ids=list(range(NCORE)))
    out = np.concatenate([np.asarray(r["y"]) for r in res.results], axis=0)
    return out.reshape(1, T, D).astype(np.float32)


# version 06
# speedup vs baseline: 1.0518x; 1.0518x over previous
import contextlib
import numpy as np
import concourse.bass as bass
import concourse.mybir as mybir
from concourse.bass_utils import run_bass_kernel_spmd

F32 = mybir.dt.float32
BF16 = mybir.dt.bfloat16
FP8 = mybir.dt.float8e4
AL = mybir.AluOpType
AF = mybir.ActivationFunctionType
AX = mybir.AxisListType.X

MAGIC = 3.0 * 2.0**22
LN2 = float(np.log(2.0))
NCORE = 8
T = 8192
TL = T // NCORE
D = 2048
H = 16
DH = 128
HL = H // NCORE
CH = 64
NCHK = T // CH
NCHKL = TL // CH
I0 = 8192
HID1 = 8192
SQD = float(np.sqrt(DH))
NTC = TL // 128
GATE_NORM = 16.0
CONST_VALS = [-2.0 * float(np.log(2.0)), 2.0 * float(np.log(2.0)),
              4.0 * float(np.log(2.0)), -4.0 * float(np.log(2.0))]

CFG = {"phases": ("l0a", "l0f", "l1a", "l1f"), "taps": ()}


class Sched:
    def __init__(self, nc):
        self.nc = nc
        self.steps = []
        self.cnt = {k: 0 for k in ("t", "v", "s", "g", "q", "gd", "sd", "cc")}
        self.sems = {}
        self.hiw = {}

    def make_sems(self, stack):
        for k in self.cnt:
            self.sems[k] = stack.enter_context(self.nc.semaphore("sem_" + k))

    def all(self):
        return {k: v for k, v in self.cnt.items() if v > 0}

    def emit_for(self, ek_engine, eng):
        for ek, fn, waits in self.steps:
            if ek != ek_engine:
                continue
            for sk, val in waits.items():
                hk = (ek, sk)
                if self.hiw.get(hk, 0) >= val:
                    continue
                self.hiw[hk] = val
                eng.wait_ge(self.sems[sk], val)
            fn(eng)


class KB:
    def __init__(self, cfg):
        self.cfg = cfg
        self.nc = bass.Bass("TRN2", target_bir_lowering=False, debug=False)
        self.S = Sched(self.nc)
        self.stack = contextlib.ExitStack()
        self._declare_dram()
        self._declare_sbuf()
        self.S.make_sems(self.stack)
        self.taps = []
        self._dynreg = {}

    def _declare_dram(self):
        nc = self.nc

        def din(name, shape, dt=F32):
            return nc.dram_tensor(name, list(shape), dt, kind="ExternalInput").ap()

        def mk(name, shape, dt):
            return nc.dram_tensor(name, list(shape), dt, kind="Internal").ap()

        self.x_in = din("x", (TL, D))
        self.y_out = nc.dram_tensor("y", [TL, D], F32, kind="ExternalOutput").ap()
        w = {}
        for n, shape in [("l0_wq", (D, D)), ("l0_wk", (D, D)), ("l0_wv", (D, D)),
                         ("l0_wo", (D, D)), ("l0_wg", (I0, D)), ("l0_wu", (I0, D)),
                         ("l0_wd", (D, I0)), ("l1_wq", (D, D)), ("l1_wk", (D, D)),
                         ("l1_wv", (D, D)), ("l1_wg", (D, D)), ("l1_wo", (D, D)),
                         ("l1_b", (5, D)), ("l1_wnorm", (1, D)),
                         ("l1_wgv", (2 * HID1, D)), ("l1_wout", (D, HID1))]:
            w[n] = din(n, shape)
        self.w = w
        c = {}
        for k, shape in [("eye", (128, 128)), ("ones", (128, 128)),
                         ("msk", (64, 5 * 512)), ("bmask", (128, 4 * 512)),
                         ("rsin", (TL, 64)), ("rcos", (TL, 64)),
                         ("sel", (128, NCORE))]:
            c[k] = din("c_" + k, shape)
        self.c = c
        self.w0_in = mk("w0_in", (4 * 256, D), BF16)
        self.w0_out = mk("w0_out", (NCORE * 4 * 256, D), BF16)
        self.w1_in = mk("w1_in", (2 * 1024, D), BF16)
        self.w1_out = mk("w1_out", (2 * I0, D), BF16)
        self.w1b_in = mk("w1b_in", (256, I0), BF16)
        self.w1b_out = mk("w1b_out", (D, I0), BF16)
        self.w2_in = mk("w2_in", (5 * 256, D), BF16)
        self.w2_out = mk("w2_out", (5 * D, D), BF16)
        self.w3_in = mk("w3_in", (D, D), BF16)
        self.w3_out = mk("w3_out", (2 * HID1, D), BF16)
        self.w3b_in = mk("w3b_in", (256, HID1), BF16)
        self.w3b_out = mk("w3b_out", (D, HID1), BF16)
        self.sc_in = mk("sc_in", (1, 8), F32)
        self.sc_out = mk("sc_out", (1, 8), F32)
        self.a0_in = mk("a0_in", (3 * D, TL), BF16)
        self.a0_out = mk("a0_out", (NCORE * 3 * D, TL), BF16)
        self.a1_in = mk("a1_in", (HL * DH, T), BF16)
        self.a1_out = mk("a1_out", (D, T), BF16)
        self.s1_in = mk("s1_in", (H * DH, 136), F32)
        self.s1_out = mk("s1_out", (NCORE * H * DH, 136), F32)
        self.xstream = mk("xstream", (TL, D), F32)
        self.u_spill = mk("u_spill", (TL, HID1), BF16)
        self.qkvg_sp = mk("qkvg_sp", (4 * TL, D), BF16)
        self.brow_sp = mk("brow_sp", (1, 6 * 2048), BF16)
        self.st_w0 = [mk("st_w0_%d" % i, (256, D), F32) for i in range(4)]
        self.st_wg = mk("st_wg", (1024, D), F32)
        self.st_wu = mk("st_wu", (1024, D), F32)
        self.st_wd = mk("st_wd", (256, I0), F32)
        self.st_wgv = mk("st_wgv", (2048, D), F32)
        self.st_wout = mk("st_wout", (256, I0), F32)
        self.hstage = mk("hstage", (128, NCORE * 3 * 1024), BF16)
        self.a1st = mk("a1st", (1, D, TL), BF16)
        self.mrow_sp = mk("mrow_sp", (HL, T), BF16)
        self.mk = mk

    def _declare_sbuf(self):
        nc, st = self.nc, self.stack

        def sbuf(name, shape, dt):
            return st.enter_context(nc.sbuf_tensor(name, list(shape), dt))

        def psum(name, shape):
            return st.enter_context(nc.psum_tensor(name, list(shape), F32))

        self.AF = sbuf("A_f32", (128, 16384), F32)
        self.AB = sbuf("A_bf", (128, 24576), BF16)
        self.SBB = sbuf("s_bf", (128, 2048), BF16)
        self.WS0 = sbuf("ws0", (128, 8192), BF16)
        self.WS1 = sbuf("ws1", (128, 8192), BF16)
        self.T0 = sbuf("t0", (128, 2048), F32)
        self.T1 = sbuf("t1", (128, 2048), F32)
        self.ROPE = sbuf("ropebuf", (128, 128), F32)
        self.EYEB = sbuf("eyeb", (128, 128), BF16)
        self.ONEF = sbuf("onef", (128, 128), F32)
        self.ONEB = sbuf("oneb", (128, 128), BF16)
        self.MSK = sbuf("mskS", (64, 5 * 512), BF16)
        self.BMASK = sbuf("bmaskb", (128, 4 * 512), BF16)
        self.SEL = sbuf("selS", (128, NCORE), F32)
        self.SM0 = sbuf("sm0", (128, 128), F32)
        self.SM1 = sbuf("sm1", (128, 128), F32)
        self.SM2 = sbuf("sm2", (128, 128), F32)
        self.SM3 = sbuf("sm3", (128, 128), F32)
        self.CONSTC = sbuf("constc", (128, 16), F32)
        for ci, val in enumerate(CONST_VALS):
            self.nc.const_aps.register(F32, val, self.CONSTC[:, 3 + ci:4 + ci])
        self.ROWV = sbuf("rowv", (1, 1024), F32)
        self.BIASR = sbuf("biasrow", (1, 2048), BF16)
        self.WNBB = sbuf("wnbb", (128, 2048), BF16)
        self.FQ = sbuf("fq", (128, 1024), FP8)
        self.P = [psum("p%d" % i, (128, 512)) for i in range(6)]
        self.PT = self.stack.enter_context(self.nc.psum_tensor("pt0", [128, 512], BF16))
        self.PT2 = self.stack.enter_context(self.nc.psum_tensor("pt1", [128, 512], BF16))

    def dma(self, qk, out, in_, after=None, **kw):
        S = self.S
        ek = {"q": "sync", "gd": "g", "sd": "s"}[qk]
        S.cnt[qk] += 16
        sem = S.sems[qk]

        def fn(eng, out=out, in_=in_, kw=kw):
            eng.dma_start(out=out, in_=in_, **kw).then_inc(sem, 16)
        S.steps.append((ek, fn, {k: v for k, v in (after or {}).items() if v > 0}))
        return S.cnt[qk]

    def dmaT(self, qk, out, in_, after=None):
        S = self.S
        ek = {"q": "sync", "gd": "g", "sd": "s"}[qk]
        S.cnt[qk] += 16
        sem = S.sems[qk]

        def fn(eng, out=out, in_=in_):
            eng.dma_start_transpose(out=out, in_=in_).then_inc(sem, 16)
        S.steps.append((ek, fn, {k: v for k, v in (after or {}).items() if v > 0}))
        return S.cnt[qk]

    def _dyn_off(self, eng, mult):
        key = (id(eng), mult)
        if key not in self._dynreg:
            pid = self.nc.partition_id()
            self._dynreg[key] = eng.compute_val(pid * mult)
        return self._dynreg[key]

    def dma_dyn(self, qk, out, src_tensor, mult, roff, nrows, cols, after=None):
        S = self.S
        ek = {"q": "sync", "gd": "g", "sd": "s"}[qk]
        S.cnt[qk] += 16
        sem = S.sems[qk]

        def fn(eng, out=out):
            off = self._dyn_off(eng, mult)
            nr = src_tensor.shape[0]
            base = src_tensor[roff:nr, cols] if roff else src_tensor[:, cols]
            eng.dma_start(out=out, in_=base[bass.ds(off, nrows), :]).then_inc(sem, 16)
        S.steps.append((ek, fn, {k: v for k, v in (after or {}).items() if v > 0}))
        return S.cnt[qk]

    def step(self, ek, after=None):
        S = self.S
        S.cnt[ek] += 1
        sem = S.sems[ek]

        def deco(fn):
            def wrapped(eng, fn=fn):
                last = fn(eng)
                assert last is not None
                last.then_inc(sem, 1)
            S.steps.append((ek, wrapped, {k: v for k, v in (after or {}).items() if v > 0}))
        return deco

    def cc(self, kind, op, ins, outs, after=None):
        S = self.S
        S.cnt["cc"] += 1
        sem = S.sems["cc"]

        def fn(eng, ins=ins, outs=outs):
            eng.collective_compute(
                kind, op, replica_groups=[list(range(NCORE))],
                ins=[a.opt() for a in ins], outs=[a.opt() for a in outs],
            ).then_inc(sem, 1)
        S.steps.append(("g", fn, {k: v for k, v in (after or {}).items() if v > 0}))
        return S.cnt["cc"]

    def tap(self, name, shape, dt, src_fn):
        if name not in self.cfg.get("taps", ()):
            return
        t = self.nc.dram_tensor("tap_" + name, list(shape), dt, kind="ExternalOutput").ap()
        self.dma("gd", t[:, :], src_fn(), after=self.S.all())

    def plan_setup(self):
        c, S = self.c, self.S
        self.dma("q", self.T1[:, 0:128], c["eye"][:, :])
        self.dma("q", self.ONEF[:, :], c["ones"][:, :])
        self.dma("q", self.AF[0:64, 0:2560], c["msk"][:, :])
        self.dma("q", self.SEL[:, :], c["sel"][:, :])
        self.dma("q", self.T0[:, 0:2048], c["bmask"][:, :])
        tq = self.dma("q", self.xstream[:, :], self.x_in[:, :])

        @self.step("v", after={"q": tq})
        def _(v):
            v.tensor_copy(out=self.EYEB[:, :], in_=self.T1[:, 0:128])
            v.tensor_copy(out=self.ONEB[:, :], in_=self.ONEF[:, :])
            v.tensor_copy(out=self.MSK[:, :], in_=self.AF[0:64, 0:2560])
            v.tensor_copy(out=self.BMASK[:, :], in_=self.T0[:, 0:2048])
            v.memset(self.CONSTC[:, 0:1], 2.0)
            v.memset(self.CONSTC[:, 1:2], 4.0)
            v.memset(self.CONSTC[:, 2:3], 0.0)
            for ci, val in enumerate(CONST_VALS):
                v.memset(self.CONSTC[:, 3 + ci:4 + ci], val)
            return v.drain()
        for mb in range(5):
            tq = self.dma("q", self.AF[0:1, mb * 2048:(mb + 1) * 2048],
                          self.w["l1_b"][mb:mb + 1, :],
                          after={"v": S.cnt["v"]} if mb == 0 else None)
        tq = self.dma("q", self.AF[0:1, 10240:12288], self.w["l1_wnorm"][:, :])
        for mth in range(6):
            @self.step("v", after={"q": tq} if mth == 0 else {"gd": S.cnt["gd"]})
            def _(v, mth=mth):
                return v.tensor_copy(out=self.BIASR[0:1, 0:2048],
                                     in_=self.AF[0:1, mth * 2048:(mth + 1) * 2048])
            self.dma("gd", self.brow_sp[0:1, mth * 2048:(mth + 1) * 2048],
                     self.BIASR[0:1, 0:2048], after={"v": S.cnt["v"]})

    def plan_weight_prep(self):
        S = self.S
        for i, wn in enumerate(["l0_wq", "l0_wk", "l0_wv", "l0_wo"]):
            self.dma_dyn("q", self.st_w0[i][:, :], self.w[wn], 256, 0, 256, slice(0, D))
        self.dma_dyn("q", self.st_wg[:, :], self.w["l0_wg"], 1024, 0, 1024, slice(0, D))
        self.dma_dyn("q", self.st_wu[:, :], self.w["l0_wu"], 1024, 0, 1024, slice(0, D))
        self.dma_dyn("q", self.st_wd[:, :], self.w["l0_wd"], 256, 0, 256, slice(0, I0))
        self.dma_dyn("q", self.st_wgv[:, :], self.w["l1_wgv"], 2048, 0, 2048, slice(0, D))
        t_stage = self.dma_dyn("q", self.st_wout[:, :], self.w["l1_wout"], 256, 0, 256,
                               slice(0, I0))
        self.t_stage = t_stage
        for i, wn in enumerate(["l0_wq", "l0_wk", "l0_wv", "l0_wo"]):
            for half in range(2):
                tq = self.dma("q", self.T0[:, 0:2048],
                              self.st_w0[i][half * 128:(half + 1) * 128, :],
                              after={"v": S.cnt["v"], "gd": S.cnt["gd"]})

                @self.step("v", after={"q": tq})
                def _(v):
                    return v.tensor_copy(out=self.SBB[:, 0:2048], in_=self.T0[:, 0:2048])
                r0 = i * 256 + half * 128
                self.dma("gd", self.w0_in[r0:r0 + 128, :], self.SBB[:, 0:2048],
                         after={"v": S.cnt["v"]})
        self.t_w0 = self.cc("AllGather", AL.bypass, [self.w0_in], [self.w0_out],
                            after={"gd": S.cnt["gd"]})

        jobs = []
        for i, wn in enumerate(["l0_wg", "l0_wu"]):
            for r in range(8):
                jobs.append((wn, 1024, r * 128, slice(0, 2048),
                             self.w1_in, i * 1024 + r * 128, slice(0, 2048)))
        for r in range(2):
            for p in range(4):
                jobs.append(("l0_wd", 256, r * 128, slice(p * 2048, (p + 1) * 2048),
                             self.w1b_in, r * 128, slice(p * 2048, (p + 1) * 2048)))
        stmap = {"l0_wg": self.st_wg, "l0_wu": self.st_wu, "l0_wd": self.st_wd,
                 "l1_wgv": self.st_wgv, "l1_wout": self.st_wout}
        for (wn, srows, roff, csl, dest, drow, dsl) in jobs:
            tq = self.dma("q", self.AF[:, 0:2048], stmap[wn][roff:roff + 128, csl],
                          after={"v": S.cnt["v"], "s": S.cnt["s"], "gd": S.cnt["gd"]})
            self._emit_e2m1(after={"q": tq})
            self.dma("gd", dest[drow:drow + 128, dsl], self.SBB[:, 0:2048],
                     after={"v": S.cnt["v"]})
        self.t_w1 = self.cc("AllGather", AL.bypass, [self.w1_in], [self.w1_out],
                            after={"gd": S.cnt["gd"]})
        self.t_w1b = self.cc("AllGather", AL.bypass, [self.w1b_in], [self.w1b_out],
                             after={"gd": S.cnt["gd"]})

        self._plan_l1w_prep()
        self._plan_ternary()

    def _emit_e2m1(self, after):
        nb = 64
        x3 = self.AF[:, 0:2048].rearrange("p (n b) -> p n b", b=32)

        @self.step("v", after=after)
        def _(v):
            v.tensor_reduce(out=self.SM0[:, 0:nb], in_=x3, axis=AX, op=AL.max,
                            apply_absolute_value=True)
            v.drain()
            v.tensor_scalar(out=self.SM0[:, 0:nb], in0=self.SM0[:, 0:nb],
                            scalar1=1e-30, scalar2=None, op0=AL.max)
            return v.drain()

        @self.step("s", after={"v": self.S.cnt["v"]})
        def _(s):
            return s.activation(out=self.SM1[:, 0:nb], in_=self.SM0[:, 0:nb], func=AF.Ln)

        @self.step("v", after={"s": self.S.cnt["s"]})
        def _(v):
            v.tensor_scalar(out=self.SM1[:, 0:nb], in0=self.SM1[:, 0:nb],
                            scalar1=1.0 / LN2, scalar2=-0.5, op0=AL.mult, op1=AL.add)
            v.drain()
            v.tensor_scalar(out=self.SM1[:, 0:nb], in0=self.SM1[:, 0:nb],
                            scalar1=MAGIC, scalar2=MAGIC, op0=AL.add, op1=AL.subtract)
            return v.drain()

        @self.step("s", after={"v": self.S.cnt["v"]})
        def _(s):
            s.activation(out=self.SM0[:, 0:nb], in_=self.SM1[:, 0:nb], func=AF.Exp,
                         scale=LN2, bias=-2.0 * LN2)
            return s.activation(out=self.SM2[:, 0:nb], in_=self.SM1[:, 0:nb], func=AF.Exp,
                                scale=-LN2, bias=2.0 * LN2)

        @self.step("v", after={"s": self.S.cnt["s"]})
        def _(v):
            v.drain()
            ibs = self.SM2[:, 0:nb].broadcast_to((128, nb, 32))
            bs = self.SM0[:, 0:nb].broadcast_to((128, nb, 32))
            yy = self.T1[:, 0:2048]
            y3 = yy.rearrange("p (n b) -> p n b", b=32)
            a = self.T0[:, 0:2048]
            t2 = self.AF[:, 4096:6144]
            t4 = self.AF[:, 6144:8192]
            c2 = self.CONSTC[:, 0:1].broadcast_to((128, 2048))
            c4 = self.CONSTC[:, 1:2].broadcast_to((128, 2048))
            v.tensor_tensor(out=y3, in0=x3, in1=ibs, op=AL.mult)
            v.tensor_scalar(out=a, in0=yy, scalar1=-1.0, scalar2=None, op0=AL.mult)
            v.tensor_tensor(out=a, in0=a, in1=yy, op=AL.max)
            v.tensor_tensor(out=t2, in0=a, in1=c2, op=AL.is_ge)
            v.tensor_tensor(out=t4, in0=a, in1=c4, op=AL.is_ge)
            v.tensor_scalar(out=a, in0=t4, scalar1=-0.5, scalar2=2.0, op0=AL.mult, op1=AL.add)
            v.scalar_tensor_tensor(out=a, in0=t2, scalar=-1.0, in1=a, op0=AL.mult, op1=AL.add)
            v.tensor_tensor(out=yy, in0=yy, in1=a, op=AL.mult)
            v.tensor_scalar(out=yy, in0=yy, scalar1=MAGIC, scalar2=MAGIC,
                            op0=AL.add, op1=AL.subtract)
            v.tensor_scalar(out=a, in0=t2, scalar1=0.5, scalar2=0.5, op0=AL.mult, op1=AL.add)
            v.tensor_tensor(out=a, in0=a, in1=t4, op=AL.add)
            v.tensor_tensor(out=yy, in0=yy, in1=a, op=AL.mult)
            v.tensor_scalar(out=yy, in0=yy, scalar1=6.0, scalar2=-6.0, op0=AL.min, op1=AL.max)
            o3 = self.SBB[:, 0:2048].rearrange("p (n b) -> p n b", b=32)
            return v.tensor_tensor(out=o3, in0=y3, in1=bs, op=AL.mult)

    def _plan_l1w_prep(self):
        S = self.S
        self.l1wst = {}
        for i, wn in enumerate(["l1_wq", "l1_wk", "l1_wv", "l1_wg", "l1_wo"]):
            st = self.mk("l1wst_" + wn, (1, 2048, 256), F32)
            self.l1wst[wn] = st
            Sq = self.S
            Sq.cnt["q"] += 16
            sem = Sq.sems["q"]

            def stfn(eng, wn=wn, st=st, sem=sem):
                p1 = self._dyn_off(eng, 1)
                view = self.w[wn].rearrange("r (n c) -> n r c", c=256)
                eng.dma_start(out=st[:, :, :], in_=view[bass.ds(p1, 1), :, :]).then_inc(sem, 16)
            Sq.steps.append(("sync", stfn, {}))
        t_stage = S.cnt["q"]
        for i, wn in enumerate(["l1_wq", "l1_wk", "l1_wv", "l1_wg", "l1_wo"]):
            perm = wn in ("l1_wq", "l1_wk")
            for rt in range(16):
                ek_sem = S.sems["q"]
                S.cnt["q"] += 32 if perm else 16
                waits = {"v": S.cnt["v"], "t": S.cnt["t"]}

                def ldfn(eng, wn=wn, rt=rt, perm=perm, sem=ek_sem):
                    st = self.l1wst[wn]
                    if perm:
                        eng.dma_start(out=self.T0[0:64, 0:256],
                                      in_=st[0, rt * 128 + 0: rt * 128 + 128: 2, :]).then_inc(sem, 16)
                        eng.dma_start(out=self.T0[64:128, 0:256],
                                      in_=st[0, rt * 128 + 1: rt * 128 + 128: 2, :]).then_inc(sem, 16)
                    else:
                        eng.dma_start(out=self.T0[:, 0:256],
                                      in_=st[0, rt * 128: rt * 128 + 128, :]).then_inc(sem, 16)
                S.steps.append(("sync", ldfn, waits))
                tq = S.cnt["q"]

                @self.step("v", after={"q": tq})
                def _(v):
                    return v.tensor_copy(out=self.SBB[:, 0:256], in_=self.T0[:, 0:256])

                @self.step("t", after={"v": S.cnt["v"]})
                def _(t):
                    t.transpose(out=self.PT[:, 0:128], in_=self.SBB[:, 0:128],
                                identity=self.EYEB[:, :])
                    return t.transpose(out=self.PT[:, 128:256], in_=self.SBB[:, 128:256],
                                       identity=self.EYEB[:, :])

                @self.step("v", after={"t": S.cnt["t"]})
                def _(v):
                    return v.tensor_copy(out=self.SBB[:, 256:512], in_=self.PT[:, 0:256])
                tv = S.cnt["v"]
                self.dma("gd", self.w2_in[i * 256: i * 256 + 128, rt * 128: rt * 128 + 128],
                         self.SBB[:, 256:384], after={"v": tv})
                self.dma("gd", self.w2_in[i * 256 + 128: i * 256 + 256, rt * 128: rt * 128 + 128],
                         self.SBB[:, 384:512], after={"v": tv})
        self.t_w2 = self.cc("AllGather", AL.bypass, [self.w2_in], [self.w2_out],
                            after={"gd": S.cnt["gd"]})

    def _plan_ternary(self):
        S = self.S

        @self.step("v", after={"v": S.cnt["v"]})
        def _(v):
            v.memset(self.SM3[:, 0:2], 0.0)
            return v.drain()
        jobs = [("l1_wgv", 2048, r * 128, slice(0, 2048), 0, self.w3_in, r * 128, slice(0, 2048))
                for r in range(16)]
        for r in range(2):
            for p in range(4):
                jobs.append(("l1_wout", 256, r * 128, slice(p * 2048, (p + 1) * 2048), 1,
                             self.w3b_in, r * 128, slice(p * 2048, (p + 1) * 2048)))
        stmap = {"l1_wgv": self.st_wgv, "l1_wout": self.st_wout}
        for (wn, srows, roff, csl, col, dest, drow, dsl) in jobs:
            tq = self.dma("q", self.AF[:, 0:2048], stmap[wn][roff:roff + 128, csl],
                          after={"v": S.cnt["v"]})

            @self.step("v", after={"q": tq})
            def _(v, col=col):
                v.tensor_reduce(out=self.SM1[:, 4:5], in_=self.AF[:, 0:2048], axis=AX,
                                op=AL.add, apply_absolute_value=True)
                v.drain()
                v.tensor_tensor(out=self.SM3[:, col:col + 1], in0=self.SM3[:, col:col + 1],
                                in1=self.SM1[:, 4:5], op=AL.add)
                return v.drain()

        @self.step("v", after={"v": S.cnt["v"]})
        def _(v):
            return v.tensor_copy(out=self.SBB[:, 0:2], in_=self.SM3[:, 0:2])

        @self.step("t", after={"v": S.cnt["v"]})
        def _(t):
            return t.matmul(out=self.P[0][0:1, 0:2], lhsT=self.ONEB[:, 0:1],
                            rhs=self.SBB[:, 0:2], start=True, stop=True)

        @self.step("v", after={"t": S.cnt["t"]})
        def _(v):
            return v.tensor_copy(out=self.ROWV[0:1, 0:2], in_=self.P[0][0:1, 0:2])
        self.dma("gd", self.sc_in[:, 0:2], self.ROWV[0:1, 0:2], after={"v": S.cnt["v"]})
        t_ar = self.cc("AllReduce", AL.add, [self.sc_in], [self.sc_out],
                       after={"gd": S.cnt["gd"]})
        tq = self.dma("q", self.ROWV[0:1, 4:6], self.sc_out[:, 0:2], after={"cc": t_ar})

        @self.step("v", after={"q": tq})
        def _(v):
            v.tensor_scalar(out=self.ROWV[0:1, 6:7], in0=self.ROWV[0:1, 4:5],
                            scalar1=1.0 / (2.0 * HID1 * D), scalar2=1e-5,
                            op0=AL.mult, op1=AL.max)
            v.tensor_scalar(out=self.ROWV[0:1, 7:8], in0=self.ROWV[0:1, 5:6],
                            scalar1=1.0 / (1.0 * D * HID1), scalar2=1e-5,
                            op0=AL.mult, op1=AL.max)
            v.drain()
            v.reciprocal(out=self.ROWV[0:1, 8:9], in_=self.ROWV[0:1, 6:7])
            v.drain()
            v.reciprocal(out=self.ROWV[0:1, 9:10], in_=self.ROWV[0:1, 7:8])
            return v.drain()

        @self.step("t", after={"v": S.cnt["v"]})
        def _(t):
            return t.matmul(out=self.P[0][:, 0:4], lhsT=self.ONEF[0:1, :],
                            rhs=self.ROWV[0:1, 6:10], start=True, stop=True)

        @self.step("v", after={"t": S.cnt["t"]})
        def _(v):
            v.tensor_copy(out=self.SM3[:, 4:6], in_=self.P[0][:, 0:2])
            v.tensor_copy(out=self.SM3[:, 6:8], in_=self.P[0][:, 2:4])
            return v.drain()
        for (wn, srows, roff, csl, col, dest, drow, dsl) in jobs:
            tq = self.dma("q", self.AF[:, 0:2048], stmap[wn][roff:roff + 128, csl],
                          after={"v": S.cnt["v"], "gd": S.cnt["gd"]})

            @self.step("v", after={"q": tq})
            def _(v, col=col):
                yy = self.T1[:, 0:2048]
                v.tensor_scalar(out=yy, in0=self.AF[:, 0:2048],
                                scalar1=self.SM3[:, 6 + col:7 + col], scalar2=MAGIC,
                                op0=AL.mult, op1=AL.add)
                v.tensor_scalar(out=yy, in0=yy, scalar1=MAGIC, scalar2=1.0,
                                op0=AL.subtract, op1=AL.min)
                return v.tensor_scalar(out=self.SBB[:, 0:2048], in0=yy, scalar1=-1.0,
                                       scalar2=None, op0=AL.max)
            self.dma("gd", dest[drow:drow + 128, dsl], self.SBB[:, 0:2048],
                     after={"v": S.cnt["v"]})
        self.t_w3 = self.cc("AllGather", AL.bypass, [self.w3_in], [self.w3_out],
                            after={"gd": S.cnt["gd"]})
        self.t_w3b = self.cc("AllGather", AL.bypass, [self.w3b_in], [self.w3b_out],
                             after={"gd": S.cnt["gd"]})

    def _emit_mxact(self, x_ap, nb, out_bf, fq_cols, after):
        ncol = nb * 32
        x3 = x_ap.rearrange("p (n b) -> p n b", b=32)
        cb = slice(64, 64 + nb)

        @self.step("v", after=after)
        def _(v):
            v.tensor_reduce(out=self.SM0[:, cb], in_=x3, axis=AX, op=AL.max,
                            apply_absolute_value=True)
            v.drain()
            v.tensor_scalar(out=self.SM0[:, cb], in0=self.SM0[:, cb],
                            scalar1=1e-30, scalar2=None, op0=AL.max)
            return v.drain()

        @self.step("s", after={"v": self.S.cnt["v"]})
        def _(s):
            return s.activation(out=self.SM1[:, cb], in_=self.SM0[:, cb], func=AF.Ln)

        @self.step("v", after={"s": self.S.cnt["s"]})
        def _(v):
            v.tensor_scalar(out=self.SM1[:, cb], in0=self.SM1[:, cb],
                            scalar1=1.0 / LN2, scalar2=-0.5, op0=AL.mult, op1=AL.add)
            v.drain()
            v.tensor_scalar(out=self.SM1[:, cb], in0=self.SM1[:, cb],
                            scalar1=MAGIC, scalar2=MAGIC, op0=AL.add, op1=AL.subtract)
            return v.drain()

        @self.step("s", after={"v": self.S.cnt["v"]})
        def _(s):
            s.activation(out=self.SM0[:, cb], in_=self.SM1[:, cb], func=AF.Exp,
                         scale=LN2, bias=4.0 * LN2)
            return s.activation(out=self.SM2[:, cb], in_=self.SM1[:, cb], func=AF.Exp,
                                scale=-LN2, bias=-4.0 * LN2)

        @self.step("v", after={"s": self.S.cnt["s"]})
        def _(v):
            v.drain()
            ibs = self.SM2[:, cb].broadcast_to((128, nb, 32))
            bs = self.SM0[:, cb].broadcast_to((128, nb, 32))
            fq = self.FQ[:, fq_cols]
            f3 = fq.rearrange("p (n b) -> p n b", b=32)
            v.tensor_tensor(out=f3, in0=x3, in1=ibs, op=AL.mult)
            v.tensor_scalar(out=fq, in0=fq, scalar1=0.1171875, scalar2=-0.1171875,
                            op0=AL.min, op1=AL.max)
            o3 = out_bf.rearrange("p (n b) -> p n b", b=32)
            return v.tensor_tensor(out=o3, in0=f3, in1=bs, op=AL.mult)

    def trans_tiles(self, pairs, after):
        S = self.S
        n = len(pairs)
        for i0 in range(0, n, 2):
            grp = pairs[i0:i0 + 2]

            @self.step("t", after={"v": S.cnt["v"], "t": S.cnt["t"]} if i0 else dict(after))
            def _(t, grp=grp, i0=i0):
                last = None
                for j, (srcap, _d) in enumerate(grp):
                    pp = self.PT if j == 0 else self.PT2
                    last = t.transpose(out=pp[:, 0:128], in_=srcap,
                                       identity=self.EYEB[:, :])
                return last

            @self.step("v", after={"t": S.cnt["t"]})
            def _(v, grp=grp):
                last = None
                for j, (_s, dstap) in enumerate(grp):
                    pp = self.PT if j == 0 else self.PT2
                    last = v.tensor_copy(out=dstap, in_=pp[:, 0:128])
                return last
        return S.cnt["v"]

    def ln_write(self, t1_ap, dst_hbm_rows, after, extra_scale=None):
        S = self.S

        @self.step("v", after=after)
        def _(v):
            v.tensor_reduce(out=self.SM1[:, 8:9], in_=t1_ap, axis=AX, op=AL.add)
            v.tensor_tensor(out=self.T0[:, 0:D], in0=t1_ap, in1=t1_ap, op=AL.mult)
            v.tensor_reduce(out=self.SM1[:, 9:10], in_=self.T0[:, 0:D], axis=AX, op=AL.add)
            v.drain()
            v.tensor_scalar(out=self.SM1[:, 8:9], in0=self.SM1[:, 8:9],
                            scalar1=1.0 / D, scalar2=None, op0=AL.mult)
            v.tensor_scalar(out=self.SM1[:, 9:10], in0=self.SM1[:, 9:10],
                            scalar1=1.0 / D, scalar2=None, op0=AL.mult)
            v.drain()
            v.tensor_tensor(out=self.SM1[:, 10:11], in0=self.SM1[:, 8:9],
                            in1=self.SM1[:, 8:9], op=AL.mult)
            v.drain()
            v.tensor_tensor(out=self.SM1[:, 10:11], in0=self.SM1[:, 9:10],
                            in1=self.SM1[:, 10:11], op=AL.subtract)
            v.drain()
            v.tensor_scalar(out=self.SM1[:, 10:11], in0=self.SM1[:, 10:11],
                            scalar1=1e-5, scalar2=None, op0=AL.add)
            return v.drain()

        @self.step("s", after={"v": S.cnt["v"]})
        def _(s):
            return s.activation(out=self.SM1[:, 11:12], in_=self.SM1[:, 10:11], func=AF.Sqrt)

        @self.step("v", after={"s": S.cnt["s"]})
        def _(v):
            v.reciprocal(out=self.SM1[:, 12:13], in_=self.SM1[:, 11:12])
            v.drain()
            return v.tensor_scalar(out=self.T1[:, 0:D], in0=t1_ap,
                                   scalar1=self.SM1[:, 8:9], scalar2=self.SM1[:, 12:13],
                                   op0=AL.subtract, op1=AL.mult)
        tq = self.dma("q", dst_hbm_rows, self.T1[:, 0:D], after={"v": S.cnt["v"]})
        return tq

    def plan_l0_ffn(self):
        S = self.S
        start = S.all()
        for tc in range(NTC):
            tq = self.dma("q", self.T0[:, 0:D], self.xstream[tc * 128:(tc + 1) * 128, :],
                          after=start if tc == 0 else {"v": S.cnt["v"]})

            @self.step("v", after={"q": tq})
            def _(v):
                return v.tensor_copy(out=self.AF[:, 0:D], in_=self.T0[:, 0:D])
            self._emit_mxact(self.AF[:, 0:1024], 32, self.SBB[:, 0:1024], slice(0, 1024),
                             after={"v": S.cnt["v"]})
            self._emit_mxact(self.AF[:, 1024:2048], 32, self.SBB[:, 1024:2048], slice(0, 1024),
                             after={"v": S.cnt["v"]})
            pairs = [(self.SBB[:, dc * 128:(dc + 1) * 128],
                      self.AB[:, dc * 1024 + tc * 128: dc * 1024 + tc * 128 + 128])
                     for dc in range(16)]
            self.trans_tiles(pairs, after={"v": S.cnt["v"]})
        t_xq = S.all()

        WST = self.WS0
        for os in range(8):
            tw = 0
            for dc in range(16):
                for j in range(2):
                    buf = self.WS0 if dc < 8 else self.WS1
                    col = (dc % 8) * 1024 + j * 512
                    tw = self.dmaT("q", buf[:, col:col + 512],
                                   self.w1_out[os * 2048 + j * 512: os * 2048 + (j + 1) * 512,
                                               dc * 128:(dc + 1) * 128],
                                   after={"t": S.cnt["t"], "v": S.cnt["v"]} if (dc == 0 and j == 0) else None)
            sh = [tw, tw]
            for tc in range(NTC):
                b0 = self.P[0] if tc % 2 == 0 else self.P[2]
                b1 = self.P[1] if tc % 2 == 0 else self.P[3]

                @self.step("t", after={"q": tw, "s": sh[tc % 2] if tc >= 2 else 0})
                def _(t, tc=tc, b0=b0, b1=b1):
                    last = None
                    for half in range(2):
                        pp = b0 if half == 0 else b1
                        for dc in range(16):
                            buf = self.WS0 if dc < 8 else self.WS1
                            col = (dc % 8) * 1024 + half * 512
                            last = t.matmul(out=pp[:, :],
                                            lhsT=self.AB[:, dc * 1024 + tc * 128: dc * 1024 + tc * 128 + 128],
                                            rhs=buf[:, col:col + 512],
                                            start=(dc == 0), stop=(dc == 15), skip_group_check=True)
                    return last

                @self.step("s", after={"t": S.cnt["t"]})
                def _(s, tc=tc, b0=b0, b1=b1):
                    s.activation(out=self.AB[:, 16384 + tc * 1024: 16384 + tc * 1024 + 512],
                                 in_=b0[:, :], func=AF.Silu)
                    return s.activation(out=self.AB[:, 16384 + tc * 1024 + 512: 16384 + (tc + 1) * 1024],
                                        in_=b1[:, :], func=AF.Silu)
                sh[tc % 2] = S.cnt["s"]
            tw = 0
            for dc in range(16):
                for j in range(2):
                    buf = self.WS0 if dc < 8 else self.WS1
                    col = (dc % 8) * 1024 + j * 512
                    tw = self.dmaT("q", buf[:, col:col + 512],
                                   self.w1_out[os * 2048 + 1024 + j * 512: os * 2048 + 1024 + (j + 1) * 512,
                                               dc * 128:(dc + 1) * 128],
                                   after={"t": S.cnt["t"], "s": S.cnt["s"]} if (dc == 0 and j == 0) else None)
            vh = [tw, tw]
            for tc in range(NTC):
                b0 = self.P[0] if tc % 2 == 0 else self.P[2]
                b1 = self.P[1] if tc % 2 == 0 else self.P[3]

                @self.step("t", after={"q": tw, "v": vh[tc % 2] if tc >= 2 else 0})
                def _(t, tc=tc, b0=b0, b1=b1):
                    last = None
                    for half in range(2):
                        pp = b0 if half == 0 else b1
                        for dc in range(16):
                            buf = self.WS0 if dc < 8 else self.WS1
                            col = (dc % 8) * 1024 + half * 512
                            last = t.matmul(out=pp[:, :],
                                            lhsT=self.AB[:, dc * 1024 + tc * 128: dc * 1024 + tc * 128 + 128],
                                            rhs=buf[:, col:col + 512],
                                            start=(dc == 0), stop=(dc == 15), skip_group_check=True)
                    return last

                @self.step("v", after={"t": S.cnt["t"]})
                def _(v, tc=tc, b0=b0, b1=b1):
                    sg = self.AB[:, 16384 + tc * 1024: 16384 + (tc + 1) * 1024]
                    v.tensor_tensor(out=self.T1[:, 0:512], in0=b0[:, :],
                                    in1=sg[:, 0:512], op=AL.mult)
                    return v.tensor_tensor(out=self.T1[:, 512:1024], in0=b1[:, :],
                                           in1=sg[:, 512:1024], op=AL.mult)
                vh[tc % 2] = S.cnt["v"]
                self._emit_mxact(self.T1[:, 0:1024], 32, self.SBB[:, 0:1024],
                                 slice(0, 1024), after={"v": S.cnt["v"]})
                pairs = [(self.SBB[:, i * 128:(i + 1) * 128],
                          self.AB[:, 16384 + tc * 1024 + i * 128: 16384 + tc * 1024 + (i + 1) * 128])
                         for i in range(8)]
                self.trans_tiles(pairs, after={"v": S.cnt["v"]})
            tw = 0
            for ou in range(8):
                for dsl in range(4):
                    tw = self.dmaT("q", self.WS0[:, ou * 1024 + dsl * 512 - (ou // 2) * 0: ou * 1024 + dsl * 512 + 512]
                                   if False else self.WS0[:, (ou % 4) * 2048 + dsl * 512:(ou % 4) * 2048 + dsl * 512 + 512]
                                   if ou < 4 else self.WS1[:, (ou % 4) * 2048 + dsl * 512:(ou % 4) * 2048 + dsl * 512 + 512],
                                   self.w1b_out[dsl * 512:(dsl + 1) * 512,
                                                os * 1024 + ou * 128: os * 1024 + (ou + 1) * 128],
                                   after={"t": S.cnt["t"]} if (ou == 0 and dsl == 0) else None)
            for tc in range(NTC):
                @self.step("t", after={"q": tw} if tc == 0 else {"v": S.cnt["v"]})
                def _(t, tc=tc):
                    last = None
                    for dsl in range(4):
                        pp = self.P[2 + dsl]
                        for ou in range(8):
                            buf = self.WS0 if ou < 4 else self.WS1
                            col = (ou % 4) * 2048 + dsl * 512
                            last = t.matmul(out=pp[:, :],
                                            lhsT=self.AB[:, 16384 + tc * 1024 + ou * 128: 16384 + tc * 1024 + (ou + 1) * 128],
                                            rhs=buf[:, col:col + 512],
                                            start=(ou == 0), stop=(ou == 7))
                    return last

                @self.step("v", after={"t": S.cnt["t"]})
                def _(v, tc=tc, os=os):
                    last = None
                    for dsl in range(4):
                        dst = self.AF[:, tc * 2048 + dsl * 512: tc * 2048 + (dsl + 1) * 512]
                        if os == 0:
                            last = v.tensor_copy(out=dst, in_=self.P[2 + dsl][:, :])
                        else:
                            last = v.tensor_tensor(out=dst, in0=dst, in1=self.P[2 + dsl][:, :],
                                                   op=AL.add)
                    return last

        for tc in range(NTC):
            tq = self.dma("q", self.T0[:, 0:D], self.xstream[tc * 128:(tc + 1) * 128, :],
                          after={"v": S.cnt["v"]})

            @self.step("v", after={"q": tq})
            def _(v, tc=tc):
                return v.tensor_tensor(out=self.T1[:, 0:D], in0=self.AF[:, tc * 2048:(tc + 1) * 2048],
                                       in1=self.T0[:, 0:D], op=AL.add)

            @self.step("v", after={"v": S.cnt["v"]})
            def _(v):
                return v.tensor_copy(out=self.AF[:, 0:D], in_=self.T1[:, 0:D])
            self.ln_write(self.AF[:, 0:D], self.xstream[tc * 128:(tc + 1) * 128, :],
                          after={"v": S.cnt["v"]})

    def plan_l1_ffn(self, last_block=True):
        S = self.S
        start = S.all()
        tb = self.dma("q", self.BIASR[0:1, 0:2048],
                      self.brow_sp[0:1, 5 * 2048:6 * 2048], after=start)

        @self.step("t", after={"q": tb})
        def _(t):
            last = None
            for j in range(4):
                last = t.matmul(out=self.P[j][:, :], lhsT=self.ONEB[0:1, :],
                                rhs=self.BIASR[0:1, j * 512:(j + 1) * 512],
                                start=True, stop=True)
            return last

        @self.step("v", after={"t": S.cnt["t"]})
        def _(v):
            last = None
            for j in range(4):
                last = v.tensor_copy(out=self.WNBB[:, j * 512:(j + 1) * 512],
                                     in_=self.P[j][:, :])
            return last

        for tc in range(NTC):
            tq = self.dma("q", self.T0[:, 0:D], self.xstream[tc * 128:(tc + 1) * 128, :],
                          after={"v": S.cnt["v"]})

            @self.step("v", after={"q": tq})
            def _(v, tc=tc):
                v.tensor_tensor(out=self.T1[:, 0:D], in0=self.T0[:, 0:D],
                                in1=self.T0[:, 0:D], op=AL.mult)
                v.tensor_reduce(out=self.SM1[:, 0:1], in_=self.T1[:, 0:D], axis=AX, op=AL.add)
                v.drain()
                v.tensor_scalar(out=self.SM1[:, 0:1], in0=self.SM1[:, 0:1],
                                scalar1=1.0 / D, scalar2=1e-5, op0=AL.mult, op1=AL.add)
                return v.drain()

            @self.step("s", after={"v": S.cnt["v"]})
            def _(s):
                return s.activation(out=self.SM1[:, 1:2], in_=self.SM1[:, 0:1], func=AF.Sqrt)

            @self.step("v", after={"s": S.cnt["s"]})
            def _(v, tc=tc):
                v.reciprocal(out=self.SM1[:, 2:3], in_=self.SM1[:, 1:2])
                v.drain()
                v.tensor_scalar(out=self.T1[:, 0:D], in0=self.T0[:, 0:D],
                                scalar1=self.SM1[:, 2:3], scalar2=None, op0=AL.mult)
                v.tensor_tensor(out=self.T1[:, 0:D], in0=self.T1[:, 0:D],
                                in1=self.WNBB[:, :], op=AL.mult)
                v.tensor_reduce(out=self.SM1[:, 3:4], in_=self.T1[:, 0:D], axis=AX,
                                op=AL.max, apply_absolute_value=True)
                v.drain()
                v.tensor_scalar(out=self.SM1[:, 3:4], in0=self.SM1[:, 3:4],
                                scalar1=1e-5, scalar2=None, op0=AL.max)
                v.drain()
                v.reciprocal(out=self.SM1[:, 4:5], in_=self.SM1[:, 3:4])
                v.drain()
                v.tensor_scalar(out=self.SM1[:, 5:6], in0=self.SM1[:, 4:5],
                                scalar1=127.0, scalar2=None, op0=AL.mult)
                v.tensor_scalar(out=self.SM1[:, 6:7], in0=self.SM1[:, 3:4],
                                scalar1=1.0 / 127.0, scalar2=None, op0=AL.mult)
                v.drain()
                v.tensor_tensor(out=self.SM2[:, 16 + tc:17 + tc], in0=self.SM1[:, 6:7],
                                in1=self.SM3[:, 4:5], op=AL.mult)
                v.tensor_copy(out=self.SM2[:, 24 + tc:25 + tc], in_=self.SM1[:, 6:7])
                v.drain()
                v.tensor_scalar(out=self.T0[:, 0:D], in0=self.T1[:, 0:D],
                                scalar1=self.SM1[:, 5:6], scalar2=MAGIC,
                                op0=AL.mult, op1=AL.add)
                v.tensor_scalar(out=self.T0[:, 0:D], in0=self.T0[:, 0:D],
                                scalar1=MAGIC, scalar2=127.0, op0=AL.subtract, op1=AL.min)
                return v.tensor_scalar(out=self.SBB[:, 0:D], in0=self.T0[:, 0:D],
                                       scalar1=-128.0, scalar2=None, op0=AL.max)
            pairs = [(self.SBB[:, dc * 128:(dc + 1) * 128],
                      self.AB[:, dc * 1024 + tc * 128: dc * 1024 + tc * 128 + 128])
                     for dc in range(16)]
            self.trans_tiles(pairs, after={"v": S.cnt["v"]})

        @self.step("v", after={"v": S.cnt["v"]})
        def _(v):
            v.memset(self.SM2[:, 32:40], 0.0)
            return v.drain()
        for os in range(16):
            tw = 0
            for dc in range(16):
                tw = self.dmaT("q", self.WS0[:, dc * 512:(dc + 1) * 512],
                               self.w3_out[os * 512:(os + 1) * 512, dc * 128:(dc + 1) * 128],
                               after={"t": S.cnt["t"], "v": S.cnt["v"]} if dc == 0 else None)
                tw = self.dmaT("q", self.WS1[:, dc * 512:(dc + 1) * 512],
                               self.w3_out[HID1 + os * 512: HID1 + (os + 1) * 512,
                                           dc * 128:(dc + 1) * 128])
            vh = [tw, tw]
            for tc in range(NTC):
                pe = self.P[0] if tc % 2 == 0 else self.P[4]
                po = self.P[1] if tc % 2 == 0 else self.P[5]

                @self.step("t", after={"q": tw, "v": vh[tc % 2] if tc >= 2 else 0})
                def _(t, tc=tc, pe=pe, po=po):
                    last = None
                    for dc in range(16):
                        t.matmul(out=pe[:, :],
                                 lhsT=self.AB[:, dc * 1024 + tc * 128: dc * 1024 + tc * 128 + 128],
                                 rhs=self.WS0[:, dc * 512:(dc + 1) * 512],
                                 start=(dc == 0), stop=(dc == 15), skip_group_check=True)
                        last = t.matmul(out=po[:, :],
                                        lhsT=self.AB[:, dc * 1024 + tc * 128: dc * 1024 + tc * 128 + 128],
                                        rhs=self.WS1[:, dc * 512:(dc + 1) * 512],
                                        start=(dc == 0), stop=(dc == 15), skip_group_check=True)
                    return last

                @self.step("v", after={"t": S.cnt["t"]})
                def _(v, tc=tc, pe=pe, po=po):
                    v.tensor_scalar(out=self.T0[:, 0:512], in0=pe[:, :],
                                    scalar1=self.SM2[:, 16 + tc:17 + tc], scalar2=None,
                                    op0=AL.mult)
                    return v.tensor_scalar(out=self.T1[:, 0:512], in0=po[:, :],
                                           scalar1=self.SM2[:, 16 + tc:17 + tc], scalar2=None,
                                           op0=AL.mult)
                vh[tc % 2] = S.cnt["v"]

                @self.step("s", after={"v": S.cnt["v"]})
                def _(s):
                    return s.activation(out=self.T1[:, 512:1024], in_=self.T1[:, 0:512],
                                        func=AF.Silu)

                @self.step("v", after={"s": S.cnt["s"]})
                def _(v, tc=tc):
                    sb = self.SBB[:, (tc % 2) * 512:(tc % 2) * 512 + 512]
                    v.tensor_tensor(out=sb, in0=self.T0[:, 0:512],
                                    in1=self.T1[:, 512:1024], op=AL.mult)
                    v.drain()
                    v.tensor_reduce(out=self.SM1[:, 7:8], in_=sb, axis=AX,
                                    op=AL.max, apply_absolute_value=True)
                    v.drain()
                    return v.tensor_tensor(out=self.SM2[:, 32 + tc:33 + tc],
                                           in0=self.SM2[:, 32 + tc:33 + tc],
                                           in1=self.SM1[:, 7:8], op=AL.max)
                self.dma("gd", self.u_spill[tc * 128:(tc + 1) * 128, os * 512:(os + 1) * 512],
                         self.SBB[:, (tc % 2) * 512:(tc % 2) * 512 + 512],
                         after={"v": S.cnt["v"]})

        @self.step("v", after={"v": S.cnt["v"]})
        def _(v):
            v.tensor_scalar(out=self.SM2[:, 32:40], in0=self.SM2[:, 32:40],
                            scalar1=1e-5, scalar2=None, op0=AL.max)
            v.drain()
            v.reciprocal(out=self.SM2[:, 40:48], in_=self.SM2[:, 32:40])
            v.drain()
            v.tensor_scalar(out=self.SM2[:, 40:48], in0=self.SM2[:, 40:48],
                            scalar1=127.0, scalar2=None, op0=AL.mult)
            v.tensor_scalar(out=self.SM2[:, 48:56], in0=self.SM2[:, 32:40],
                            scalar1=1.0 / 127.0, scalar2=None, op0=AL.mult)
            v.drain()
            last = None
            for tc in range(NTC):
                last = v.tensor_tensor(out=self.SM2[:, 56 + tc:57 + tc],
                                       in0=self.SM2[:, 48 + tc:49 + tc],
                                       in1=self.SM3[:, 5:6], op=AL.mult)
            return last

        for og in range(8):
            tw = 0
            for ou in range(8):
                for dsl in range(4):
                    buf = self.WS0 if ou < 4 else self.WS1
                    col = (ou % 4) * 2048 + dsl * 512
                    tw = self.dmaT("q", buf[:, col:col + 512],
                                   self.w3b_out[dsl * 512:(dsl + 1) * 512,
                                                og * 1024 + ou * 128: og * 1024 + (ou + 1) * 128],
                                   after={"t": S.cnt["t"]} if (ou == 0 and dsl == 0) else None)
            for tc in range(NTC):
                tq = self.dma("q", self.SBB[:, 0:1024],
                              self.u_spill[tc * 128:(tc + 1) * 128, og * 1024:(og + 1) * 1024],
                              after={"v": S.cnt["v"]})

                @self.step("v", after={"q": tq})
                def _(v, tc=tc):
                    v.tensor_scalar(out=self.T0[:, 0:1024], in0=self.SBB[:, 0:1024],
                                    scalar1=self.SM2[:, 40 + tc:41 + tc], scalar2=MAGIC,
                                    op0=AL.mult, op1=AL.add)
                    v.tensor_scalar(out=self.T0[:, 0:1024], in0=self.T0[:, 0:1024],
                                    scalar1=MAGIC, scalar2=127.0, op0=AL.subtract, op1=AL.min)
                    return v.tensor_scalar(out=self.SBB[:, 1024:2048], in0=self.T0[:, 0:1024],
                                           scalar1=-128.0, scalar2=None, op0=AL.max)
                pairs = [(self.SBB[:, 1024 + i * 128: 1024 + (i + 1) * 128],
                          self.AB[:, 16384 + i * 128: 16384 + (i + 1) * 128])
                         for i in range(8)]
                self.trans_tiles(pairs, after={"v": S.cnt["v"]})

                @self.step("t", after={"v": S.cnt["v"]})
                def _(t, tc=tc):
                    last = None
                    for dsl in range(4):
                        for ou in range(8):
                            buf = self.WS0 if ou < 4 else self.WS1
                            col = (ou % 4) * 2048 + dsl * 512
                            last = t.matmul(out=self.P[2 + dsl][:, :],
                                            lhsT=self.AB[:, 16384 + ou * 128: 16384 + (ou + 1) * 128],
                                            rhs=buf[:, col:col + 512],
                                            start=(ou == 0), stop=(ou == 7))
                    return last

                @self.step("v", after={"t": S.cnt["t"]})
                def _(v, tc=tc, og=og):
                    last = None
                    for dsl in range(4):
                        dst = self.AF[:, tc * 2048 + dsl * 512: tc * 2048 + (dsl + 1) * 512]
                        if og == 0:
                            last = v.tensor_copy(out=dst, in_=self.P[2 + dsl][:, :])
                        else:
                            last = v.tensor_tensor(out=dst, in0=dst,
                                                   in1=self.P[2 + dsl][:, :], op=AL.add)
                    return last

        for tc in range(NTC):
            tq = self.dma("q", self.T0[:, 0:D], self.xstream[tc * 128:(tc + 1) * 128, :],
                          after={"v": S.cnt["v"]})

            @self.step("v", after={"q": tq})
            def _(v, tc=tc):
                v.tensor_scalar(out=self.T1[:, 0:D], in0=self.AF[:, tc * 2048:(tc + 1) * 2048],
                                scalar1=self.SM2[:, 56 + tc:57 + tc], scalar2=None,
                                op0=AL.mult)
                v.drain()
                v.scalar_tensor_tensor(out=self.T1[:, 0:D], in0=self.T0[:, 0:D],
                                       scalar=2.0, in1=self.T1[:, 0:D],
                                       op0=AL.mult, op1=AL.add)
                v.drain()
                return v.tensor_copy(out=self.AF[:, 0:D], in_=self.T1[:, 0:D])
            dst = self.y_out[tc * 128:(tc + 1) * 128, :] if last_block \
                else self.xstream[tc * 128:(tc + 1) * 128, :]
            self.ln_write(self.AF[:, 0:D], dst, after={"v": S.cnt["v"]})

    def plan_output_wait(self):
        S = self.S

        @self.step("g", after=S.all())
        def _(g):
            return g.nop()

    def finalize(self):
        nc, S = self.nc, self.S
        with nc.Block() as block:
            @block.sync
            def _(sync):
                S.emit_for("sync", sync)

            @block.tensor
            def _(t):
                S.emit_for("t", t)

            @block.vector
            def _(v):
                S.emit_for("v", v)

            @block.scalar
            def _(s):
                S.emit_for("s", s)

            @block.gpsimd
            def _(g):
                S.emit_for("g", g)
        self.stack.close()
        return nc


    def dmaT_dyn(self, qk, out, src_tensor, mult, roff, nrows, cols, after=None):
        S = self.S
        ek = {"q": "sync", "gd": "g", "sd": "s"}[qk]
        S.cnt[qk] += 16
        sem = S.sems[qk]

        def fn(eng, out=out):
            off = self._dyn_off(eng, mult)
            nr = src_tensor.shape[0]
            base = src_tensor[roff:nr, cols] if roff else src_tensor[:, cols]
            eng.dma_start_transpose(out=out, in_=base[bass.ds(off, nrows), :]).then_inc(sem, 16)
        S.steps.append((ek, fn, {k: v for k, v in (after or {}).items() if v > 0}))
        return S.cnt[qk]

    def w2row(self, m, kc):
        return (kc // 2) * 1280 + m * 256 + (kc % 2) * 128

    def plan_l1_attn(self):
        S = self.S
        start = S.all()
        for tc in range(NTC):
            tq = self.dma("q", self.T0[:, 0:D], self.xstream[tc * 128:(tc + 1) * 128, :],
                          after=start if tc == 0 else {"t": S.cnt["t"]})

            @self.step("v", after={"q": tq, "v": S.cnt["v"]})
            def _(v):
                return v.tensor_copy(out=self.SBB[:, 0:D], in_=self.T0[:, 0:D])
            pairs = [(self.SBB[:, dc * 128:(dc + 1) * 128],
                      self.AB[:, dc * 1024 + tc * 128: dc * 1024 + tc * 128 + 128])
                     for dc in range(16)]
            self.trans_tiles(pairs, after={"v": S.cnt["v"]})

        for m in range(4):
            for dhalf in range(2):
                tw = 0
                for dsl_in in range(2):
                    dsl = dhalf * 2 + dsl_in
                    buf = self.WS0 if dsl_in == 0 else self.WS1
                    for kc in range(16):
                        tw = self.dma("q", buf[:, kc * 512:(kc + 1) * 512],
                                      self.w2_out[self.w2row(m, kc): self.w2row(m, kc) + 128,
                                                  dsl * 512:(dsl + 1) * 512],
                                      after={"t": S.cnt["t"], "v": S.cnt["v"], "s": S.cnt["s"]}
                                      if (dsl_in == 0 and kc == 0) else None)
                tb = self.dma("q", self.BIASR[0:1, 0:1024],
                              self.brow_sp[0:1, m * 2048 + dhalf * 1024: m * 2048 + (dhalf + 1) * 1024],
                              after={"t": S.cnt["t"]})
                phv = [0, 0]
                phs = [0, 0]
                for tc in range(NTC):
                    bk0 = self.P[0] if tc % 2 == 0 else self.P[4]
                    bk1 = self.P[1] if tc % 2 == 0 else self.P[5]
                    sbo = (tc % 2) * 1024

                    @self.step("t", after={"q": max(tw, tb),
                                           "v": phv[tc % 2] if tc >= 2 else 0,
                                           "s": phs[tc % 2] if tc >= 2 else 0})
                    def _(t, m=m, tc=tc, dhalf=dhalf, bk0=bk0, bk1=bk1):
                        last = None
                        for dsl_in in range(2):
                            buf = self.WS0 if dsl_in == 0 else self.WS1
                            pp = bk0 if dsl_in == 0 else bk1
                            for kc in range(16):
                                t.matmul(out=pp[:, :],
                                         lhsT=self.AB[:, kc * 1024 + tc * 128: kc * 1024 + tc * 128 + 128],
                                         rhs=buf[:, kc * 512:(kc + 1) * 512],
                                         start=(kc == 0), stop=False, skip_group_check=True)
                            last = t.matmul(out=pp[:, :], lhsT=self.ONEB[0:1, :],
                                            rhs=self.BIASR[0:1, dsl_in * 512:(dsl_in + 1) * 512],
                                            start=False, stop=True, skip_group_check=True)
                        return last
                    if m <= 1:
                        tq = self.dma("q", self.ROPE[:, 0:64],
                                      self.c["rsin"][tc * 128:(tc + 1) * 128, :],
                                      after={"v": S.cnt["v"]})
                        tq = self.dma("q", self.ROPE[:, 64:128],
                                      self.c["rcos"][tc * 128:(tc + 1) * 128, :])

                        @self.step("v", after={"t": S.cnt["t"], "q": tq})
                        def _(v, bk0=bk0, bk1=bk1):
                            v.tensor_copy(out=self.T0[:, 0:512], in_=bk0[:, :])
                            v.tensor_copy(out=self.T0[:, 512:1024], in_=bk1[:, :])
                            v.drain()
                            r3 = lambda ap: ap.rearrange("p (h x) -> p h x", x=128)
                            x1 = r3(self.T0[:, 0:1024])[:, :, 0:64]
                            x2 = r3(self.T0[:, 0:1024])[:, :, 64:128]
                            o1 = r3(self.T1[:, 0:1024])[:, :, 0:64]
                            o2 = r3(self.T1[:, 0:1024])[:, :, 64:128]
                            a1 = r3(self.AF[:, 0:1024])[:, :, 0:64]
                            a2 = r3(self.AF[:, 0:1024])[:, :, 64:128]
                            sin = self.ROPE[:, 0:64].rearrange("p (o x) -> p o x", o=1).broadcast_to((128, 8, 64))
                            cos = self.ROPE[:, 64:128].rearrange("p (o x) -> p o x", o=1).broadcast_to((128, 8, 64))
                            v.tensor_tensor(out=o1, in0=x1, in1=cos, op=AL.mult)
                            v.tensor_tensor(out=o2, in0=x2, in1=cos, op=AL.mult)
                            v.tensor_tensor(out=a1, in0=x2, in1=sin, op=AL.mult)
                            v.tensor_tensor(out=a2, in0=x1, in1=sin, op=AL.mult)
                            v.drain()
                            v.tensor_tensor(out=o1, in0=o1, in1=a1, op=AL.subtract)
                            v.tensor_tensor(out=o2, in0=o2, in1=a2, op=AL.add)
                            v.drain()
                            v.tensor_scalar(out=self.T0[:, 0:1024], in0=self.T1[:, 0:1024],
                                            scalar1=0.0, scalar2=None, op0=AL.min)
                            return v.tensor_scalar(out=self.AF[:, 0:1024], in0=self.T1[:, 0:1024],
                                                   scalar1=0.0, scalar2=None, op0=AL.max)

                        @self.step("s", after={"v": S.cnt["v"]})
                        def _(s):
                            return s.activation(out=self.T0[:, 0:1024], in_=self.T0[:, 0:1024],
                                                func=AF.Exp)

                        @self.step("v", after={"s": S.cnt["s"]})
                        def _(v, sbo=sbo):
                            v.tensor_tensor(out=self.T0[:, 0:1024], in0=self.T0[:, 0:1024],
                                            in1=self.AF[:, 0:1024], op=AL.add)
                            v.drain()
                            return v.tensor_copy(out=self.SBB[:, sbo:sbo + 1024],
                                                 in_=self.T0[:, 0:1024])
                    elif m == 2:
                        @self.step("v", after={"t": S.cnt["t"]})
                        def _(v, bk0=bk0, bk1=bk1, sbo=sbo):
                            v.tensor_copy(out=self.SBB[:, sbo:sbo + 512], in_=bk0[:, :])
                            return v.tensor_copy(out=self.SBB[:, sbo + 512:sbo + 1024], in_=bk1[:, :])
                    else:
                        @self.step("s", after={"t": S.cnt["t"], "v": S.cnt["v"]})
                        def _(s, bk0=bk0, bk1=bk1, sbo=sbo):
                            s.activation(out=self.SBB[:, sbo:sbo + 512], in_=bk0[:, :],
                                         func=AF.Sigmoid)
                            return s.activation(out=self.SBB[:, sbo + 512:sbo + 1024], in_=bk1[:, :],
                                                func=AF.Sigmoid)
                    phv[tc % 2] = S.cnt["v"]
                    phs[tc % 2] = S.cnt["s"]
                    self.dma("gd", self.qkvg_sp[m * TL + tc * 128: m * TL + (tc + 1) * 128,
                                                dhalf * 1024:(dhalf + 1) * 1024],
                             self.SBB[:, sbo:sbo + 1024], after={"v": S.cnt["v"], "s": S.cnt["s"]})

        tgd = S.cnt["gd"]
        for c in range(NCHKL):
            tq = self.dma("q", self.SBB[0:64, 0:D],
                          self.qkvg_sp[TL + c * 64: TL + (c + 1) * 64, :],
                          after={"gd": tgd} if c == 0 else {"t": S.cnt["t"]})
            tq = self.dma("q", self.WS0[0:64, 0:D],
                          self.qkvg_sp[2 * TL + c * 64: 2 * TL + (c + 1) * 64, :])

            @self.step("t", after={"q": tq})
            def _(t, c=c):
                last = None
                for h in range(H):
                    t.matmul(out=self.P[h // 4][:, (h % 4) * 128:(h % 4) * 128 + 128],
                             lhsT=self.SBB[0:64, h * 128:(h + 1) * 128],
                             rhs=self.WS0[0:64, h * 128:(h + 1) * 128],
                             start=(c == 0 and h % 4 == 0), stop=(c == NCHKL - 1),
                             skip_group_check=True)
                    last = t.matmul(out=self.P[4][:, h:h + 1],
                                    lhsT=self.SBB[0:64, h * 128:(h + 1) * 128],
                                    rhs=self.ONEB[0:64, 0:1],
                                    start=(c == 0 and h == 0), stop=(c == NCHKL - 1),
                                    skip_group_check=True)
                return last

        @self.step("v", after={"t": S.cnt["t"]})
        def _(v):
            last = None
            for h in range(H):
                last = v.tensor_copy(out=self.AF[:, h * 136: h * 136 + 128],
                                     in_=self.P[h // 4][:, (h % 4) * 128:(h % 4) * 128 + 128])
                last = v.tensor_copy(out=self.AF[:, h * 136 + 128: h * 136 + 129],
                                     in_=self.P[4][:, h:h + 1])
            return last
        tv = S.cnt["v"]
        self.dma("gd", self.s1_in[:, :].rearrange("(h p) c -> p h c", p=128),
                 self.AF[:, 0:H * 136].rearrange("p (h c) -> p h c", c=136), after={"v": tv})
        t_s1 = self.cc("AllGather", AL.bypass, [self.s1_in], [self.s1_out],
                       after={"gd": S.cnt["gd"]})

        @self.step("v", after={"cc": t_s1})
        def _(v):
            v.memset(self.AF[:, 0:H * 136], 0.0)
            return v.drain()
        for b in range(NCORE):
            tq = self.dma("q", self.T0[:, 0:1088].rearrange("p (h c) -> p h c", c=136),
                          self.s1_out[b * 2048: b * 2048 + 1024, 0:136]
                          .rearrange("(h p) c -> p h c", p=128),
                          after={"v": S.cnt["v"]})
            tq = self.dma("q", self.T1[:, 0:1088].rearrange("p (h c) -> p h c", c=136),
                          self.s1_out[b * 2048 + 1024: b * 2048 + 2048, 0:136]
                          .rearrange("(h p) c -> p h c", p=128))

            @self.step("v", after={"q": tq})
            def _(v, b=b):
                v.scalar_tensor_tensor(out=self.AF[:, 0:8 * 136], in0=self.T0[:, 0:1088],
                                       scalar=self.SEL[:, b:b + 1], in1=self.AF[:, 0:8 * 136],
                                       op0=AL.mult, op1=AL.add)
                v.scalar_tensor_tensor(out=self.AF[:, 8 * 136:16 * 136], in0=self.T1[:, 0:1088],
                                       scalar=self.SEL[:, b:b + 1], in1=self.AF[:, 8 * 136:16 * 136],
                                       op0=AL.mult, op1=AL.add)
                return v.drain()

        for c in range(NCHKL):
            tq = self.dma("q", self.WS0[0:64, 0:D],
                          self.qkvg_sp[0 * TL + c * 64: 0 * TL + (c + 1) * 64, :],
                          after={"v": S.cnt["v"], "t": S.cnt["t"]})
            tq = self.dma("q", self.WS0[0:64, 2048:2048 + D],
                          self.qkvg_sp[1 * TL + c * 64: 1 * TL + (c + 1) * 64, :])
            tq = self.dma("q", self.WS0[0:64, 4096:4096 + D],
                          self.qkvg_sp[2 * TL + c * 64: 2 * TL + (c + 1) * 64, :])
            tq = self.dma("q", self.WS0[0:64, 6144:6144 + D],
                          self.qkvg_sp[3 * TL + c * 64: 3 * TL + (c + 1) * 64, :])

            @self.step("t", after={"q": tq})
            def _(t):
                last = None
                for h in range(H):
                    t.matmul(out=self.P[h // 4][:, (h % 4) * 128:(h % 4) * 128 + 128],
                             lhsT=self.WS0[0:64, 2048 + h * 128: 2048 + (h + 1) * 128],
                             rhs=self.WS0[0:64, 4096 + h * 128: 4096 + (h + 1) * 128],
                             start=(h % 4 == 0), stop=(h % 4 == 3), skip_group_check=True)
                    last = t.matmul(out=self.P[4][:, h:h + 1],
                                    lhsT=self.WS0[0:64, 2048 + h * 128: 2048 + (h + 1) * 128],
                                    rhs=self.ONEB[0:64, 0:1],
                                    start=(h == 0), stop=(h == H - 1), skip_group_check=True)
                return last

            @self.step("v", after={"t": S.cnt["t"]})
            def _(v):
                acc3 = self.AF[:, 0:16 * 136].rearrange("p (h c) -> p h c", c=136)
                for bk in range(4):
                    acc = acc3[:, bk * 4:(bk + 1) * 4, 0:128]
                    pin = self.P[bk][:, :].rearrange("p (h c) -> p h c", c=128)
                    v.tensor_tensor(out=acc, in0=acc, in1=pin, op=AL.add)
                accz = acc3[:, :, 128:129]
                pz = self.P[4][:, 0:16].rearrange("p (h c) -> p h c", c=1)
                v.tensor_tensor(out=accz, in0=accz, in1=pz, op=AL.add)
                v.drain()
                return v.tensor_copy(out=self.AB[:, 0:16 * 136], in_=self.AF[:, 0:16 * 136])

            @self.step("t", after={"v": S.cnt["v"]})
            def _(t):
                last = None
                for h in range(H):
                    pp = self.PT if h < 8 else self.PT2
                    last = t.transpose(out=pp[:, (h % 8) * 64:(h % 8) * 64 + 64],
                                       in_=self.WS0[0:64, h * 128:(h + 1) * 128],
                                       identity=self.EYEB[0:64, 0:64])
                return last

            @self.step("v", after={"t": S.cnt["t"]})
            def _(v):
                v.tensor_copy(out=self.SBB[:, 0:512], in_=self.PT[:, :])
                return v.tensor_copy(out=self.SBB[:, 512:1024], in_=self.PT2[:, :])

            @self.step("t", after={"v": S.cnt["v"]})
            def _(t):
                last = None
                for h in range(H):
                    t.matmul(out=self.P[h // 4][0:64, (h % 4) * 128:(h % 4) * 128 + 128],
                             lhsT=self.SBB[:, h * 64:(h + 1) * 64],
                             rhs=self.AB[:, h * 136: h * 136 + 128],
                             start=(h % 4 == 0), stop=(h % 4 == 3), skip_group_check=True)
                    last = t.matmul(out=self.P[5][0:64, h:h + 1],
                                    lhsT=self.SBB[:, h * 64:(h + 1) * 64],
                                    rhs=self.AB[:, h * 136 + 128: h * 136 + 129],
                                    start=(h == 0), stop=(h == H - 1), skip_group_check=True)
                return last

            @self.step("v", after={"t": S.cnt["t"]})
            def _(v, c=c):
                v.tensor_scalar(out=self.SM0[0:64, 0:16], in0=self.P[5][0:64, 0:16],
                                scalar1=1e-6, scalar2=None, op0=AL.add)
                v.drain()
                v.reciprocal(out=self.SM0[0:64, 16:32], in_=self.SM0[0:64, 0:16])
                v.drain()
                last = None
                for h in range(H):
                    last = v.scalar_tensor_tensor(
                        out=self.SBB[0:64, h * 128:(h + 1) * 128],
                        in0=self.P[h // 4][0:64, (h % 4) * 128:(h % 4) * 128 + 128],
                        scalar=self.SM0[0:64, 16 + h:17 + h],
                        in1=self.WS0[0:64, 6144 + h * 128: 6144 + (h + 1) * 128],
                        op0=AL.mult, op1=AL.mult)
                return last
            self.dma("gd", self.qkvg_sp[0 * TL + c * 64: 0 * TL + (c + 1) * 64, :],
                     self.SBB[0:64, 0:D], after={"v": S.cnt["v"]})

        tgd = S.cnt["gd"]
        for tc in range(NTC):
            tq = self.dma("q", self.SBB[:, 0:D],
                          self.qkvg_sp[tc * 128:(tc + 1) * 128, :],
                          after={"gd": tgd, "v": S.cnt["v"], "t": S.cnt["t"]})

            @self.step("v", after={"q": tq})
            def _(v):
                return v.engine_nop()
            pairs = [(self.SBB[:, dc * 128:(dc + 1) * 128],
                      self.AB[:, dc * 1024 + tc * 128: dc * 1024 + tc * 128 + 128])
                     for dc in range(16)]
            self.trans_tiles(pairs, after={"v": S.cnt["v"]})
        for dsl in range(4):
            tw = 0
            for kc in range(16):
                tw = self.dma("q", self.WS1[:, kc * 512:(kc + 1) * 512],
                              self.w2_out[self.w2row(4, kc): self.w2row(4, kc) + 128,
                                          dsl * 512:(dsl + 1) * 512],
                              after={"t": S.cnt["t"]} if kc == 0 else None)
            tb = self.dma("q", self.BIASR[0:1, 0:512],
                          self.brow_sp[0:1, 4 * 2048 + dsl * 512: 4 * 2048 + (dsl + 1) * 512],
                          after={"t": S.cnt["t"]})
            for tc in range(NTC):
                @self.step("t", after={"q": max(tw, tb)} if tc == 0 else {"v": S.cnt["v"]})
                def _(t, tc=tc, dsl=dsl):
                    for kc in range(16):
                        t.matmul(out=self.P[tc % 2][:, :],
                                 lhsT=self.AB[:, kc * 1024 + tc * 128: kc * 1024 + tc * 128 + 128],
                                 rhs=self.WS1[:, kc * 512:(kc + 1) * 512],
                                 start=(kc == 0), stop=False, skip_group_check=True)
                    return t.matmul(out=self.P[tc % 2][:, :], lhsT=self.ONEB[0:1, :],
                                    rhs=self.BIASR[0:1, 0:512],
                                    start=False, stop=True, skip_group_check=True)

                @self.step("v", after={"t": S.cnt["t"]})
                def _(v, tc=tc, dsl=dsl):
                    return v.tensor_copy(out=self.AF[:, tc * 2048 + dsl * 512: tc * 2048 + (dsl + 1) * 512],
                                         in_=self.P[tc % 2][:, :])
        for tc in range(NTC):
            tq = self.dma("q", self.T0[:, 0:D], self.xstream[tc * 128:(tc + 1) * 128, :],
                          after={"v": S.cnt["v"]})

            @self.step("v", after={"q": tq})
            def _(v, tc=tc):
                v.tensor_tensor(out=self.T1[:, 0:D], in0=self.AF[:, tc * 2048:(tc + 1) * 2048],
                                in1=self.T0[:, 0:D], op=AL.add)
                v.drain()
                return v.tensor_copy(out=self.AF[:, 0:D], in_=self.T1[:, 0:D])
            self.ln_write(self.AF[:, 0:D], self.xstream[tc * 128:(tc + 1) * 128, :],
                          after={"v": S.cnt["v"]})

    def w0row(self, m, kc):
        return (kc // 2) * 1024 + m * 256 + (kc % 2) * 128

    def plan_l0_attn(self):
        S = self.S
        start = S.all()
        self.olin_sp = self.mk("olin_sp", (HL * 128, T), BF16)
        for tc in range(NTC):
            tq = self.dma("q", self.T0[:, 0:D], self.xstream[tc * 128:(tc + 1) * 128, :],
                          after=start if tc == 0 else {"t": S.cnt["t"]})

            @self.step("v", after={"q": tq, "v": S.cnt["v"]})
            def _(v):
                return v.tensor_copy(out=self.SBB[:, 0:D], in_=self.T0[:, 0:D])
            pairs = [(self.SBB[:, dc * 128:(dc + 1) * 128],
                      self.AB[:, dc * 1024 + tc * 128: dc * 1024 + tc * 128 + 128])
                     for dc in range(16)]
            self.trans_tiles(pairs, after={"v": S.cnt["v"]})
        self._l0p_vh = [0, 0]
        for m in range(3):
            wn = ["l0_wq", "l0_wk", "l0_wv"][m]
            for dc in range(16):
                tw = 0
                for kc in range(16):
                    tw = self.dma("q", self.WS0[:, kc * 128:(kc + 1) * 128],
                                  self.w0_out[self.w0row(m, kc): self.w0row(m, kc) + 128,
                                              dc * 128:(dc + 1) * 128],
                                  after={"t": S.cnt["t"], "cc": self.t_w0} if kc == 0 else None)

                rot = dc % 2
                b0 = self.P[0] if rot == 0 else self.P[2]
                b1 = self.P[1] if rot == 0 else self.P[3]
                sbo = rot * 1024

                @self.step("t", after={"q": tw, "v": self._l0p_vh[rot] if dc >= 2 else 0})
                def _(t, b0=b0, b1=b1):
                    last = None
                    for half in range(2):
                        pp = b0 if half == 0 else b1
                        for kc in range(16):
                            last = t.matmul(out=pp[:, :],
                                            lhsT=self.WS0[:, kc * 128:(kc + 1) * 128],
                                            rhs=self.AB[:, kc * 1024 + half * 512: kc * 1024 + (half + 1) * 512],
                                            start=(kc == 0), stop=(kc == 15),
                                            skip_group_check=True)
                    return last

                @self.step("v", after={"t": S.cnt["t"]})
                def _(v, b0=b0, b1=b1, sbo=sbo):
                    v.tensor_copy(out=self.SBB[:, sbo:sbo + 512], in_=b0[:, :])
                    return v.tensor_copy(out=self.SBB[:, sbo + 512:sbo + 1024], in_=b1[:, :])
                self._l0p_vh[rot] = S.cnt["v"]
                self.dma("gd", self.a0_in[m * 2048 + dc * 128: m * 2048 + (dc + 1) * 128, :],
                         self.SBB[:, sbo:sbo + 1024], after={"v": S.cnt["v"]})
        t_a0 = self.cc("AllGather", AL.bypass, [self.a0_in], [self.a0_out],
                       after={"gd": S.cnt["gd"]})

        for hh in range(HL):
            self._l0_head(hh, t_a0)

        t_a1 = self.cc("AllGather", AL.bypass, [self.a1_in], [self.a1_out],
                       after={"gd": S.cnt["gd"]})

        for dc in range(16):
            self.dma_dyn("q", self.AB[:, dc * 1024:(dc + 1) * 1024], None,
                         None, 0, None, after={"cc": t_a1, "t": S.cnt["t"], "v": S.cnt["v"]}) \
                if False else None
        Sq = self.S
        Sq.cnt["q"] += 16
        semq = Sq.sems["q"]

        def a1fn(eng, sem=semq):
            off = self._dyn_off(eng, 1)
            view = self.a1_out.rearrange("d (blk t) -> blk d t", t=TL)
            eng.dma_start(out=self.a1st[:, :, :],
                          in_=view[bass.ds(off, 1), :, :]).then_inc(sem, 16)
        Sq.steps.append(("sync", a1fn,
                         {k: v for k, v in dict(cc=t_a1, t=S.cnt["t"], v=S.cnt["v"]).items() if v > 0}))
        t_a1st = Sq.cnt["q"]
        for dc in range(16):
            self.dma("q", self.AB[:, dc * 1024:(dc + 1) * 1024],
                     self.a1st[0, dc * 128:(dc + 1) * 128, :],
                     after={"q": t_a1st} if dc == 0 else None)
        t_ot = S.cnt["q"]
        for dsl in range(4):
            tw = 0
            for kc in range(16):
                tw = self.dma("q", self.WS1[:, kc * 512:(kc + 1) * 512],
                              self.w0_out[self.w0row(3, kc): self.w0row(3, kc) + 128,
                                          dsl * 512:(dsl + 1) * 512],
                              after={"t": S.cnt["t"], "q": t_ot} if kc == 0 else None)
            for tc in range(NTC):
                @self.step("t", after={"q": tw} if tc == 0 else {"v": S.cnt["v"]})
                def _(t, tc=tc):
                    last = None
                    for kc in range(16):
                        last = t.matmul(out=self.P[tc % 2][:, :],
                                        lhsT=self.AB[:, kc * 1024 + tc * 128: kc * 1024 + tc * 128 + 128],
                                        rhs=self.WS1[:, kc * 512:(kc + 1) * 512],
                                        start=(kc == 0), stop=(kc == 15),
                                        skip_group_check=True)
                    return last

                @self.step("v", after={"t": S.cnt["t"]})
                def _(v, tc=tc, dsl=dsl):
                    return v.tensor_copy(out=self.AF[:, tc * 2048 + dsl * 512: tc * 2048 + (dsl + 1) * 512],
                                         in_=self.P[tc % 2][:, :])
        for tc in range(NTC):
            tq = self.dma("q", self.T0[:, 0:D], self.xstream[tc * 128:(tc + 1) * 128, :],
                          after={"v": S.cnt["v"]})

            @self.step("v", after={"q": tq})
            def _(v, tc=tc):
                v.tensor_tensor(out=self.T1[:, 0:D], in0=self.AF[:, tc * 2048:(tc + 1) * 2048],
                                in1=self.T0[:, 0:D], op=AL.add)
                v.drain()
                return v.tensor_copy(out=self.AF[:, 0:D], in_=self.T1[:, 0:D])
            self.ln_write(self.AF[:, 0:D], self.xstream[tc * 128:(tc + 1) * 128, :],
                          after={"v": S.cnt["v"]})

    def _l0_head(self, hh, t_a0):
        S = self.S
        Sq = self.S
        Sq.cnt["q"] += 16
        semq = Sq.sems["q"]

        def stfn(eng, hh=hh, sem=semq):
            off = self._dyn_off(eng, 256)
            view = self.a0_out.rearrange("(b m r) t -> r b m t", m=3, r=2048)
            base = view[hh * 128: 2048, :, :, :]
            dst = self.hstage[:, :].rearrange("r (b m t) -> r b m t", m=3, t=TL)
            eng.dma_start(out=dst, in_=base[bass.ds(off, 128), :, :, :]).then_inc(sem, 16)
        Sq.steps.append(("sync", stfn,
                         {k: v for k, v in dict(cc=t_a0, t=S.cnt["t"], v=S.cnt["v"],
                                                s=S.cnt["s"], gd=S.cnt["gd"]).items() if v > 0}))
        t_hst = Sq.cnt["q"]
        hs = self.hstage
        tq = 0
        for b in range(NCORE):
            tq = self.dma("q", self.AB[:, 0 + b * 1024: 0 + (b + 1) * 1024],
                          hs[:, b * 3072 + 1024: b * 3072 + 2048],
                          after={"q": t_hst} if b == 0 else None)
            tq = self.dma("q", self.AB[:, 8192 + b * 1024: 8192 + (b + 1) * 1024],
                          hs[:, b * 3072 + 0: b * 3072 + 1024])
            for sub in range(8):
                jt = b * 8 + sub
                tq = self.dmaT("q", self.AB[:, 16384 + jt * 128: 16384 + (jt + 1) * 128],
                               hs[:, b * 3072 + 2048 + sub * 128: b * 3072 + 2048 + (sub + 1) * 128])
        t_hload = tq

        @self.step("v", after={"q": t_hload})
        def _(v):
            v.memset(self.AF[:, 0:128], 0.0)
            v.memset(self.SBB[:, 1024:1152], 0.0)
            return v.drain()

        kh = self.AB[:, 0:8192]
        qh = self.AB[:, 8192:16384]
        W1 = self.WS1
        KSF, QSF, KBF, BT, KST, VB, KBT = 0, 512, 1024, 1536, 2560, 3584, 4608
        MOFF, MTOFF, XOFF, ZOFF, WOFF = 5632, 6144, 6656, 7168, 7680
        UOFF, WTN, ATN = BT, KBF, MTOFF
        for b in range(16):
            t0c = b * 512
            @self.step("s", after={"v": S.cnt["v"], "t": S.cnt["t"]})
            def _(s, t0c=t0c):
                return s.activation(out=self.T0[:, 0:512], in_=kh[:, t0c:t0c + 512], func=AF.Exp)

            @self.step("t", after={"s": S.cnt["s"]})
            def _(t):
                return t.matmul(out=self.P[2][0:1, :], lhsT=self.ONEF[:, 0:1],
                                rhs=self.T0[:, 0:512], start=True, stop=True)

            @self.step("v", after={"t": S.cnt["t"]})
            def _(v):
                v.reciprocal(out=self.ROWV[0:1, 0:512], in_=self.P[2][0:1, :])
                return v.drain()

            @self.step("t", after={"v": S.cnt["v"]})
            def _(t):
                return t.matmul(out=self.P[3][:, :], lhsT=self.ONEF[0:1, :],
                                rhs=self.ROWV[0:1, 0:512], start=True, stop=True)

            @self.step("v", after={"t": S.cnt["t"]})
            def _(v):
                return v.tensor_tensor(out=W1[:, KSF:KSF + 512], in0=self.T0[:, 0:512],
                                       in1=self.P[3][:, :], op=AL.mult)
            @self.step("s", after={"v": S.cnt["v"]})
            def _(s, t0c=t0c):
                return s.activation(out=self.T0[:, 0:512], in_=qh[:, t0c:t0c + 512], func=AF.Exp)

            @self.step("t", after={"s": S.cnt["s"]})
            def _(t):
                return t.matmul(out=self.P[2][0:1, :], lhsT=self.ONEF[:, 0:1],
                                rhs=self.T0[:, 0:512], start=True, stop=True)

            @self.step("v", after={"t": S.cnt["t"]})
            def _(v):
                v.reciprocal(out=self.ROWV[0:1, 0:512], in_=self.P[2][0:1, :])
                v.drain()
                v.tensor_scalar(out=self.ROWV[0:1, 0:512], in0=self.ROWV[0:1, 0:512],
                                scalar1=1.0 / SQD, scalar2=None, op0=AL.mult)
                return v.drain()

            @self.step("t", after={"v": S.cnt["v"]})
            def _(t):
                return t.matmul(out=self.P[3][:, :], lhsT=self.ONEF[0:1, :],
                                rhs=self.ROWV[0:1, 0:512], start=True, stop=True)

            @self.step("v", after={"t": S.cnt["t"]})
            def _(v):
                return v.tensor_tensor(out=W1[:, QSF:QSF + 512], in0=self.T0[:, 0:512],
                                       in1=self.P[3][:, :], op=AL.mult)
            @self.step("s", after={"v": S.cnt["v"]})
            def _(s, t0c=t0c):
                return s.activation(out=self.T1[:, 0:512], in_=kh[:, t0c:t0c + 512],
                                    func=AF.Exp, scale=-1.0)

            @self.step("v", after={"s": S.cnt["s"]})
            def _(v):
                return v.tensor_scalar(out=self.T1[:, 0:512], in0=self.T1[:, 0:512],
                                       scalar1=1.0, scalar2=None, op0=AL.add)

            @self.step("s", after={"v": S.cnt["v"]})
            def _(s):
                return s.activation(out=self.T1[:, 0:512], in_=self.T1[:, 0:512],
                                    func=AF.Ln)

            @self.step("v", after={"s": S.cnt["s"]})
            def _(v):
                v.scalar_tensor_tensor(out=W1[:, KBF:KBF + 512], in0=self.T1[:, 0:512],
                                       scalar=-1.0 / GATE_NORM, in1=W1[:, KSF:KSF + 512],
                                       op0=AL.mult, op1=AL.mult)
                return v.tensor_scalar(out=self.SBB[:, 0:512], in0=self.T1[:, 0:512],
                                       scalar1=-1.0 / GATE_NORM, scalar2=None, op0=AL.mult)
            @self.step("t", after={"v": S.cnt["v"]})
            def _(t):
                last = None
                for i in range(8):
                    pp = self.PT if i < 4 else self.PT2
                    last = t.transpose(out=pp[0:64, (i % 4) * 128:(i % 4) * 128 + 128],
                                       in_=self.SBB[:, i * 64:(i + 1) * 64],
                                       identity=self.EYEB[:, :])
                return last

            @self.step("v", after={"t": S.cnt["t"]})
            def _(v):
                v.tensor_copy(out=W1[0:64, BT:BT + 512], in_=self.PT[0:64, :])
                return v.tensor_copy(out=W1[0:64, BT + 512:BT + 1024], in_=self.PT2[0:64, :])

            @self.step("t", after={"v": S.cnt["v"]})
            def _(t):
                last = None
                for i in range(8):
                    pp = self.PT if i < 4 else self.PT2
                    last = t.transpose(out=pp[0:64, (i % 4) * 128:(i % 4) * 128 + 128],
                                       in_=W1[:, KSF + i * 64: KSF + (i + 1) * 64],
                                       identity=self.EYEB[:, :])
                return last

            @self.step("v", after={"t": S.cnt["t"]})
            def _(v):
                v.tensor_copy(out=W1[0:64, KST:KST + 512], in_=self.PT[0:64, :])
                return v.tensor_copy(out=W1[0:64, KST + 512:KST + 1024], in_=self.PT2[0:64, :])
            blk0 = t0c // 1024
            coff = t0c % 1024
            tqv = self.dma("q", self.SBB[:, 1536:2048],
                           self.hstage[:, blk0 * 3072 + 2048 + coff:
                                       blk0 * 3072 + 2048 + coff + 512],
                           after={"v": S.cnt["v"]})

            @self.step("t", after={"q": tqv, "v": S.cnt["v"]})
            def _(t):
                last = None
                for i in range(8):
                    pp = self.PT if i < 4 else self.PT2
                    last = t.transpose(out=pp[0:64, (i % 4) * 128:(i % 4) * 128 + 128],
                                       in_=self.SBB[:, 1536 + i * 64: 1536 + (i + 1) * 64],
                                       identity=self.EYEB[:, :])
                return last

            @self.step("v", after={"t": S.cnt["t"]})
            def _(v):
                v.tensor_copy(out=W1[0:64, VB:VB + 512], in_=self.PT[0:64, :])
                v.tensor_copy(out=W1[0:64, VB + 512:VB + 1024], in_=self.PT2[0:64, :])
                v.drain()
                v.tensor_tensor(out=W1[0:64, VB:VB + 1024], in0=W1[0:64, VB:VB + 1024],
                                in1=W1[0:64, BT:BT + 1024], op=AL.mult)
                return v.tensor_tensor(out=W1[0:64, KBT:KBT + 1024], in0=W1[0:64, KST:KST + 1024],
                                       in1=W1[0:64, BT:BT + 1024], op=AL.mult)
            @self.step("t", after={"v": S.cnt["v"]})
            def _(t):
                last = None
                for i in range(8):
                    t.matmul(out=self.P[0][0:64, i * 64:(i + 1) * 64],
                             lhsT=W1[:, KBF + i * 64: KBF + (i + 1) * 64],
                             rhs=W1[:, KSF + i * 64: KSF + (i + 1) * 64],
                             start=(i == 0), stop=(i == 7), skip_group_check=True)
                    last = t.matmul(out=self.P[1][0:64, i * 64:(i + 1) * 64],
                                    lhsT=W1[:, KSF + i * 64: KSF + (i + 1) * 64],
                                    rhs=W1[:, KBF + i * 64: KBF + (i + 1) * 64],
                                    start=(i == 0), stop=(i == 7), skip_group_check=True)
                return last

            @self.step("v", after={"t": S.cnt["t"]})
            def _(v):
                v.tensor_tensor(out=self.T0[0:64, 0:512], in0=self.P[0][0:64, :],
                                in1=self.MSK[:, 0:512], op=AL.mult)
                v.tensor_tensor(out=self.T1[0:64, 0:512], in0=self.P[1][0:64, :],
                                in1=self.MSK[:, 512:1024], op=AL.mult)
                v.drain()
                v.tensor_tensor(out=W1[0:64, MOFF:MOFF + 512], in0=self.T0[0:64, 0:512],
                                in1=self.MSK[:, 3 * 512:4 * 512], op=AL.add)
                v.tensor_tensor(out=W1[0:64, MTOFF:MTOFF + 512], in0=self.T1[0:64, 0:512],
                                in1=self.MSK[:, 3 * 512:4 * 512], op=AL.add)
                v.drain()
                v.scalar_tensor_tensor(out=W1[0:64, XOFF:XOFF + 512], in0=W1[0:64, MTOFF:MTOFF + 512],
                                       scalar=-1.0, in1=self.MSK[:, 4 * 512:5 * 512],
                                       op0=AL.mult, op1=AL.add)
                v.scalar_tensor_tensor(out=W1[0:64, ZOFF:ZOFF + 512], in0=W1[0:64, MOFF:MOFF + 512],
                                       scalar=-1.0, in1=self.MSK[:, 4 * 512:5 * 512],
                                       op0=AL.mult, op1=AL.add)
                return v.drain()
            for it in range(5):
                @self.step("t", after={"v": S.cnt["v"]})
                def _(t):
                    last = None
                    for i in range(8):
                        sl = slice(i * 64, (i + 1) * 64)
                        t.matmul(out=self.P[0][0:64, sl],
                                 lhsT=W1[0:64, MOFF + i * 64: MOFF + (i + 1) * 64],
                                 rhs=W1[0:64, XOFF + i * 64: XOFF + (i + 1) * 64],
                                 start=(i == 0), stop=(i == 7), skip_group_check=True)
                        last = t.matmul(out=self.P[1][0:64, sl],
                                        lhsT=W1[0:64, XOFF + i * 64: XOFF + (i + 1) * 64],
                                        rhs=W1[0:64, MOFF + i * 64: MOFF + (i + 1) * 64],
                                        start=(i == 0), stop=(i == 7), skip_group_check=True)
                    return last

                @self.step("v", after={"t": S.cnt["t"]})
                def _(v):
                    v.scalar_tensor_tensor(out=W1[0:64, WOFF:WOFF + 512], in0=self.P[0][0:64, :],
                                           scalar=-1.0, in1=self.MSK[:, 4 * 512:5 * 512],
                                           op0=AL.mult, op1=AL.add)
                    v.scalar_tensor_tensor(out=self.SBB[0:64, 0:512], in0=self.P[1][0:64, :],
                                           scalar=-1.0, in1=self.MSK[:, 4 * 512:5 * 512],
                                           op0=AL.mult, op1=AL.add)
                    return v.drain()

                @self.step("t", after={"v": S.cnt["v"]})
                def _(t):
                    last = None
                    for i in range(8):
                        sl = slice(i * 64, (i + 1) * 64)
                        t.matmul(out=self.P[0][0:64, sl],
                                 lhsT=W1[0:64, ZOFF + i * 64: ZOFF + (i + 1) * 64],
                                 rhs=W1[0:64, WOFF + i * 64: WOFF + (i + 1) * 64],
                                 start=(i == 0), stop=(i == 7), skip_group_check=True)
                        last = t.matmul(out=self.P[1][0:64, sl],
                                        lhsT=W1[0:64, WOFF + i * 64: WOFF + (i + 1) * 64],
                                        rhs=self.SBB[0:64, sl] if False else W1[0:64, ZOFF + i * 64: ZOFF + (i + 1) * 64],
                                        start=(i == 0), stop=(i == 7), skip_group_check=True)
                    return last

                @self.step("v", after={"t": S.cnt["t"]})
                def _(v):
                    v.tensor_copy(out=W1[0:64, XOFF:XOFF + 512], in_=self.P[0][0:64, :])
                    v.tensor_copy(out=W1[0:64, ZOFF:ZOFF + 512], in_=self.P[1][0:64, :])
                    return v.drain()
            @self.step("t", after={"v": S.cnt["v"]})
            def _(t):
                last = None
                for i in range(8):
                    t.matmul(out=self.P[i // 4][0:64, (i % 4) * 128:(i % 4) * 128 + 128],
                             lhsT=W1[0:64, XOFF + i * 64: XOFF + (i + 1) * 64],
                             rhs=W1[0:64, VB + i * 128: VB + (i + 1) * 128],
                             start=(i % 4 == 0), stop=(i % 4 == 3), skip_group_check=True)
                    t.matmul(out=self.P[2][:, i * 64:(i + 1) * 64],
                             lhsT=W1[0:64, KBT + i * 128: KBT + (i + 1) * 128],
                             rhs=W1[0:64, XOFF + i * 64: XOFF + (i + 1) * 64],
                             start=(i == 0), stop=(i == 7), skip_group_check=True)
                    last = t.matmul(out=self.P[3][0:64, i * 64:(i + 1) * 64],
                                    lhsT=W1[:, KSF + i * 64: KSF + (i + 1) * 64],
                                    rhs=W1[:, QSF + i * 64: QSF + (i + 1) * 64],
                                    start=(i == 0), stop=(i == 7), skip_group_check=True)
                return last

            @self.step("v", after={"t": S.cnt["t"]})
            def _(v):
                v.tensor_copy(out=W1[0:64, UOFF:UOFF + 512], in_=self.P[0][0:64, :])
                v.tensor_copy(out=W1[0:64, UOFF + 512:UOFF + 1024], in_=self.P[1][0:64, :])
                v.tensor_scalar(out=W1[:, WTN:WTN + 512], in0=self.P[2][:, :],
                                scalar1=-1.0, scalar2=None, op0=AL.mult)
                v.tensor_tensor(out=W1[0:64, ATN:ATN + 512], in0=self.P[3][0:64, :],
                                in1=self.MSK[:, 2 * 512:3 * 512], op=AL.mult)
                return v.drain()
            for i in range(8):
                c = b * 8 + i

                @self.step("t", after={"v": S.cnt["v"]})
                def _(t, i=i):
                    return t.matmul(out=self.P[4][0:64, 0:128],
                                    lhsT=W1[:, WTN + i * 64: WTN + (i + 1) * 64],
                                    rhs=self.SBB[:, 1024:1152],
                                    start=True, stop=True, skip_group_check=True)

                @self.step("v", after={"t": S.cnt["t"]})
                def _(v, i=i):
                    return v.tensor_tensor(out=W1[0:64, UOFF + i * 128: UOFF + (i + 1) * 128],
                                           in0=self.P[4][0:64, 0:128],
                                           in1=W1[0:64, UOFF + i * 128: UOFF + (i + 1) * 128],
                                           op=AL.add)

                @self.step("t", after={"v": S.cnt["v"]})
                def _(t, i=i):
                    t.matmul(out=self.P[5][0:64, 0:128],
                             lhsT=W1[:, QSF + i * 64: QSF + (i + 1) * 64],
                             rhs=self.SBB[:, 1024:1152],
                             start=True, stop=False, skip_group_check=True)
                    t.matmul(out=self.P[5][0:64, 0:128],
                             lhsT=W1[0:64, ATN + i * 64: ATN + (i + 1) * 64],
                             rhs=W1[0:64, UOFF + i * 128: UOFF + (i + 1) * 128],
                             start=False, stop=True, skip_group_check=True)
                    return t.matmul(out=self.P[4][:, 0:128],
                                    lhsT=W1[0:64, KST + i * 64 * 2 - i * 64: KST + i * 128 + 128][0:64, 0:128]
                                    if False else W1[0:64, KST + i * 128: KST + (i + 1) * 128],
                                    rhs=W1[0:64, UOFF + i * 128: UOFF + (i + 1) * 128],
                                    start=True, stop=True, skip_group_check=True)

                @self.step("v", after={"t": S.cnt["t"]})
                def _(v, i=i):
                    v.tensor_tensor(out=self.AF[:, 0:128], in0=self.AF[:, 0:128],
                                    in1=self.P[4][:, 0:128], op=AL.add)
                    v.drain()
                    v.tensor_copy(out=self.SBB[:, 1024:1152], in_=self.AF[:, 0:128])
                    v.tensor_copy(out=self.WS0[0:64, i * 128:(i + 1) * 128],
                                  in_=self.P[5][0:64, 0:128])
                    return v.drain()
            @self.step("t", after={"v": S.cnt["v"]})
            def _(t):
                last = None
                for i in range(8):
                    last = t.transpose(out=self.PT[:, i * 64:(i + 1) * 64],
                                       in_=self.WS0[0:64, i * 128:(i + 1) * 128],
                                       identity=self.EYEB[0:64, 0:64])
                return last

            @self.step("v", after={"t": S.cnt["t"]})
            def _(v):
                return v.tensor_scalar(out=self.SBB[:, 1536:2048], in0=self.PT[:, :],
                                       scalar1=0.5, scalar2=None, op0=AL.mult)
            self.dma("gd", self.olin_sp[hh * 128:(hh + 1) * 128, b * 512:(b + 1) * 512],
                     self.SBB[:, 1536:2048], after={"v": S.cnt["v"]})

        @self.step("v", after={"v": S.cnt["v"], "gd": S.cnt["gd"]})
        def _(v):
            v.memset(self.ROWV[0:1, 520:521], 0.0)
            return v.drain()

        @self.step("s", after={"v": S.cnt["v"]})
        def _(s):
            return s.activation(out=self.WS0[:, :], in_=kh, func=AF.Square)
        for w in range(4):
            @self.step("t", after={"s": S.cnt["s"], "v": S.cnt["v"]})
            def _(t, w=w):
                last = None
                for j in range(4):
                    last = t.matmul(out=self.P[j][0:1, :], lhsT=self.ONEB[:, 0:1],
                                    rhs=self.WS0[:, (w * 4 + j) * 512:(w * 4 + j + 1) * 512],
                                    start=True, stop=True, skip_group_check=True)
                return last

            @self.step("v", after={"t": S.cnt["t"]})
            def _(v, w=w):
                for j in range(4):
                    v.tensor_copy(out=self.T0[0:1, j * 512:(j + 1) * 512], in_=self.P[j][0:1, :])
                v.drain()
                v.tensor_reduce(out=self.ROWV[0:1, 521:522], in_=self.T0[0:1, 0:2048],
                                axis=AX, op=AL.max)
                v.drain()
                v.tensor_tensor(out=self.ROWV[0:1, 520:521], in0=self.ROWV[0:1, 520:521],
                                in1=self.ROWV[0:1, 521:522], op=AL.max)
                return v.drain()

        @self.step("s", after={"v": S.cnt["v"]})
        def _(s):
            return s.activation(out=self.WS0[:, :], in_=qh, func=AF.Square)
        for w in range(4):
            @self.step("t", after={"s": S.cnt["s"], "v": S.cnt["v"]})
            def _(t, w=w):
                last = None
                for j in range(4):
                    last = t.matmul(out=self.P[j][0:1, :], lhsT=self.ONEB[:, 0:1],
                                    rhs=self.WS0[:, (w * 4 + j) * 512:(w * 4 + j + 1) * 512],
                                    start=True, stop=True, skip_group_check=True)
                return last

            @self.step("v", after={"t": S.cnt["t"]})
            def _(v, w=w):
                for j in range(4):
                    v.tensor_copy(out=self.T0[0:1, j * 512:(j + 1) * 512], in_=self.P[j][0:1, :])
                v.drain()
                return v.tensor_scalar(out=self.T1[0:1, 0:2048], in0=self.T0[0:1, 0:2048],
                                       scalar1=self.ROWV[0:1, 520:521], scalar2=None, op0=AL.mult)

            @self.step("s", after={"v": S.cnt["v"]})
            def _(s):
                return s.activation(out=self.T0[0:1, 0:2048], in_=self.T1[0:1, 0:2048],
                                    func=AF.Sqrt)

            @self.step("v", after={"s": S.cnt["s"]})
            def _(v, w=w):
                return v.tensor_scalar(out=self.SBB[0:1, 0:2048],
                                       in0=self.T0[0:1, 0:2048],
                                       scalar1=-1.0, scalar2=None, op0=AL.mult)
            self.dma("gd", self.mrow_sp[hh:hh + 1, w * 2048:(w + 1) * 2048],
                     self.SBB[0:1, 0:2048], after={"v": S.cnt["v"]})

        for qg in range(16):
            njt = 4 * (qg + 1)
            tbm = self.dma("q", self.BIASR[0:1, 1024:1536],
                           self.mrow_sp[hh:hh + 1, qg * 512:(qg + 1) * 512],
                           after={"t": S.cnt["t"], "gd": S.cnt["gd"]})
            shist = [S.cnt["s"], S.cnt["s"]]
            for jt in range(njt):
                pb = self.P[0] if jt % 2 == 0 else self.P[3]

                @self.step("t", after={"s": shist[jt % 2], "q": tbm})
                def _(t, jt=jt, qg=qg, pb=pb):
                    t.matmul(out=pb[:, :], lhsT=kh[:, jt * 128:(jt + 1) * 128],
                             rhs=qh[:, qg * 512:(qg + 1) * 512],
                             start=True, stop=False, skip_group_check=True)
                    return t.matmul(out=pb[:, :], lhsT=self.ONEB[0:1, :],
                                    rhs=self.BIASR[0:1, 1024:1536],
                                    start=False, stop=True, skip_group_check=True)

                @self.step("s", after={"t": S.cnt["t"]})
                def _(s, jt=jt, pb=pb):
                    return s.activation(out=self.SBB[:, (jt % 2) * 512:(jt % 2) * 512 + 512],
                                        in_=pb[:, :], func=AF.Exp, scale=1.0 / SQD)
                shist[jt % 2] = S.cnt["s"]
                if jt >= 4 * qg:
                    @self.step("v", after={"s": S.cnt["s"]})
                    def _(v, r=jt - 4 * qg, jt=jt):
                        sb = self.SBB[:, (jt % 2) * 512:(jt % 2) * 512 + 512]
                        return v.tensor_tensor(out=sb, in0=sb,
                                               in1=self.BMASK[:, r * 512:(r + 1) * 512],
                                               op=AL.mult)

                @self.step("t", after={"s": S.cnt["s"], "v": S.cnt["v"]})
                def _(t, jt=jt, njt=njt):
                    sb = self.SBB[:, (jt % 2) * 512:(jt % 2) * 512 + 512]
                    t.matmul(out=self.P[2][0:1, :], lhsT=self.ONEB[:, 0:1],
                             rhs=sb, start=(jt == 0), stop=(jt == njt - 1),
                             skip_group_check=True)
                    return t.matmul(out=self.P[1][:, :],
                                    lhsT=self.AB[:, 16384 + jt * 128: 16384 + (jt + 1) * 128],
                                    rhs=sb, start=(jt == 0), stop=(jt == njt - 1),
                                    skip_group_check=True)
            @self.step("v", after={"t": S.cnt["t"]})
            def _(v):
                v.reciprocal(out=self.ROWV[0:1, 0:512], in_=self.P[2][0:1, :])
                return v.drain()

            @self.step("t", after={"v": S.cnt["v"]})
            def _(t):
                return t.matmul(out=self.P[4][:, :], lhsT=self.ONEF[0:1, :],
                                rhs=self.ROWV[0:1, 0:512], start=True, stop=True)
            tq = self.dma("q", self.SBB[:, 1024:1536],
                          self.olin_sp[hh * 128:(hh + 1) * 128, qg * 512:(qg + 1) * 512],
                          after={"v": S.cnt["v"]})

            @self.step("v", after={"t": S.cnt["t"], "q": tq})
            def _(v):
                v.tensor_copy(out=self.T0[:, 0:512], in_=self.P[1][:, :])
                v.drain()
                v.scalar_tensor_tensor(out=self.T1[:, 0:512], in0=self.T0[:, 0:512],
                                       scalar=0.5, in1=self.P[4][:, :],
                                       op0=AL.mult, op1=AL.mult)
                v.drain()
                return v.tensor_tensor(out=self.SBB[:, 1536:2048], in0=self.T1[:, 0:512],
                                       in1=self.SBB[:, 1024:1536], op=AL.add)
            self.dma("gd", self.a1_in[hh * 128:(hh + 1) * 128, qg * 512:(qg + 1) * 512],
                     self.SBB[:, 1536:2048], after={"v": S.cnt["v"]})


def host_constants(core):
    c = {}
    c["c_eye"] = np.eye(128, dtype=np.float32)
    c["c_ones"] = np.ones((128, 128), dtype=np.float32)
    sl = np.tril(np.ones((64, 64), np.float32), -1)
    su = np.triu(np.ones((64, 64), np.float32), 1)
    iu = np.triu(np.ones((64, 64), np.float32), 0)
    e64 = np.eye(64, dtype=np.float32)
    c["c_msk"] = np.concatenate([np.tile(sl, (1, 8)), np.tile(su, (1, 8)),
                                 np.tile(iu, (1, 8)), np.tile(e64, (1, 8)),
                                 np.tile(2 * e64, (1, 8))], axis=1)
    j = np.arange(128)[:, None]
    q = np.arange(512)[None, :]
    bm = [(j + r * 128 <= q).astype(np.float32) for r in range(4)]
    c["c_bmask"] = np.concatenate(bm, axis=1)
    pos = np.arange(TL, dtype=np.float64) + core * TL
    inv = 1.0 / (10000.0 ** (np.arange(0, DH, 2, dtype=np.float64) / DH))
    ang = pos[:, None] * inv[None, :]
    c["c_rsin"] = np.sin(ang).astype(np.float32)
    c["c_rcos"] = np.cos(ang).astype(np.float32)
    sel = np.zeros((128, NCORE), np.float32)
    sel[:, :core] = 1.0
    c["c_sel"] = sel
    return c


def qk_perm():
    p = []
    for h in range(H):
        base = h * DH
        p.extend(range(base, base + DH, 2))
        p.extend(range(base + 1, base + DH, 2))
    return np.array(p, dtype=np.int64)


def host_inputs(core, inputs):
    m = {}
    m["x"] = inputs["L0_attn_x"][0, core * TL:(core + 1) * TL, :]
    m["l0_wq"] = inputs["L0_attn_Wq"]
    m["l0_wk"] = inputs["L0_attn_Wk"]
    m["l0_wv"] = inputs["L0_attn_Wv"]
    m["l0_wo"] = inputs["L0_attn_Wo"]
    m["l0_wg"] = inputs["L0_ffn_w_gate"]
    m["l0_wu"] = inputs["L0_ffn_w_up"]
    m["l0_wd"] = inputs["L0_ffn_w_down"]
    m["l1_wq"] = inputs["L1_attn_Wq"]
    m["l1_wk"] = inputs["L1_attn_Wk"]
    m["l1_wv"] = inputs["L1_attn_Wv"]
    m["l1_wg"] = inputs["L1_attn_Wg"]
    m["l1_wo"] = inputs["L1_attn_Wo"]
    perm = qk_perm()
    b = np.stack([np.asarray(inputs["L1_attn_bq"])[perm],
                  np.asarray(inputs["L1_attn_bk"])[perm],
                  np.asarray(inputs["L1_attn_bv"]),
                  np.asarray(inputs["L1_attn_bg"]),
                  np.asarray(inputs["L1_attn_bo"])], axis=0).astype(np.float32)
    m["l1_b"] = b
    m["l1_wnorm"] = np.asarray(inputs["L1_ffn_w_norm"]).reshape(1, D)
    m["l1_wgv"] = inputs["L1_ffn_w_gv"]
    m["l1_wout"] = inputs["L1_ffn_w_out"]
    m.update(host_constants(core))
    return {k: np.ascontiguousarray(np.asarray(v, dtype=np.float32)) for k, v in m.items()}


def build_full(cfg=None):
    cfg = cfg or CFG
    kb = KB(cfg)
    kb.plan_setup()
    ph = cfg.get("phases", ("l0a", "l0f", "l1a", "l1f"))
    need_prep = bool({"l0a", "l0f", "l1a", "l1f"} & set(ph))
    if need_prep:
        kb.plan_weight_prep()
    if "l0a" in ph:
        kb.plan_l0_attn()
    if "l0f" in ph:
        kb.plan_l0_ffn()
    if "l1a" in ph:
        kb.plan_l1_attn()
    if "l1f" in ph:
        kb.plan_l1_ffn(last_block=True)
    else:
        for tc in range(NTC):
            tq = kb.dma("q", kb.T0[:, 0:D], kb.xstream[tc * 128:(tc + 1) * 128, :],
                        after=kb.S.all() if tc == 0 else {"v": kb.S.cnt["v"]})

            @kb.step("v", after={"q": tq})
            def _(v):
                return v.tensor_copy(out=kb.T1[:, 0:D], in_=kb.T0[:, 0:D])
            kb.dma("q", kb.y_out[tc * 128:(tc + 1) * 128, :], kb.T1[:, 0:D],
                   after={"v": kb.S.cnt["v"]})
    kb.plan_output_wait()
    nc = kb.finalize()
    return nc


def kernel(**inputs):
    nc = build_full(CFG)
    in_maps = [host_inputs(c, inputs) for c in range(NCORE)]
    res = run_bass_kernel_spmd(nc, in_maps, core_ids=list(range(NCORE)))
    out = np.concatenate([np.asarray(r["y"]) for r in res.results], axis=0)
    return out.reshape(1, T, D).astype(np.float32)


# version 07
# speedup vs baseline: 1.0737x; 1.0209x over previous
import contextlib
import numpy as np
import concourse.bass as bass
import concourse.mybir as mybir
from concourse.bass_utils import run_bass_kernel_spmd

F32 = mybir.dt.float32
BF16 = mybir.dt.bfloat16
FP8 = mybir.dt.float8e4
AL = mybir.AluOpType
AF = mybir.ActivationFunctionType
AX = mybir.AxisListType.X

MAGIC = 3.0 * 2.0**22
LN2 = float(np.log(2.0))
NCORE = 8
T = 8192
TL = T // NCORE
D = 2048
H = 16
DH = 128
HL = H // NCORE
CH = 64
NCHK = T // CH
NCHKL = TL // CH
I0 = 8192
HID1 = 8192
SQD = float(np.sqrt(DH))
NTC = TL // 128
GATE_NORM = 16.0
CONST_VALS = [-2.0 * float(np.log(2.0)), 2.0 * float(np.log(2.0)),
              4.0 * float(np.log(2.0)), -4.0 * float(np.log(2.0))]

CFG = {"phases": ("l0a", "l0f", "l1a", "l1f"), "taps": ()}


class Sched:
    def __init__(self, nc):
        self.nc = nc
        self.steps = []
        self.cnt = {k: 0 for k in ("t", "v", "s", "g", "q", "gd", "sd", "cc")}
        self.sems = {}
        self.hiw = {}

    def make_sems(self, stack):
        for k in self.cnt:
            self.sems[k] = stack.enter_context(self.nc.semaphore("sem_" + k))

    def all(self):
        return {k: v for k, v in self.cnt.items() if v > 0}

    def emit_for(self, ek_engine, eng):
        for ek, fn, waits in self.steps:
            if ek != ek_engine:
                continue
            for sk, val in waits.items():
                hk = (ek, sk)
                if self.hiw.get(hk, 0) >= val:
                    continue
                self.hiw[hk] = val
                eng.wait_ge(self.sems[sk], val)
            fn(eng)


class KB:
    def __init__(self, cfg):
        self.cfg = cfg
        self.nc = bass.Bass("TRN2", target_bir_lowering=False, debug=False)
        self.S = Sched(self.nc)
        self.stack = contextlib.ExitStack()
        self._declare_dram()
        self._declare_sbuf()
        self.S.make_sems(self.stack)
        self.taps = []
        self._dynreg = {}

    def _declare_dram(self):
        nc = self.nc

        def din(name, shape, dt=F32):
            return nc.dram_tensor(name, list(shape), dt, kind="ExternalInput").ap()

        def mk(name, shape, dt):
            return nc.dram_tensor(name, list(shape), dt, kind="Internal").ap()

        self.x_in = din("x", (TL, D))
        self.y_out = nc.dram_tensor("y", [TL, D], F32, kind="ExternalOutput").ap()
        w = {}
        for n, shape in [("l0_wq", (D, D)), ("l0_wk", (D, D)), ("l0_wv", (D, D)),
                         ("l0_wo", (D, D)), ("l0_wg", (I0, D)), ("l0_wu", (I0, D)),
                         ("l0_wd", (D, I0)), ("l1_wq", (D, D)), ("l1_wk", (D, D)),
                         ("l1_wv", (D, D)), ("l1_wg", (D, D)), ("l1_wo", (D, D)),
                         ("l1_b", (5, D)), ("l1_wnorm", (1, D)),
                         ("l1_wgv", (2 * HID1, D)), ("l1_wout", (D, HID1))]:
            w[n] = din(n, shape)
        self.w = w
        c = {}
        for k, shape in [("eye", (128, 128)), ("ones", (128, 128)),
                         ("msk", (64, 5 * 512)), ("bmask", (128, 4 * 512)),
                         ("rsin", (TL, 64)), ("rcos", (TL, 64)),
                         ("sel", (128, NCORE))]:
            c[k] = din("c_" + k, shape)
        self.c = c
        self.w0_in = mk("w0_in", (4 * 256, D), BF16)
        self.w0_out = mk("w0_out", (NCORE * 4 * 256, D), BF16)
        self.w1_in = mk("w1_in", (2 * 1024, D), BF16)
        self.w1_out = mk("w1_out", (2 * I0, D), BF16)
        self.w1b_in = mk("w1b_in", (256, I0), BF16)
        self.w1b_out = mk("w1b_out", (D, I0), BF16)
        self.w2_in = mk("w2_in", (5 * 256, D), BF16)
        self.w2_out = mk("w2_out", (5 * D, D), BF16)
        self.w3_in = mk("w3_in", (D, D), BF16)
        self.w3_out = mk("w3_out", (2 * HID1, D), BF16)
        self.w3b_in = mk("w3b_in", (256, HID1), BF16)
        self.w3b_out = mk("w3b_out", (D, HID1), BF16)
        self.sc_in = mk("sc_in", (1, 8), F32)
        self.sc_out = mk("sc_out", (1, 8), F32)
        self.a0_in = mk("a0_in", (3 * D, TL), BF16)
        self.a0_out = mk("a0_out", (NCORE * 3 * D, TL), BF16)
        self.a1_in = mk("a1_in", (HL * DH, T), BF16)
        self.a1_out = mk("a1_out", (D, T), BF16)
        self.s1_in = mk("s1_in", (H * DH, 136), F32)
        self.s1_out = mk("s1_out", (NCORE * H * DH, 136), F32)
        self.xstream = mk("xstream", (TL, D), F32)
        self.u_spill = mk("u_spill", (TL, HID1), BF16)
        self.qkvg_sp = mk("qkvg_sp", (4 * TL, D), BF16)
        self.brow_sp = mk("brow_sp", (1, 6 * 2048), BF16)
        self.st_w0 = [mk("st_w0_%d" % i, (256, D), F32) for i in range(4)]
        self.st_wg = mk("st_wg", (1024, D), F32)
        self.st_wu = mk("st_wu", (1024, D), F32)
        self.st_wd = mk("st_wd", (256, I0), F32)
        self.st_wgv = mk("st_wgv", (2048, D), F32)
        self.st_wout = mk("st_wout", (256, I0), F32)
        self.hstage = mk("hstage", (128, NCORE * 3 * 1024), BF16)
        self.a1st = mk("a1st", (1, D, TL), BF16)
        self.mrow_sp = mk("mrow_sp", (HL, T), BF16)
        self.mk = mk

    def _declare_sbuf(self):
        nc, st = self.nc, self.stack

        def sbuf(name, shape, dt):
            return st.enter_context(nc.sbuf_tensor(name, list(shape), dt))

        def psum(name, shape):
            return st.enter_context(nc.psum_tensor(name, list(shape), F32))

        self.AF = sbuf("A_f32", (128, 16384), F32)
        self.AB = sbuf("A_bf", (128, 24576), BF16)
        self.SBB = sbuf("s_bf", (128, 2048), BF16)
        self.WS0 = sbuf("ws0", (128, 8192), BF16)
        self.WS1 = sbuf("ws1", (128, 8192), BF16)
        self.T0 = sbuf("t0", (128, 2048), F32)
        self.T1 = sbuf("t1", (128, 2048), F32)
        self.ROPE = sbuf("ropebuf", (128, 128), F32)
        self.EYEB = sbuf("eyeb", (128, 128), BF16)
        self.ONEF = sbuf("onef", (128, 128), F32)
        self.ONEB = sbuf("oneb", (128, 128), BF16)
        self.MSK = sbuf("mskS", (64, 5 * 512), BF16)
        self.BMASK = sbuf("bmaskb", (128, 4 * 512), BF16)
        self.SEL = sbuf("selS", (128, NCORE), F32)
        self.SM0 = sbuf("sm0", (128, 128), F32)
        self.SM1 = sbuf("sm1", (128, 128), F32)
        self.SM2 = sbuf("sm2", (128, 128), F32)
        self.SM3 = sbuf("sm3", (128, 128), F32)
        self.CONSTC = sbuf("constc", (128, 16), F32)
        for ci, val in enumerate(CONST_VALS):
            self.nc.const_aps.register(F32, val, self.CONSTC[:, 3 + ci:4 + ci])
        self.ROWV = sbuf("rowv", (1, 1024), F32)
        self.BIASR = sbuf("biasrow", (1, 2048), BF16)
        self.WNBB = sbuf("wnbb", (128, 2048), BF16)
        self.FQ = sbuf("fq", (128, 1024), FP8)
        self.P = [psum("p%d" % i, (128, 512)) for i in range(6)]
        self.PT = self.stack.enter_context(self.nc.psum_tensor("pt0", [128, 512], BF16))
        self.PT2 = self.stack.enter_context(self.nc.psum_tensor("pt1", [128, 512], BF16))

    def dma(self, qk, out, in_, after=None, **kw):
        S = self.S
        ek = {"q": "sync", "gd": "g", "sd": "s"}[qk]
        S.cnt[qk] += 16
        sem = S.sems[qk]

        def fn(eng, out=out, in_=in_, kw=kw):
            eng.dma_start(out=out, in_=in_, **kw).then_inc(sem, 16)
        S.steps.append((ek, fn, {k: v for k, v in (after or {}).items() if v > 0}))
        return S.cnt[qk]

    def dmaT(self, qk, out, in_, after=None):
        S = self.S
        ek = {"q": "sync", "gd": "g", "sd": "s"}[qk]
        S.cnt[qk] += 16
        sem = S.sems[qk]

        def fn(eng, out=out, in_=in_):
            eng.dma_start_transpose(out=out, in_=in_).then_inc(sem, 16)
        S.steps.append((ek, fn, {k: v for k, v in (after or {}).items() if v > 0}))
        return S.cnt[qk]

    def _dyn_off(self, eng, mult):
        key = (id(eng), mult)
        if key not in self._dynreg:
            pid = self.nc.partition_id()
            self._dynreg[key] = eng.compute_val(pid * mult)
        return self._dynreg[key]

    def dma_dyn(self, qk, out, src_tensor, mult, roff, nrows, cols, after=None):
        S = self.S
        ek = {"q": "sync", "gd": "g", "sd": "s"}[qk]
        S.cnt[qk] += 16
        sem = S.sems[qk]

        def fn(eng, out=out):
            off = self._dyn_off(eng, mult)
            nr = src_tensor.shape[0]
            base = src_tensor[roff:nr, cols] if roff else src_tensor[:, cols]
            eng.dma_start(out=out, in_=base[bass.ds(off, nrows), :]).then_inc(sem, 16)
        S.steps.append((ek, fn, {k: v for k, v in (after or {}).items() if v > 0}))
        return S.cnt[qk]

    def step(self, ek, after=None):
        S = self.S
        S.cnt[ek] += 1
        sem = S.sems[ek]

        def deco(fn):
            def wrapped(eng, fn=fn):
                last = fn(eng)
                assert last is not None
                last.then_inc(sem, 1)
            S.steps.append((ek, wrapped, {k: v for k, v in (after or {}).items() if v > 0}))
        return deco

    def cc(self, kind, op, ins, outs, after=None):
        S = self.S
        S.cnt["cc"] += 1
        sem = S.sems["cc"]

        def fn(eng, ins=ins, outs=outs):
            eng.collective_compute(
                kind, op, replica_groups=[list(range(NCORE))],
                ins=[a.opt() for a in ins], outs=[a.opt() for a in outs],
            ).then_inc(sem, 1)
        S.steps.append(("g", fn, {k: v for k, v in (after or {}).items() if v > 0}))
        return S.cnt["cc"]

    def tap(self, name, shape, dt, src_fn):
        if name not in self.cfg.get("taps", ()):
            return
        t = self.nc.dram_tensor("tap_" + name, list(shape), dt, kind="ExternalOutput").ap()
        self.dma("gd", t[:, :], src_fn(), after=self.S.all())

    def plan_setup(self):
        c, S = self.c, self.S
        self.dma("q", self.T1[:, 0:128], c["eye"][:, :])
        self.dma("q", self.ONEF[:, :], c["ones"][:, :])
        self.dma("q", self.AF[0:64, 0:2560], c["msk"][:, :])
        self.dma("q", self.SEL[:, :], c["sel"][:, :])
        self.dma("q", self.T0[:, 0:2048], c["bmask"][:, :])
        tq = self.dma("q", self.xstream[:, :], self.x_in[:, :])

        @self.step("v", after={"q": tq})
        def _(v):
            v.tensor_copy(out=self.EYEB[:, :], in_=self.T1[:, 0:128])
            v.tensor_copy(out=self.ONEB[:, :], in_=self.ONEF[:, :])
            v.tensor_copy(out=self.MSK[:, :], in_=self.AF[0:64, 0:2560])
            v.tensor_copy(out=self.BMASK[:, :], in_=self.T0[:, 0:2048])
            v.memset(self.CONSTC[:, 0:1], 2.0)
            v.memset(self.CONSTC[:, 1:2], 4.0)
            v.memset(self.CONSTC[:, 2:3], 0.0)
            for ci, val in enumerate(CONST_VALS):
                v.memset(self.CONSTC[:, 3 + ci:4 + ci], val)
            return v.drain()
        for mb in range(5):
            tq = self.dma("q", self.AF[0:1, mb * 2048:(mb + 1) * 2048],
                          self.w["l1_b"][mb:mb + 1, :],
                          after={"v": S.cnt["v"]} if mb == 0 else None)
        tq = self.dma("q", self.AF[0:1, 10240:12288], self.w["l1_wnorm"][:, :])
        for mth in range(6):
            @self.step("v", after={"q": tq} if mth == 0 else {"gd": S.cnt["gd"]})
            def _(v, mth=mth):
                return v.tensor_copy(out=self.BIASR[0:1, 0:2048],
                                     in_=self.AF[0:1, mth * 2048:(mth + 1) * 2048])
            self.dma("gd", self.brow_sp[0:1, mth * 2048:(mth + 1) * 2048],
                     self.BIASR[0:1, 0:2048], after={"v": S.cnt["v"]})

    def plan_weight_prep(self):
        S = self.S
        for i, wn in enumerate(["l0_wq", "l0_wk", "l0_wv", "l0_wo"]):
            self.dma_dyn("q", self.st_w0[i][:, :], self.w[wn], 256, 0, 256, slice(0, D))
        self.dma_dyn("q", self.st_wg[:, :], self.w["l0_wg"], 1024, 0, 1024, slice(0, D))
        self.dma_dyn("q", self.st_wu[:, :], self.w["l0_wu"], 1024, 0, 1024, slice(0, D))
        self.dma_dyn("q", self.st_wd[:, :], self.w["l0_wd"], 256, 0, 256, slice(0, I0))
        self.dma_dyn("q", self.st_wgv[:, :], self.w["l1_wgv"], 2048, 0, 2048, slice(0, D))
        t_stage = self.dma_dyn("q", self.st_wout[:, :], self.w["l1_wout"], 256, 0, 256,
                               slice(0, I0))
        self.t_stage = t_stage
        for i, wn in enumerate(["l0_wq", "l0_wk", "l0_wv", "l0_wo"]):
            for half in range(2):
                tq = self.dma("q", self.T0[:, 0:2048],
                              self.st_w0[i][half * 128:(half + 1) * 128, :],
                              after={"v": S.cnt["v"], "gd": S.cnt["gd"]})

                @self.step("v", after={"q": tq})
                def _(v):
                    return v.tensor_copy(out=self.SBB[:, 0:2048], in_=self.T0[:, 0:2048])
                r0 = i * 256 + half * 128
                self.dma("gd", self.w0_in[r0:r0 + 128, :], self.SBB[:, 0:2048],
                         after={"v": S.cnt["v"]})
        self.t_w0 = self.cc("AllGather", AL.bypass, [self.w0_in], [self.w0_out],
                            after={"gd": S.cnt["gd"]})

        jobs = []
        for i, wn in enumerate(["l0_wg", "l0_wu"]):
            for r in range(8):
                jobs.append((wn, 1024, r * 128, slice(0, 2048),
                             self.w1_in, i * 1024 + r * 128, slice(0, 2048)))
        for r in range(2):
            for p in range(4):
                jobs.append(("l0_wd", 256, r * 128, slice(p * 2048, (p + 1) * 2048),
                             self.w1b_in, r * 128, slice(p * 2048, (p + 1) * 2048)))
        stmap = {"l0_wg": self.st_wg, "l0_wu": self.st_wu, "l0_wd": self.st_wd,
                 "l1_wgv": self.st_wgv, "l1_wout": self.st_wout}
        vh = [0, 0]
        gh = [0, 0]
        for j, (wn, srows, roff, csl, dest, drow, dsl) in enumerate(jobs):
            ro = j % 2
            tq = self.dma("q", self.AF[:, ro * 2048:(ro + 1) * 2048],
                          stmap[wn][roff:roff + 128, csl],
                          after={"v": vh[ro], "gd": gh[ro], "q": self.t_stage if j < 2 else 0})
            self._emit_e2m1(after={"q": tq}, ro=ro)
            vh[ro] = S.cnt["v"]
            outb = self.SBB[:, 0:2048] if ro == 0 else self.WS1[:, 0:2048]
            gh[ro] = self.dma("gd", dest[drow:drow + 128, dsl], outb,
                              after={"v": S.cnt["v"]})
        self.t_w1 = self.cc("AllGather", AL.bypass, [self.w1_in], [self.w1_out],
                            after={"gd": S.cnt["gd"]})
        self.t_w1b = self.cc("AllGather", AL.bypass, [self.w1b_in], [self.w1b_out],
                             after={"gd": S.cnt["gd"]})

        self._plan_l1w_prep()
        self._plan_ternary()

    def _emit_e2m1(self, after, ro=0):
        nb = 64
        co = ro * 64
        xin = self.AF[:, ro * 2048:(ro + 1) * 2048]
        x3 = xin.rearrange("p (n b) -> p n b", b=32)
        yy = self.AF[:, 4096 + ro * 2048: 4096 + (ro + 1) * 2048]
        aa = self.AF[:, 8192 + ro * 2048: 8192 + (ro + 1) * 2048]
        t2 = self.WS0[:, ro * 4096: ro * 4096 + 2048]
        t4 = self.WS0[:, ro * 4096 + 2048: ro * 4096 + 4096]
        outb = self.SBB[:, 0:2048] if ro == 0 else self.WS1[:, 0:2048]
        sm0 = self.SM0[:, co:co + nb]
        sm1 = self.SM1[:, co:co + nb]
        sm2 = self.SM2[:, co:co + nb]

        @self.step("v", after=after)
        def _(v):
            v.tensor_reduce(out=sm0, in_=x3, axis=AX, op=AL.max,
                            apply_absolute_value=True)
            v.drain()
            v.tensor_scalar(out=sm0, in0=sm0, scalar1=1e-30, scalar2=None, op0=AL.max)
            return v.drain()

        @self.step("s", after={"v": self.S.cnt["v"]})
        def _(s):
            return s.activation(out=sm1, in_=sm0, func=AF.Ln)

        @self.step("v", after={"s": self.S.cnt["s"]})
        def _(v):
            v.tensor_scalar(out=sm1, in0=sm1, scalar1=1.0 / LN2, scalar2=-0.5,
                            op0=AL.mult, op1=AL.add)
            v.drain()
            v.tensor_scalar(out=sm1, in0=sm1, scalar1=MAGIC, scalar2=MAGIC,
                            op0=AL.add, op1=AL.subtract)
            return v.drain()

        @self.step("s", after={"v": self.S.cnt["v"]})
        def _(s):
            s.activation(out=sm0, in_=sm1, func=AF.Exp, scale=LN2, bias=-2.0 * LN2)
            return s.activation(out=sm2, in_=sm1, func=AF.Exp, scale=-LN2, bias=2.0 * LN2)

        @self.step("v", after={"s": self.S.cnt["s"]})
        def _(v):
            v.drain()
            ibs = sm2.broadcast_to((128, nb, 32))
            bs = sm0.broadcast_to((128, nb, 32))
            y3 = yy.rearrange("p (n b) -> p n b", b=32)
            c2 = self.CONSTC[:, 0:1].broadcast_to((128, 2048))
            c4 = self.CONSTC[:, 1:2].broadcast_to((128, 2048))
            v.tensor_tensor(out=y3, in0=x3, in1=ibs, op=AL.mult)
            v.tensor_scalar(out=aa, in0=yy, scalar1=-1.0, scalar2=None, op0=AL.mult)
            v.tensor_tensor(out=aa, in0=aa, in1=yy, op=AL.max)
            v.tensor_tensor(out=t2, in0=aa, in1=c2, op=AL.is_ge)
            v.tensor_tensor(out=t4, in0=aa, in1=c4, op=AL.is_ge)
            v.tensor_scalar(out=aa, in0=t4, scalar1=-0.5, scalar2=2.0, op0=AL.mult, op1=AL.add)
            v.scalar_tensor_tensor(out=aa, in0=t2, scalar=-1.0, in1=aa, op0=AL.mult, op1=AL.add)
            v.tensor_tensor(out=yy, in0=yy, in1=aa, op=AL.mult)
            v.tensor_scalar(out=yy, in0=yy, scalar1=MAGIC, scalar2=MAGIC,
                            op0=AL.add, op1=AL.subtract)
            v.tensor_scalar(out=aa, in0=t2, scalar1=0.5, scalar2=0.5, op0=AL.mult, op1=AL.add)
            v.tensor_tensor(out=aa, in0=aa, in1=t4, op=AL.add)
            v.tensor_tensor(out=yy, in0=yy, in1=aa, op=AL.mult)
            v.tensor_scalar(out=yy, in0=yy, scalar1=6.0, scalar2=-6.0, op0=AL.min, op1=AL.max)
            o3 = outb.rearrange("p (n b) -> p n b", b=32)
            return v.tensor_tensor(out=o3, in0=y3, in1=bs, op=AL.mult)

    def _plan_l1w_prep(self):
        S = self.S
        self.l1wst = {}
        for i, wn in enumerate(["l1_wq", "l1_wk", "l1_wv", "l1_wg", "l1_wo"]):
            st = self.mk("l1wst_" + wn, (1, 2048, 256), F32)
            self.l1wst[wn] = st
            Sq = self.S
            Sq.cnt["q"] += 16
            sem = Sq.sems["q"]

            def stfn(eng, wn=wn, st=st, sem=sem):
                p1 = self._dyn_off(eng, 1)
                view = self.w[wn].rearrange("r (n c) -> n r c", c=256)
                eng.dma_start(out=st[:, :, :], in_=view[bass.ds(p1, 1), :, :]).then_inc(sem, 16)
            Sq.steps.append(("sync", stfn, {}))
        t_stage = S.cnt["q"]
        for i, wn in enumerate(["l1_wq", "l1_wk", "l1_wv", "l1_wg", "l1_wo"]):
            perm = wn in ("l1_wq", "l1_wk")
            for rt in range(16):
                ek_sem = S.sems["q"]
                S.cnt["q"] += 32 if perm else 16
                waits = {"v": S.cnt["v"], "t": S.cnt["t"]}

                def ldfn(eng, wn=wn, rt=rt, perm=perm, sem=ek_sem):
                    st = self.l1wst[wn]
                    if perm:
                        eng.dma_start(out=self.T0[0:64, 0:256],
                                      in_=st[0, rt * 128 + 0: rt * 128 + 128: 2, :]).then_inc(sem, 16)
                        eng.dma_start(out=self.T0[64:128, 0:256],
                                      in_=st[0, rt * 128 + 1: rt * 128 + 128: 2, :]).then_inc(sem, 16)
                    else:
                        eng.dma_start(out=self.T0[:, 0:256],
                                      in_=st[0, rt * 128: rt * 128 + 128, :]).then_inc(sem, 16)
                S.steps.append(("sync", ldfn, waits))
                tq = S.cnt["q"]

                @self.step("v", after={"q": tq, "gd": S.cnt["gd"]})
                def _(v):
                    return v.tensor_copy(out=self.SBB[:, 0:256], in_=self.T0[:, 0:256])

                @self.step("t", after={"v": S.cnt["v"]})
                def _(t):
                    t.transpose(out=self.PT[:, 0:128], in_=self.SBB[:, 0:128],
                                identity=self.EYEB[:, :])
                    return t.transpose(out=self.PT[:, 128:256], in_=self.SBB[:, 128:256],
                                       identity=self.EYEB[:, :])

                @self.step("v", after={"t": S.cnt["t"]})
                def _(v):
                    return v.tensor_copy(out=self.SBB[:, 256:512], in_=self.PT[:, 0:256])
                tv = S.cnt["v"]
                self.dma("gd", self.w2_in[i * 256: i * 256 + 128, rt * 128: rt * 128 + 128],
                         self.SBB[:, 256:384], after={"v": tv})
                self.dma("gd", self.w2_in[i * 256 + 128: i * 256 + 256, rt * 128: rt * 128 + 128],
                         self.SBB[:, 384:512], after={"v": tv})
        self.t_w2 = self.cc("AllGather", AL.bypass, [self.w2_in], [self.w2_out],
                            after={"gd": S.cnt["gd"]})

    def _plan_ternary(self):
        S = self.S

        @self.step("v", after={"v": S.cnt["v"]})
        def _(v):
            v.memset(self.SM3[:, 0:2], 0.0)
            return v.drain()
        jobs = [("l1_wgv", 2048, r * 128, slice(0, 2048), 0, self.w3_in, r * 128, slice(0, 2048))
                for r in range(16)]
        for r in range(2):
            for p in range(4):
                jobs.append(("l1_wout", 256, r * 128, slice(p * 2048, (p + 1) * 2048), 1,
                             self.w3b_in, r * 128, slice(p * 2048, (p + 1) * 2048)))
        stmap = {"l1_wgv": self.st_wgv, "l1_wout": self.st_wout}
        vh = [0, 0]
        for j, (wn, srows, roff, csl, col, dest, drow, dsl) in enumerate(jobs):
            ro = j % 2
            tq = self.dma("q", self.AF[:, ro * 2048:(ro + 1) * 2048],
                          stmap[wn][roff:roff + 128, csl],
                          after={"v": vh[ro], "q": self.t_stage if j < 2 else 0})

            @self.step("v", after={"q": tq})
            def _(v, col=col, ro=ro):
                v.tensor_reduce(out=self.SM1[:, 4:5], in_=self.AF[:, ro * 2048:(ro + 1) * 2048],
                                axis=AX, op=AL.add, apply_absolute_value=True)
                v.drain()
                v.tensor_tensor(out=self.SM3[:, col:col + 1], in0=self.SM3[:, col:col + 1],
                                in1=self.SM1[:, 4:5], op=AL.add)
                return v.drain()
            vh[ro] = S.cnt["v"]

        @self.step("v", after={"v": S.cnt["v"]})
        def _(v):
            return v.tensor_copy(out=self.SBB[:, 0:2], in_=self.SM3[:, 0:2])

        @self.step("t", after={"v": S.cnt["v"]})
        def _(t):
            return t.matmul(out=self.P[0][0:1, 0:2], lhsT=self.ONEB[:, 0:1],
                            rhs=self.SBB[:, 0:2], start=True, stop=True)

        @self.step("v", after={"t": S.cnt["t"]})
        def _(v):
            return v.tensor_copy(out=self.ROWV[0:1, 0:2], in_=self.P[0][0:1, 0:2])
        self.dma("gd", self.sc_in[:, 0:2], self.ROWV[0:1, 0:2], after={"v": S.cnt["v"]})
        t_ar = self.cc("AllReduce", AL.add, [self.sc_in], [self.sc_out],
                       after={"gd": S.cnt["gd"]})
        tq = self.dma("q", self.ROWV[0:1, 4:6], self.sc_out[:, 0:2], after={"cc": t_ar})

        @self.step("v", after={"q": tq})
        def _(v):
            v.tensor_scalar(out=self.ROWV[0:1, 6:7], in0=self.ROWV[0:1, 4:5],
                            scalar1=1.0 / (2.0 * HID1 * D), scalar2=1e-5,
                            op0=AL.mult, op1=AL.max)
            v.tensor_scalar(out=self.ROWV[0:1, 7:8], in0=self.ROWV[0:1, 5:6],
                            scalar1=1.0 / (1.0 * D * HID1), scalar2=1e-5,
                            op0=AL.mult, op1=AL.max)
            v.drain()
            v.reciprocal(out=self.ROWV[0:1, 8:9], in_=self.ROWV[0:1, 6:7])
            v.drain()
            v.reciprocal(out=self.ROWV[0:1, 9:10], in_=self.ROWV[0:1, 7:8])
            return v.drain()

        @self.step("t", after={"v": S.cnt["v"]})
        def _(t):
            return t.matmul(out=self.P[0][:, 0:4], lhsT=self.ONEF[0:1, :],
                            rhs=self.ROWV[0:1, 6:10], start=True, stop=True)

        @self.step("v", after={"t": S.cnt["t"]})
        def _(v):
            v.tensor_copy(out=self.SM3[:, 4:6], in_=self.P[0][:, 0:2])
            v.tensor_copy(out=self.SM3[:, 6:8], in_=self.P[0][:, 2:4])
            return v.drain()
        vh = [0, 0]
        gh = [S.cnt["gd"], S.cnt["gd"]]
        for j, (wn, srows, roff, csl, col, dest, drow, dsl) in enumerate(jobs):
            ro = j % 2
            tq = self.dma("q", self.AF[:, ro * 2048:(ro + 1) * 2048],
                          stmap[wn][roff:roff + 128, csl],
                          after={"v": vh[ro], "gd": gh[ro]})

            @self.step("v", after={"q": tq})
            def _(v, col=col, ro=ro):
                yy = self.AF[:, 4096 + ro * 2048: 4096 + (ro + 1) * 2048]
                outb = self.SBB[:, 0:2048] if ro == 0 else self.WS1[:, 0:2048]
                v.tensor_scalar(out=yy, in0=self.AF[:, ro * 2048:(ro + 1) * 2048],
                                scalar1=self.SM3[:, 6 + col:7 + col], scalar2=MAGIC,
                                op0=AL.mult, op1=AL.add)
                v.tensor_scalar(out=yy, in0=yy, scalar1=MAGIC, scalar2=1.0,
                                op0=AL.subtract, op1=AL.min)
                return v.tensor_scalar(out=outb, in0=yy, scalar1=-1.0,
                                       scalar2=None, op0=AL.max)
            vh[ro] = S.cnt["v"]
            outb = self.SBB[:, 0:2048] if ro == 0 else self.WS1[:, 0:2048]
            gh[ro] = self.dma("gd", dest[drow:drow + 128, dsl], outb,
                              after={"v": S.cnt["v"]})
        self.t_w3 = self.cc("AllGather", AL.bypass, [self.w3_in], [self.w3_out],
                            after={"gd": S.cnt["gd"]})
        self.t_w3b = self.cc("AllGather", AL.bypass, [self.w3b_in], [self.w3b_out],
                             after={"gd": S.cnt["gd"]})

    def _emit_mxact(self, x_ap, nb, out_bf, fq_cols, after):
        ncol = nb * 32
        x3 = x_ap.rearrange("p (n b) -> p n b", b=32)
        cb = slice(64, 64 + nb)

        @self.step("v", after=after)
        def _(v):
            v.tensor_reduce(out=self.SM0[:, cb], in_=x3, axis=AX, op=AL.max,
                            apply_absolute_value=True)
            v.drain()
            v.tensor_scalar(out=self.SM0[:, cb], in0=self.SM0[:, cb],
                            scalar1=1e-30, scalar2=None, op0=AL.max)
            return v.drain()

        @self.step("s", after={"v": self.S.cnt["v"]})
        def _(s):
            return s.activation(out=self.SM1[:, cb], in_=self.SM0[:, cb], func=AF.Ln)

        @self.step("v", after={"s": self.S.cnt["s"]})
        def _(v):
            v.tensor_scalar(out=self.SM1[:, cb], in0=self.SM1[:, cb],
                            scalar1=1.0 / LN2, scalar2=-0.5, op0=AL.mult, op1=AL.add)
            v.drain()
            v.tensor_scalar(out=self.SM1[:, cb], in0=self.SM1[:, cb],
                            scalar1=MAGIC, scalar2=MAGIC, op0=AL.add, op1=AL.subtract)
            return v.drain()

        @self.step("s", after={"v": self.S.cnt["v"]})
        def _(s):
            s.activation(out=self.SM0[:, cb], in_=self.SM1[:, cb], func=AF.Exp,
                         scale=LN2, bias=4.0 * LN2)
            return s.activation(out=self.SM2[:, cb], in_=self.SM1[:, cb], func=AF.Exp,
                                scale=-LN2, bias=-4.0 * LN2)

        @self.step("v", after={"s": self.S.cnt["s"]})
        def _(v):
            v.drain()
            ibs = self.SM2[:, cb].broadcast_to((128, nb, 32))
            bs = self.SM0[:, cb].broadcast_to((128, nb, 32))
            fq = self.FQ[:, fq_cols]
            f3 = fq.rearrange("p (n b) -> p n b", b=32)
            v.tensor_tensor(out=f3, in0=x3, in1=ibs, op=AL.mult)
            v.tensor_scalar(out=fq, in0=fq, scalar1=0.1171875, scalar2=-0.1171875,
                            op0=AL.min, op1=AL.max)
            o3 = out_bf.rearrange("p (n b) -> p n b", b=32)
            return v.tensor_tensor(out=o3, in0=f3, in1=bs, op=AL.mult)

    def trans_tiles(self, pairs, after):
        S = self.S
        n = len(pairs)
        for i0 in range(0, n, 2):
            grp = pairs[i0:i0 + 2]

            @self.step("t", after={"v": S.cnt["v"], "t": S.cnt["t"]} if i0 else dict(after))
            def _(t, grp=grp, i0=i0):
                last = None
                for j, (srcap, _d) in enumerate(grp):
                    pp = self.PT if j == 0 else self.PT2
                    last = t.transpose(out=pp[:, 0:128], in_=srcap,
                                       identity=self.EYEB[:, :])
                return last

            @self.step("v", after={"t": S.cnt["t"]})
            def _(v, grp=grp):
                last = None
                for j, (_s, dstap) in enumerate(grp):
                    pp = self.PT if j == 0 else self.PT2
                    last = v.tensor_copy(out=dstap, in_=pp[:, 0:128])
                return last
        return S.cnt["v"]

    def ln_write(self, t1_ap, dst_hbm_rows, after, extra_scale=None):
        S = self.S

        @self.step("v", after=after)
        def _(v):
            v.tensor_reduce(out=self.SM1[:, 8:9], in_=t1_ap, axis=AX, op=AL.add)
            v.tensor_tensor(out=self.T0[:, 0:D], in0=t1_ap, in1=t1_ap, op=AL.mult)
            v.tensor_reduce(out=self.SM1[:, 9:10], in_=self.T0[:, 0:D], axis=AX, op=AL.add)
            v.drain()
            v.tensor_scalar(out=self.SM1[:, 8:9], in0=self.SM1[:, 8:9],
                            scalar1=1.0 / D, scalar2=None, op0=AL.mult)
            v.tensor_scalar(out=self.SM1[:, 9:10], in0=self.SM1[:, 9:10],
                            scalar1=1.0 / D, scalar2=None, op0=AL.mult)
            v.drain()
            v.tensor_tensor(out=self.SM1[:, 10:11], in0=self.SM1[:, 8:9],
                            in1=self.SM1[:, 8:9], op=AL.mult)
            v.drain()
            v.tensor_tensor(out=self.SM1[:, 10:11], in0=self.SM1[:, 9:10],
                            in1=self.SM1[:, 10:11], op=AL.subtract)
            v.drain()
            v.tensor_scalar(out=self.SM1[:, 10:11], in0=self.SM1[:, 10:11],
                            scalar1=1e-5, scalar2=None, op0=AL.add)
            return v.drain()

        @self.step("s", after={"v": S.cnt["v"]})
        def _(s):
            return s.activation(out=self.SM1[:, 11:12], in_=self.SM1[:, 10:11], func=AF.Sqrt)

        @self.step("v", after={"s": S.cnt["s"]})
        def _(v):
            v.reciprocal(out=self.SM1[:, 12:13], in_=self.SM1[:, 11:12])
            v.drain()
            return v.tensor_scalar(out=self.T1[:, 0:D], in0=t1_ap,
                                   scalar1=self.SM1[:, 8:9], scalar2=self.SM1[:, 12:13],
                                   op0=AL.subtract, op1=AL.mult)
        tq = self.dma("q", dst_hbm_rows, self.T1[:, 0:D], after={"v": S.cnt["v"]})
        return tq

    def plan_l0_ffn(self):
        S = self.S
        start = S.all()
        for tc in range(NTC):
            tq = self.dma("q", self.T0[:, 0:D], self.xstream[tc * 128:(tc + 1) * 128, :],
                          after=start if tc == 0 else {"v": S.cnt["v"]})

            @self.step("v", after={"q": tq})
            def _(v):
                return v.tensor_copy(out=self.AF[:, 0:D], in_=self.T0[:, 0:D])
            self._emit_mxact(self.AF[:, 0:1024], 32, self.SBB[:, 0:1024], slice(0, 1024),
                             after={"v": S.cnt["v"]})
            self._emit_mxact(self.AF[:, 1024:2048], 32, self.SBB[:, 1024:2048], slice(0, 1024),
                             after={"v": S.cnt["v"]})
            pairs = [(self.SBB[:, dc * 128:(dc + 1) * 128],
                      self.AB[:, dc * 1024 + tc * 128: dc * 1024 + tc * 128 + 128])
                     for dc in range(16)]
            self.trans_tiles(pairs, after={"v": S.cnt["v"]})
        t_xq = S.all()

        WST = self.WS0
        for os in range(8):
            tw = 0
            for dc in range(16):
                for j in range(2):
                    buf = self.WS0 if dc < 8 else self.WS1
                    col = (dc % 8) * 1024 + j * 512
                    tw = self.dmaT("q", buf[:, col:col + 512],
                                   self.w1_out[os * 2048 + j * 512: os * 2048 + (j + 1) * 512,
                                               dc * 128:(dc + 1) * 128],
                                   after={"t": S.cnt["t"], "v": S.cnt["v"]} if (dc == 0 and j == 0) else None)
            sh = [tw, tw]
            for tc in range(NTC):
                b0 = self.P[0] if tc % 2 == 0 else self.P[2]
                b1 = self.P[1] if tc % 2 == 0 else self.P[3]

                @self.step("t", after={"q": tw, "s": sh[tc % 2] if tc >= 2 else 0})
                def _(t, tc=tc, b0=b0, b1=b1):
                    last = None
                    for half in range(2):
                        pp = b0 if half == 0 else b1
                        for dc in range(16):
                            buf = self.WS0 if dc < 8 else self.WS1
                            col = (dc % 8) * 1024 + half * 512
                            last = t.matmul(out=pp[:, :],
                                            lhsT=self.AB[:, dc * 1024 + tc * 128: dc * 1024 + tc * 128 + 128],
                                            rhs=buf[:, col:col + 512],
                                            start=(dc == 0), stop=(dc == 15), skip_group_check=True)
                    return last

                @self.step("s", after={"t": S.cnt["t"]})
                def _(s, tc=tc, b0=b0, b1=b1):
                    s.activation(out=self.AB[:, 16384 + tc * 1024: 16384 + tc * 1024 + 512],
                                 in_=b0[:, :], func=AF.Silu)
                    return s.activation(out=self.AB[:, 16384 + tc * 1024 + 512: 16384 + (tc + 1) * 1024],
                                        in_=b1[:, :], func=AF.Silu)
                sh[tc % 2] = S.cnt["s"]
            tw = 0
            for dc in range(16):
                for j in range(2):
                    buf = self.WS0 if dc < 8 else self.WS1
                    col = (dc % 8) * 1024 + j * 512
                    tw = self.dmaT("q", buf[:, col:col + 512],
                                   self.w1_out[os * 2048 + 1024 + j * 512: os * 2048 + 1024 + (j + 1) * 512,
                                               dc * 128:(dc + 1) * 128],
                                   after={"t": S.cnt["t"], "s": S.cnt["s"]} if (dc == 0 and j == 0) else None)
            vh = [tw, tw]
            for tc in range(NTC):
                b0 = self.P[0] if tc % 2 == 0 else self.P[2]
                b1 = self.P[1] if tc % 2 == 0 else self.P[3]

                @self.step("t", after={"q": tw, "v": vh[tc % 2] if tc >= 2 else 0})
                def _(t, tc=tc, b0=b0, b1=b1):
                    last = None
                    for half in range(2):
                        pp = b0 if half == 0 else b1
                        for dc in range(16):
                            buf = self.WS0 if dc < 8 else self.WS1
                            col = (dc % 8) * 1024 + half * 512
                            last = t.matmul(out=pp[:, :],
                                            lhsT=self.AB[:, dc * 1024 + tc * 128: dc * 1024 + tc * 128 + 128],
                                            rhs=buf[:, col:col + 512],
                                            start=(dc == 0), stop=(dc == 15), skip_group_check=True)
                    return last

                @self.step("v", after={"t": S.cnt["t"]})
                def _(v, tc=tc, b0=b0, b1=b1):
                    sg = self.AB[:, 16384 + tc * 1024: 16384 + (tc + 1) * 1024]
                    v.tensor_tensor(out=self.T1[:, 0:512], in0=b0[:, :],
                                    in1=sg[:, 0:512], op=AL.mult)
                    return v.tensor_tensor(out=self.T1[:, 512:1024], in0=b1[:, :],
                                           in1=sg[:, 512:1024], op=AL.mult)
                vh[tc % 2] = S.cnt["v"]
                self._emit_mxact(self.T1[:, 0:1024], 32, self.SBB[:, 0:1024],
                                 slice(0, 1024), after={"v": S.cnt["v"]})
                pairs = [(self.SBB[:, i * 128:(i + 1) * 128],
                          self.AB[:, 16384 + tc * 1024 + i * 128: 16384 + tc * 1024 + (i + 1) * 128])
                         for i in range(8)]
                self.trans_tiles(pairs, after={"v": S.cnt["v"]})
            tw = 0
            for ou in range(8):
                for dsl in range(4):
                    tw = self.dmaT("q", self.WS0[:, ou * 1024 + dsl * 512 - (ou // 2) * 0: ou * 1024 + dsl * 512 + 512]
                                   if False else self.WS0[:, (ou % 4) * 2048 + dsl * 512:(ou % 4) * 2048 + dsl * 512 + 512]
                                   if ou < 4 else self.WS1[:, (ou % 4) * 2048 + dsl * 512:(ou % 4) * 2048 + dsl * 512 + 512],
                                   self.w1b_out[dsl * 512:(dsl + 1) * 512,
                                                os * 1024 + ou * 128: os * 1024 + (ou + 1) * 128],
                                   after={"t": S.cnt["t"]} if (ou == 0 and dsl == 0) else None)
            for tc in range(NTC):
                @self.step("t", after={"q": tw} if tc == 0 else {"v": S.cnt["v"]})
                def _(t, tc=tc):
                    last = None
                    for dsl in range(4):
                        pp = self.P[2 + dsl]
                        for ou in range(8):
                            buf = self.WS0 if ou < 4 else self.WS1
                            col = (ou % 4) * 2048 + dsl * 512
                            last = t.matmul(out=pp[:, :],
                                            lhsT=self.AB[:, 16384 + tc * 1024 + ou * 128: 16384 + tc * 1024 + (ou + 1) * 128],
                                            rhs=buf[:, col:col + 512],
                                            start=(ou == 0), stop=(ou == 7))
                    return last

                @self.step("v", after={"t": S.cnt["t"]})
                def _(v, tc=tc, os=os):
                    last = None
                    for dsl in range(4):
                        dst = self.AF[:, tc * 2048 + dsl * 512: tc * 2048 + (dsl + 1) * 512]
                        if os == 0:
                            last = v.tensor_copy(out=dst, in_=self.P[2 + dsl][:, :])
                        else:
                            last = v.tensor_tensor(out=dst, in0=dst, in1=self.P[2 + dsl][:, :],
                                                   op=AL.add)
                    return last

        for tc in range(NTC):
            tq = self.dma("q", self.T0[:, 0:D], self.xstream[tc * 128:(tc + 1) * 128, :],
                          after={"v": S.cnt["v"]})

            @self.step("v", after={"q": tq})
            def _(v, tc=tc):
                return v.tensor_tensor(out=self.T1[:, 0:D], in0=self.AF[:, tc * 2048:(tc + 1) * 2048],
                                       in1=self.T0[:, 0:D], op=AL.add)

            @self.step("v", after={"v": S.cnt["v"]})
            def _(v):
                return v.tensor_copy(out=self.AF[:, 0:D], in_=self.T1[:, 0:D])
            self.ln_write(self.AF[:, 0:D], self.xstream[tc * 128:(tc + 1) * 128, :],
                          after={"v": S.cnt["v"]})

    def plan_l1_ffn(self, last_block=True):
        S = self.S
        start = S.all()
        tb = self.dma("q", self.BIASR[0:1, 0:2048],
                      self.brow_sp[0:1, 5 * 2048:6 * 2048], after=start)

        @self.step("t", after={"q": tb})
        def _(t):
            last = None
            for j in range(4):
                last = t.matmul(out=self.P[j][:, :], lhsT=self.ONEB[0:1, :],
                                rhs=self.BIASR[0:1, j * 512:(j + 1) * 512],
                                start=True, stop=True)
            return last

        @self.step("v", after={"t": S.cnt["t"]})
        def _(v):
            last = None
            for j in range(4):
                last = v.tensor_copy(out=self.WNBB[:, j * 512:(j + 1) * 512],
                                     in_=self.P[j][:, :])
            return last

        for tc in range(NTC):
            tq = self.dma("q", self.T0[:, 0:D], self.xstream[tc * 128:(tc + 1) * 128, :],
                          after={"v": S.cnt["v"]})

            @self.step("v", after={"q": tq})
            def _(v, tc=tc):
                v.tensor_tensor(out=self.T1[:, 0:D], in0=self.T0[:, 0:D],
                                in1=self.T0[:, 0:D], op=AL.mult)
                v.tensor_reduce(out=self.SM1[:, 0:1], in_=self.T1[:, 0:D], axis=AX, op=AL.add)
                v.drain()
                v.tensor_scalar(out=self.SM1[:, 0:1], in0=self.SM1[:, 0:1],
                                scalar1=1.0 / D, scalar2=1e-5, op0=AL.mult, op1=AL.add)
                return v.drain()

            @self.step("s", after={"v": S.cnt["v"]})
            def _(s):
                return s.activation(out=self.SM1[:, 1:2], in_=self.SM1[:, 0:1], func=AF.Sqrt)

            @self.step("v", after={"s": S.cnt["s"]})
            def _(v, tc=tc):
                v.reciprocal(out=self.SM1[:, 2:3], in_=self.SM1[:, 1:2])
                v.drain()
                v.tensor_scalar(out=self.T1[:, 0:D], in0=self.T0[:, 0:D],
                                scalar1=self.SM1[:, 2:3], scalar2=None, op0=AL.mult)
                v.tensor_tensor(out=self.T1[:, 0:D], in0=self.T1[:, 0:D],
                                in1=self.WNBB[:, :], op=AL.mult)
                v.tensor_reduce(out=self.SM1[:, 3:4], in_=self.T1[:, 0:D], axis=AX,
                                op=AL.max, apply_absolute_value=True)
                v.drain()
                v.tensor_scalar(out=self.SM1[:, 3:4], in0=self.SM1[:, 3:4],
                                scalar1=1e-5, scalar2=None, op0=AL.max)
                v.drain()
                v.reciprocal(out=self.SM1[:, 4:5], in_=self.SM1[:, 3:4])
                v.drain()
                v.tensor_scalar(out=self.SM1[:, 5:6], in0=self.SM1[:, 4:5],
                                scalar1=127.0, scalar2=None, op0=AL.mult)
                v.tensor_scalar(out=self.SM1[:, 6:7], in0=self.SM1[:, 3:4],
                                scalar1=1.0 / 127.0, scalar2=None, op0=AL.mult)
                v.drain()
                v.tensor_tensor(out=self.SM2[:, 16 + tc:17 + tc], in0=self.SM1[:, 6:7],
                                in1=self.SM3[:, 4:5], op=AL.mult)
                v.tensor_copy(out=self.SM2[:, 24 + tc:25 + tc], in_=self.SM1[:, 6:7])
                v.drain()
                v.tensor_scalar(out=self.T0[:, 0:D], in0=self.T1[:, 0:D],
                                scalar1=self.SM1[:, 5:6], scalar2=MAGIC,
                                op0=AL.mult, op1=AL.add)
                v.tensor_scalar(out=self.T0[:, 0:D], in0=self.T0[:, 0:D],
                                scalar1=MAGIC, scalar2=127.0, op0=AL.subtract, op1=AL.min)
                return v.tensor_scalar(out=self.SBB[:, 0:D], in0=self.T0[:, 0:D],
                                       scalar1=-128.0, scalar2=None, op0=AL.max)
            pairs = [(self.SBB[:, dc * 128:(dc + 1) * 128],
                      self.AB[:, dc * 1024 + tc * 128: dc * 1024 + tc * 128 + 128])
                     for dc in range(16)]
            self.trans_tiles(pairs, after={"v": S.cnt["v"]})

        @self.step("v", after={"v": S.cnt["v"]})
        def _(v):
            v.memset(self.SM2[:, 32:40], 0.0)
            return v.drain()
        for os in range(16):
            tw = 0
            for dc in range(16):
                tw = self.dmaT("q", self.WS0[:, dc * 512:(dc + 1) * 512],
                               self.w3_out[os * 512:(os + 1) * 512, dc * 128:(dc + 1) * 128],
                               after={"t": S.cnt["t"], "v": S.cnt["v"]} if dc == 0 else None)
                tw = self.dmaT("q", self.WS1[:, dc * 512:(dc + 1) * 512],
                               self.w3_out[HID1 + os * 512: HID1 + (os + 1) * 512,
                                           dc * 128:(dc + 1) * 128])
            vh = [tw, tw]
            for tc in range(NTC):
                pe = self.P[0] if tc % 2 == 0 else self.P[4]
                po = self.P[1] if tc % 2 == 0 else self.P[5]

                @self.step("t", after={"q": tw, "v": vh[tc % 2] if tc >= 2 else 0})
                def _(t, tc=tc, pe=pe, po=po):
                    last = None
                    for dc in range(16):
                        t.matmul(out=pe[:, :],
                                 lhsT=self.AB[:, dc * 1024 + tc * 128: dc * 1024 + tc * 128 + 128],
                                 rhs=self.WS0[:, dc * 512:(dc + 1) * 512],
                                 start=(dc == 0), stop=(dc == 15), skip_group_check=True)
                        last = t.matmul(out=po[:, :],
                                        lhsT=self.AB[:, dc * 1024 + tc * 128: dc * 1024 + tc * 128 + 128],
                                        rhs=self.WS1[:, dc * 512:(dc + 1) * 512],
                                        start=(dc == 0), stop=(dc == 15), skip_group_check=True)
                    return last

                @self.step("v", after={"t": S.cnt["t"]})
                def _(v, tc=tc, pe=pe, po=po):
                    v.tensor_scalar(out=self.T0[:, 0:512], in0=pe[:, :],
                                    scalar1=self.SM2[:, 16 + tc:17 + tc], scalar2=None,
                                    op0=AL.mult)
                    return v.tensor_scalar(out=self.T1[:, 0:512], in0=po[:, :],
                                           scalar1=self.SM2[:, 16 + tc:17 + tc], scalar2=None,
                                           op0=AL.mult)
                vh[tc % 2] = S.cnt["v"]

                @self.step("s", after={"v": S.cnt["v"]})
                def _(s):
                    return s.activation(out=self.T1[:, 512:1024], in_=self.T1[:, 0:512],
                                        func=AF.Silu)

                @self.step("v", after={"s": S.cnt["s"]})
                def _(v, tc=tc):
                    sb = self.SBB[:, (tc % 2) * 512:(tc % 2) * 512 + 512]
                    v.tensor_tensor(out=sb, in0=self.T0[:, 0:512],
                                    in1=self.T1[:, 512:1024], op=AL.mult)
                    v.drain()
                    v.tensor_reduce(out=self.SM1[:, 7:8], in_=sb, axis=AX,
                                    op=AL.max, apply_absolute_value=True)
                    v.drain()
                    return v.tensor_tensor(out=self.SM2[:, 32 + tc:33 + tc],
                                           in0=self.SM2[:, 32 + tc:33 + tc],
                                           in1=self.SM1[:, 7:8], op=AL.max)
                self.dma("gd", self.u_spill[tc * 128:(tc + 1) * 128, os * 512:(os + 1) * 512],
                         self.SBB[:, (tc % 2) * 512:(tc % 2) * 512 + 512],
                         after={"v": S.cnt["v"]})

        @self.step("v", after={"v": S.cnt["v"]})
        def _(v):
            v.tensor_scalar(out=self.SM2[:, 32:40], in0=self.SM2[:, 32:40],
                            scalar1=1e-5, scalar2=None, op0=AL.max)
            v.drain()
            v.reciprocal(out=self.SM2[:, 40:48], in_=self.SM2[:, 32:40])
            v.drain()
            v.tensor_scalar(out=self.SM2[:, 40:48], in0=self.SM2[:, 40:48],
                            scalar1=127.0, scalar2=None, op0=AL.mult)
            v.tensor_scalar(out=self.SM2[:, 48:56], in0=self.SM2[:, 32:40],
                            scalar1=1.0 / 127.0, scalar2=None, op0=AL.mult)
            v.drain()
            last = None
            for tc in range(NTC):
                last = v.tensor_tensor(out=self.SM2[:, 56 + tc:57 + tc],
                                       in0=self.SM2[:, 48 + tc:49 + tc],
                                       in1=self.SM3[:, 5:6], op=AL.mult)
            return last

        for og in range(8):
            tw = 0
            for ou in range(8):
                for dsl in range(4):
                    buf = self.WS0 if ou < 4 else self.WS1
                    col = (ou % 4) * 2048 + dsl * 512
                    tw = self.dmaT("q", buf[:, col:col + 512],
                                   self.w3b_out[dsl * 512:(dsl + 1) * 512,
                                                og * 1024 + ou * 128: og * 1024 + (ou + 1) * 128],
                                   after={"t": S.cnt["t"]} if (ou == 0 and dsl == 0) else None)
            for tc in range(NTC):
                tq = self.dma("q", self.SBB[:, 0:1024],
                              self.u_spill[tc * 128:(tc + 1) * 128, og * 1024:(og + 1) * 1024],
                              after={"v": S.cnt["v"]})

                @self.step("v", after={"q": tq})
                def _(v, tc=tc):
                    v.tensor_scalar(out=self.T0[:, 0:1024], in0=self.SBB[:, 0:1024],
                                    scalar1=self.SM2[:, 40 + tc:41 + tc], scalar2=MAGIC,
                                    op0=AL.mult, op1=AL.add)
                    v.tensor_scalar(out=self.T0[:, 0:1024], in0=self.T0[:, 0:1024],
                                    scalar1=MAGIC, scalar2=127.0, op0=AL.subtract, op1=AL.min)
                    return v.tensor_scalar(out=self.SBB[:, 1024:2048], in0=self.T0[:, 0:1024],
                                           scalar1=-128.0, scalar2=None, op0=AL.max)
                pairs = [(self.SBB[:, 1024 + i * 128: 1024 + (i + 1) * 128],
                          self.AB[:, 16384 + i * 128: 16384 + (i + 1) * 128])
                         for i in range(8)]
                self.trans_tiles(pairs, after={"v": S.cnt["v"]})

                @self.step("t", after={"v": S.cnt["v"]})
                def _(t, tc=tc):
                    last = None
                    for dsl in range(4):
                        for ou in range(8):
                            buf = self.WS0 if ou < 4 else self.WS1
                            col = (ou % 4) * 2048 + dsl * 512
                            last = t.matmul(out=self.P[2 + dsl][:, :],
                                            lhsT=self.AB[:, 16384 + ou * 128: 16384 + (ou + 1) * 128],
                                            rhs=buf[:, col:col + 512],
                                            start=(ou == 0), stop=(ou == 7))
                    return last

                @self.step("v", after={"t": S.cnt["t"]})
                def _(v, tc=tc, og=og):
                    last = None
                    for dsl in range(4):
                        dst = self.AF[:, tc * 2048 + dsl * 512: tc * 2048 + (dsl + 1) * 512]
                        if og == 0:
                            last = v.tensor_copy(out=dst, in_=self.P[2 + dsl][:, :])
                        else:
                            last = v.tensor_tensor(out=dst, in0=dst,
                                                   in1=self.P[2 + dsl][:, :], op=AL.add)
                    return last

        for tc in range(NTC):
            tq = self.dma("q", self.T0[:, 0:D], self.xstream[tc * 128:(tc + 1) * 128, :],
                          after={"v": S.cnt["v"]})

            @self.step("v", after={"q": tq})
            def _(v, tc=tc):
                v.tensor_scalar(out=self.T1[:, 0:D], in0=self.AF[:, tc * 2048:(tc + 1) * 2048],
                                scalar1=self.SM2[:, 56 + tc:57 + tc], scalar2=None,
                                op0=AL.mult)
                v.drain()
                v.scalar_tensor_tensor(out=self.T1[:, 0:D], in0=self.T0[:, 0:D],
                                       scalar=2.0, in1=self.T1[:, 0:D],
                                       op0=AL.mult, op1=AL.add)
                v.drain()
                return v.tensor_copy(out=self.AF[:, 0:D], in_=self.T1[:, 0:D])
            dst = self.y_out[tc * 128:(tc + 1) * 128, :] if last_block \
                else self.xstream[tc * 128:(tc + 1) * 128, :]
            self.ln_write(self.AF[:, 0:D], dst, after={"v": S.cnt["v"]})

    def plan_output_wait(self):
        S = self.S

        @self.step("g", after=S.all())
        def _(g):
            return g.nop()

    def finalize(self):
        nc, S = self.nc, self.S
        with nc.Block() as block:
            @block.sync
            def _(sync):
                S.emit_for("sync", sync)

            @block.tensor
            def _(t):
                S.emit_for("t", t)

            @block.vector
            def _(v):
                S.emit_for("v", v)

            @block.scalar
            def _(s):
                S.emit_for("s", s)

            @block.gpsimd
            def _(g):
                S.emit_for("g", g)
        self.stack.close()
        return nc


    def dmaT_dyn(self, qk, out, src_tensor, mult, roff, nrows, cols, after=None):
        S = self.S
        ek = {"q": "sync", "gd": "g", "sd": "s"}[qk]
        S.cnt[qk] += 16
        sem = S.sems[qk]

        def fn(eng, out=out):
            off = self._dyn_off(eng, mult)
            nr = src_tensor.shape[0]
            base = src_tensor[roff:nr, cols] if roff else src_tensor[:, cols]
            eng.dma_start_transpose(out=out, in_=base[bass.ds(off, nrows), :]).then_inc(sem, 16)
        S.steps.append((ek, fn, {k: v for k, v in (after or {}).items() if v > 0}))
        return S.cnt[qk]

    def w2row(self, m, kc):
        return (kc // 2) * 1280 + m * 256 + (kc % 2) * 128

    def plan_l1_attn(self):
        S = self.S
        start = S.all()
        for tc in range(NTC):
            tq = self.dma("q", self.T0[:, 0:D], self.xstream[tc * 128:(tc + 1) * 128, :],
                          after=start if tc == 0 else {"t": S.cnt["t"]})

            @self.step("v", after={"q": tq, "v": S.cnt["v"]})
            def _(v):
                return v.tensor_copy(out=self.SBB[:, 0:D], in_=self.T0[:, 0:D])
            pairs = [(self.SBB[:, dc * 128:(dc + 1) * 128],
                      self.AB[:, dc * 1024 + tc * 128: dc * 1024 + tc * 128 + 128])
                     for dc in range(16)]
            self.trans_tiles(pairs, after={"v": S.cnt["v"]})

        for m in range(4):
            for dhalf in range(2):
                tw = 0
                for dsl_in in range(2):
                    dsl = dhalf * 2 + dsl_in
                    buf = self.WS0 if dsl_in == 0 else self.WS1
                    for kc in range(16):
                        tw = self.dma("q", buf[:, kc * 512:(kc + 1) * 512],
                                      self.w2_out[self.w2row(m, kc): self.w2row(m, kc) + 128,
                                                  dsl * 512:(dsl + 1) * 512],
                                      after={"t": S.cnt["t"], "v": S.cnt["v"], "s": S.cnt["s"]}
                                      if (dsl_in == 0 and kc == 0) else None)
                tb = self.dma("q", self.BIASR[0:1, 0:1024],
                              self.brow_sp[0:1, m * 2048 + dhalf * 1024: m * 2048 + (dhalf + 1) * 1024],
                              after={"t": S.cnt["t"]})
                phv = [0, 0]
                phs = [0, 0]
                for tc in range(NTC):
                    bk0 = self.P[0] if tc % 2 == 0 else self.P[4]
                    bk1 = self.P[1] if tc % 2 == 0 else self.P[5]
                    sbo = (tc % 2) * 1024

                    @self.step("t", after={"q": max(tw, tb),
                                           "v": phv[tc % 2] if tc >= 2 else 0,
                                           "s": phs[tc % 2] if tc >= 2 else 0})
                    def _(t, m=m, tc=tc, dhalf=dhalf, bk0=bk0, bk1=bk1):
                        last = None
                        for dsl_in in range(2):
                            buf = self.WS0 if dsl_in == 0 else self.WS1
                            pp = bk0 if dsl_in == 0 else bk1
                            for kc in range(16):
                                t.matmul(out=pp[:, :],
                                         lhsT=self.AB[:, kc * 1024 + tc * 128: kc * 1024 + tc * 128 + 128],
                                         rhs=buf[:, kc * 512:(kc + 1) * 512],
                                         start=(kc == 0), stop=False, skip_group_check=True)
                            last = t.matmul(out=pp[:, :], lhsT=self.ONEB[0:1, :],
                                            rhs=self.BIASR[0:1, dsl_in * 512:(dsl_in + 1) * 512],
                                            start=False, stop=True, skip_group_check=True)
                        return last
                    if m <= 1:
                        tq = self.dma("q", self.ROPE[:, 0:64],
                                      self.c["rsin"][tc * 128:(tc + 1) * 128, :],
                                      after={"v": S.cnt["v"]})
                        tq = self.dma("q", self.ROPE[:, 64:128],
                                      self.c["rcos"][tc * 128:(tc + 1) * 128, :])

                        @self.step("v", after={"t": S.cnt["t"], "q": tq})
                        def _(v, bk0=bk0, bk1=bk1):
                            v.tensor_copy(out=self.T0[:, 0:512], in_=bk0[:, :])
                            v.tensor_copy(out=self.T0[:, 512:1024], in_=bk1[:, :])
                            v.drain()
                            r3 = lambda ap: ap.rearrange("p (h x) -> p h x", x=128)
                            x1 = r3(self.T0[:, 0:1024])[:, :, 0:64]
                            x2 = r3(self.T0[:, 0:1024])[:, :, 64:128]
                            o1 = r3(self.T1[:, 0:1024])[:, :, 0:64]
                            o2 = r3(self.T1[:, 0:1024])[:, :, 64:128]
                            a1 = r3(self.AF[:, 0:1024])[:, :, 0:64]
                            a2 = r3(self.AF[:, 0:1024])[:, :, 64:128]
                            sin = self.ROPE[:, 0:64].rearrange("p (o x) -> p o x", o=1).broadcast_to((128, 8, 64))
                            cos = self.ROPE[:, 64:128].rearrange("p (o x) -> p o x", o=1).broadcast_to((128, 8, 64))
                            v.tensor_tensor(out=o1, in0=x1, in1=cos, op=AL.mult)
                            v.tensor_tensor(out=o2, in0=x2, in1=cos, op=AL.mult)
                            v.tensor_tensor(out=a1, in0=x2, in1=sin, op=AL.mult)
                            v.tensor_tensor(out=a2, in0=x1, in1=sin, op=AL.mult)
                            v.drain()
                            v.tensor_tensor(out=o1, in0=o1, in1=a1, op=AL.subtract)
                            v.tensor_tensor(out=o2, in0=o2, in1=a2, op=AL.add)
                            v.drain()
                            v.tensor_scalar(out=self.T0[:, 0:1024], in0=self.T1[:, 0:1024],
                                            scalar1=0.0, scalar2=None, op0=AL.min)
                            return v.tensor_scalar(out=self.AF[:, 0:1024], in0=self.T1[:, 0:1024],
                                                   scalar1=0.0, scalar2=None, op0=AL.max)

                        @self.step("s", after={"v": S.cnt["v"]})
                        def _(s):
                            return s.activation(out=self.T0[:, 0:1024], in_=self.T0[:, 0:1024],
                                                func=AF.Exp)

                        @self.step("v", after={"s": S.cnt["s"]})
                        def _(v, sbo=sbo):
                            v.tensor_tensor(out=self.T0[:, 0:1024], in0=self.T0[:, 0:1024],
                                            in1=self.AF[:, 0:1024], op=AL.add)
                            v.drain()
                            return v.tensor_copy(out=self.SBB[:, sbo:sbo + 1024],
                                                 in_=self.T0[:, 0:1024])
                    elif m == 2:
                        @self.step("v", after={"t": S.cnt["t"]})
                        def _(v, bk0=bk0, bk1=bk1, sbo=sbo):
                            v.tensor_copy(out=self.SBB[:, sbo:sbo + 512], in_=bk0[:, :])
                            return v.tensor_copy(out=self.SBB[:, sbo + 512:sbo + 1024], in_=bk1[:, :])
                    else:
                        @self.step("s", after={"t": S.cnt["t"], "v": S.cnt["v"]})
                        def _(s, bk0=bk0, bk1=bk1, sbo=sbo):
                            s.activation(out=self.SBB[:, sbo:sbo + 512], in_=bk0[:, :],
                                         func=AF.Sigmoid)
                            return s.activation(out=self.SBB[:, sbo + 512:sbo + 1024], in_=bk1[:, :],
                                                func=AF.Sigmoid)
                    phv[tc % 2] = S.cnt["v"]
                    phs[tc % 2] = S.cnt["s"]
                    self.dma("gd", self.qkvg_sp[m * TL + tc * 128: m * TL + (tc + 1) * 128,
                                                dhalf * 1024:(dhalf + 1) * 1024],
                             self.SBB[:, sbo:sbo + 1024], after={"v": S.cnt["v"], "s": S.cnt["s"]})

        tgd = S.cnt["gd"]
        for c in range(NCHKL):
            tq = self.dma("q", self.SBB[0:64, 0:D],
                          self.qkvg_sp[TL + c * 64: TL + (c + 1) * 64, :],
                          after={"gd": tgd} if c == 0 else {"t": S.cnt["t"]})
            tq = self.dma("q", self.WS0[0:64, 0:D],
                          self.qkvg_sp[2 * TL + c * 64: 2 * TL + (c + 1) * 64, :])

            @self.step("t", after={"q": tq})
            def _(t, c=c):
                last = None
                for h in range(H):
                    t.matmul(out=self.P[h // 4][:, (h % 4) * 128:(h % 4) * 128 + 128],
                             lhsT=self.SBB[0:64, h * 128:(h + 1) * 128],
                             rhs=self.WS0[0:64, h * 128:(h + 1) * 128],
                             start=(c == 0 and h % 4 == 0), stop=(c == NCHKL - 1),
                             skip_group_check=True)
                    last = t.matmul(out=self.P[4][:, h:h + 1],
                                    lhsT=self.SBB[0:64, h * 128:(h + 1) * 128],
                                    rhs=self.ONEB[0:64, 0:1],
                                    start=(c == 0 and h == 0), stop=(c == NCHKL - 1),
                                    skip_group_check=True)
                return last

        @self.step("v", after={"t": S.cnt["t"]})
        def _(v):
            last = None
            for h in range(H):
                last = v.tensor_copy(out=self.AF[:, h * 136: h * 136 + 128],
                                     in_=self.P[h // 4][:, (h % 4) * 128:(h % 4) * 128 + 128])
                last = v.tensor_copy(out=self.AF[:, h * 136 + 128: h * 136 + 129],
                                     in_=self.P[4][:, h:h + 1])
            return last
        tv = S.cnt["v"]
        self.dma("gd", self.s1_in[:, :].rearrange("(h p) c -> p h c", p=128),
                 self.AF[:, 0:H * 136].rearrange("p (h c) -> p h c", c=136), after={"v": tv})
        t_s1 = self.cc("AllGather", AL.bypass, [self.s1_in], [self.s1_out],
                       after={"gd": S.cnt["gd"]})

        @self.step("v", after={"cc": t_s1})
        def _(v):
            v.memset(self.AF[:, 0:H * 136], 0.0)
            return v.drain()
        for b in range(NCORE):
            tq = self.dma("q", self.T0[:, 0:1088].rearrange("p (h c) -> p h c", c=136),
                          self.s1_out[b * 2048: b * 2048 + 1024, 0:136]
                          .rearrange("(h p) c -> p h c", p=128),
                          after={"v": S.cnt["v"]})
            tq = self.dma("q", self.T1[:, 0:1088].rearrange("p (h c) -> p h c", c=136),
                          self.s1_out[b * 2048 + 1024: b * 2048 + 2048, 0:136]
                          .rearrange("(h p) c -> p h c", p=128))

            @self.step("v", after={"q": tq})
            def _(v, b=b):
                v.scalar_tensor_tensor(out=self.AF[:, 0:8 * 136], in0=self.T0[:, 0:1088],
                                       scalar=self.SEL[:, b:b + 1], in1=self.AF[:, 0:8 * 136],
                                       op0=AL.mult, op1=AL.add)
                v.scalar_tensor_tensor(out=self.AF[:, 8 * 136:16 * 136], in0=self.T1[:, 0:1088],
                                       scalar=self.SEL[:, b:b + 1], in1=self.AF[:, 8 * 136:16 * 136],
                                       op0=AL.mult, op1=AL.add)
                return v.drain()

        for c in range(NCHKL):
            tq = self.dma("q", self.WS0[0:64, 0:D],
                          self.qkvg_sp[0 * TL + c * 64: 0 * TL + (c + 1) * 64, :],
                          after={"v": S.cnt["v"], "t": S.cnt["t"]})
            tq = self.dma("q", self.WS0[0:64, 2048:2048 + D],
                          self.qkvg_sp[1 * TL + c * 64: 1 * TL + (c + 1) * 64, :])
            tq = self.dma("q", self.WS0[0:64, 4096:4096 + D],
                          self.qkvg_sp[2 * TL + c * 64: 2 * TL + (c + 1) * 64, :])
            tq = self.dma("q", self.WS0[0:64, 6144:6144 + D],
                          self.qkvg_sp[3 * TL + c * 64: 3 * TL + (c + 1) * 64, :])

            @self.step("t", after={"q": tq})
            def _(t):
                last = None
                for h in range(H):
                    t.matmul(out=self.P[h // 4][:, (h % 4) * 128:(h % 4) * 128 + 128],
                             lhsT=self.WS0[0:64, 2048 + h * 128: 2048 + (h + 1) * 128],
                             rhs=self.WS0[0:64, 4096 + h * 128: 4096 + (h + 1) * 128],
                             start=(h % 4 == 0), stop=(h % 4 == 3), skip_group_check=True)
                    last = t.matmul(out=self.P[4][:, h:h + 1],
                                    lhsT=self.WS0[0:64, 2048 + h * 128: 2048 + (h + 1) * 128],
                                    rhs=self.ONEB[0:64, 0:1],
                                    start=(h == 0), stop=(h == H - 1), skip_group_check=True)
                return last

            @self.step("v", after={"t": S.cnt["t"]})
            def _(v):
                acc3 = self.AF[:, 0:16 * 136].rearrange("p (h c) -> p h c", c=136)
                for bk in range(4):
                    acc = acc3[:, bk * 4:(bk + 1) * 4, 0:128]
                    pin = self.P[bk][:, :].rearrange("p (h c) -> p h c", c=128)
                    v.tensor_tensor(out=acc, in0=acc, in1=pin, op=AL.add)
                accz = acc3[:, :, 128:129]
                pz = self.P[4][:, 0:16].rearrange("p (h c) -> p h c", c=1)
                v.tensor_tensor(out=accz, in0=accz, in1=pz, op=AL.add)
                v.drain()
                return v.tensor_copy(out=self.AB[:, 0:16 * 136], in_=self.AF[:, 0:16 * 136])

            @self.step("t", after={"v": S.cnt["v"]})
            def _(t):
                last = None
                for h in range(H):
                    pp = self.PT if h < 8 else self.PT2
                    last = t.transpose(out=pp[:, (h % 8) * 64:(h % 8) * 64 + 64],
                                       in_=self.WS0[0:64, h * 128:(h + 1) * 128],
                                       identity=self.EYEB[0:64, 0:64])
                return last

            @self.step("v", after={"t": S.cnt["t"]})
            def _(v):
                v.tensor_copy(out=self.SBB[:, 0:512], in_=self.PT[:, :])
                return v.tensor_copy(out=self.SBB[:, 512:1024], in_=self.PT2[:, :])

            @self.step("t", after={"v": S.cnt["v"]})
            def _(t):
                last = None
                for h in range(H):
                    t.matmul(out=self.P[h // 4][0:64, (h % 4) * 128:(h % 4) * 128 + 128],
                             lhsT=self.SBB[:, h * 64:(h + 1) * 64],
                             rhs=self.AB[:, h * 136: h * 136 + 128],
                             start=(h % 4 == 0), stop=(h % 4 == 3), skip_group_check=True)
                    last = t.matmul(out=self.P[5][0:64, h:h + 1],
                                    lhsT=self.SBB[:, h * 64:(h + 1) * 64],
                                    rhs=self.AB[:, h * 136 + 128: h * 136 + 129],
                                    start=(h == 0), stop=(h == H - 1), skip_group_check=True)
                return last

            @self.step("v", after={"t": S.cnt["t"]})
            def _(v, c=c):
                v.tensor_scalar(out=self.SM0[0:64, 0:16], in0=self.P[5][0:64, 0:16],
                                scalar1=1e-6, scalar2=None, op0=AL.add)
                v.drain()
                v.reciprocal(out=self.SM0[0:64, 16:32], in_=self.SM0[0:64, 0:16])
                v.drain()
                last = None
                for h in range(H):
                    last = v.scalar_tensor_tensor(
                        out=self.SBB[0:64, h * 128:(h + 1) * 128],
                        in0=self.P[h // 4][0:64, (h % 4) * 128:(h % 4) * 128 + 128],
                        scalar=self.SM0[0:64, 16 + h:17 + h],
                        in1=self.WS0[0:64, 6144 + h * 128: 6144 + (h + 1) * 128],
                        op0=AL.mult, op1=AL.mult)
                return last
            self.dma("gd", self.qkvg_sp[0 * TL + c * 64: 0 * TL + (c + 1) * 64, :],
                     self.SBB[0:64, 0:D], after={"v": S.cnt["v"]})

        tgd = S.cnt["gd"]
        for tc in range(NTC):
            tq = self.dma("q", self.SBB[:, 0:D],
                          self.qkvg_sp[tc * 128:(tc + 1) * 128, :],
                          after={"gd": tgd, "v": S.cnt["v"], "t": S.cnt["t"]})

            @self.step("v", after={"q": tq})
            def _(v):
                return v.engine_nop()
            pairs = [(self.SBB[:, dc * 128:(dc + 1) * 128],
                      self.AB[:, dc * 1024 + tc * 128: dc * 1024 + tc * 128 + 128])
                     for dc in range(16)]
            self.trans_tiles(pairs, after={"v": S.cnt["v"]})
        for dsl in range(4):
            tw = 0
            for kc in range(16):
                tw = self.dma("q", self.WS1[:, kc * 512:(kc + 1) * 512],
                              self.w2_out[self.w2row(4, kc): self.w2row(4, kc) + 128,
                                          dsl * 512:(dsl + 1) * 512],
                              after={"t": S.cnt["t"]} if kc == 0 else None)
            tb = self.dma("q", self.BIASR[0:1, 0:512],
                          self.brow_sp[0:1, 4 * 2048 + dsl * 512: 4 * 2048 + (dsl + 1) * 512],
                          after={"t": S.cnt["t"]})
            for tc in range(NTC):
                @self.step("t", after={"q": max(tw, tb)} if tc == 0 else {"v": S.cnt["v"]})
                def _(t, tc=tc, dsl=dsl):
                    for kc in range(16):
                        t.matmul(out=self.P[tc % 2][:, :],
                                 lhsT=self.AB[:, kc * 1024 + tc * 128: kc * 1024 + tc * 128 + 128],
                                 rhs=self.WS1[:, kc * 512:(kc + 1) * 512],
                                 start=(kc == 0), stop=False, skip_group_check=True)
                    return t.matmul(out=self.P[tc % 2][:, :], lhsT=self.ONEB[0:1, :],
                                    rhs=self.BIASR[0:1, 0:512],
                                    start=False, stop=True, skip_group_check=True)

                @self.step("v", after={"t": S.cnt["t"]})
                def _(v, tc=tc, dsl=dsl):
                    return v.tensor_copy(out=self.AF[:, tc * 2048 + dsl * 512: tc * 2048 + (dsl + 1) * 512],
                                         in_=self.P[tc % 2][:, :])
        for tc in range(NTC):
            tq = self.dma("q", self.T0[:, 0:D], self.xstream[tc * 128:(tc + 1) * 128, :],
                          after={"v": S.cnt["v"]})

            @self.step("v", after={"q": tq})
            def _(v, tc=tc):
                v.tensor_tensor(out=self.T1[:, 0:D], in0=self.AF[:, tc * 2048:(tc + 1) * 2048],
                                in1=self.T0[:, 0:D], op=AL.add)
                v.drain()
                return v.tensor_copy(out=self.AF[:, 0:D], in_=self.T1[:, 0:D])
            self.ln_write(self.AF[:, 0:D], self.xstream[tc * 128:(tc + 1) * 128, :],
                          after={"v": S.cnt["v"]})

    def w0row(self, m, kc):
        return (kc // 2) * 1024 + m * 256 + (kc % 2) * 128

    def plan_l0_attn(self):
        S = self.S
        start = S.all()
        self.olin_sp = self.mk("olin_sp", (HL * 128, T), BF16)
        for tc in range(NTC):
            tq = self.dma("q", self.T0[:, 0:D], self.xstream[tc * 128:(tc + 1) * 128, :],
                          after=start if tc == 0 else {"t": S.cnt["t"]})

            @self.step("v", after={"q": tq, "v": S.cnt["v"]})
            def _(v):
                return v.tensor_copy(out=self.SBB[:, 0:D], in_=self.T0[:, 0:D])
            pairs = [(self.SBB[:, dc * 128:(dc + 1) * 128],
                      self.AB[:, dc * 1024 + tc * 128: dc * 1024 + tc * 128 + 128])
                     for dc in range(16)]
            self.trans_tiles(pairs, after={"v": S.cnt["v"]})
        self._l0p_vh = [0, 0]
        for m in range(3):
            wn = ["l0_wq", "l0_wk", "l0_wv"][m]
            for dc in range(16):
                tw = 0
                for kc in range(16):
                    tw = self.dma("q", self.WS0[:, kc * 128:(kc + 1) * 128],
                                  self.w0_out[self.w0row(m, kc): self.w0row(m, kc) + 128,
                                              dc * 128:(dc + 1) * 128],
                                  after={"t": S.cnt["t"], "cc": self.t_w0} if kc == 0 else None)

                rot = dc % 2
                b0 = self.P[0] if rot == 0 else self.P[2]
                b1 = self.P[1] if rot == 0 else self.P[3]
                sbo = rot * 1024

                @self.step("t", after={"q": tw, "v": self._l0p_vh[rot] if dc >= 2 else 0})
                def _(t, b0=b0, b1=b1):
                    last = None
                    for half in range(2):
                        pp = b0 if half == 0 else b1
                        for kc in range(16):
                            last = t.matmul(out=pp[:, :],
                                            lhsT=self.WS0[:, kc * 128:(kc + 1) * 128],
                                            rhs=self.AB[:, kc * 1024 + half * 512: kc * 1024 + (half + 1) * 512],
                                            start=(kc == 0), stop=(kc == 15),
                                            skip_group_check=True)
                    return last

                @self.step("v", after={"t": S.cnt["t"]})
                def _(v, b0=b0, b1=b1, sbo=sbo):
                    v.tensor_copy(out=self.SBB[:, sbo:sbo + 512], in_=b0[:, :])
                    return v.tensor_copy(out=self.SBB[:, sbo + 512:sbo + 1024], in_=b1[:, :])
                self._l0p_vh[rot] = S.cnt["v"]
                self.dma("gd", self.a0_in[m * 2048 + dc * 128: m * 2048 + (dc + 1) * 128, :],
                         self.SBB[:, sbo:sbo + 1024], after={"v": S.cnt["v"]})
        t_a0 = self.cc("AllGather", AL.bypass, [self.a0_in], [self.a0_out],
                       after={"gd": S.cnt["gd"]})

        for hh in range(HL):
            self._l0_head(hh, t_a0)

        t_a1 = self.cc("AllGather", AL.bypass, [self.a1_in], [self.a1_out],
                       after={"gd": S.cnt["gd"]})

        for dc in range(16):
            self.dma_dyn("q", self.AB[:, dc * 1024:(dc + 1) * 1024], None,
                         None, 0, None, after={"cc": t_a1, "t": S.cnt["t"], "v": S.cnt["v"]}) \
                if False else None
        Sq = self.S
        Sq.cnt["q"] += 16
        semq = Sq.sems["q"]

        def a1fn(eng, sem=semq):
            off = self._dyn_off(eng, 1)
            view = self.a1_out.rearrange("d (blk t) -> blk d t", t=TL)
            eng.dma_start(out=self.a1st[:, :, :],
                          in_=view[bass.ds(off, 1), :, :]).then_inc(sem, 16)
        Sq.steps.append(("sync", a1fn,
                         {k: v for k, v in dict(cc=t_a1, t=S.cnt["t"], v=S.cnt["v"]).items() if v > 0}))
        t_a1st = Sq.cnt["q"]
        for dc in range(16):
            self.dma("q", self.AB[:, dc * 1024:(dc + 1) * 1024],
                     self.a1st[0, dc * 128:(dc + 1) * 128, :],
                     after={"q": t_a1st} if dc == 0 else None)
        t_ot = S.cnt["q"]
        for dsl in range(4):
            tw = 0
            for kc in range(16):
                tw = self.dma("q", self.WS1[:, kc * 512:(kc + 1) * 512],
                              self.w0_out[self.w0row(3, kc): self.w0row(3, kc) + 128,
                                          dsl * 512:(dsl + 1) * 512],
                              after={"t": S.cnt["t"], "q": t_ot} if kc == 0 else None)
            for tc in range(NTC):
                @self.step("t", after={"q": tw} if tc == 0 else {"v": S.cnt["v"]})
                def _(t, tc=tc):
                    last = None
                    for kc in range(16):
                        last = t.matmul(out=self.P[tc % 2][:, :],
                                        lhsT=self.AB[:, kc * 1024 + tc * 128: kc * 1024 + tc * 128 + 128],
                                        rhs=self.WS1[:, kc * 512:(kc + 1) * 512],
                                        start=(kc == 0), stop=(kc == 15),
                                        skip_group_check=True)
                    return last

                @self.step("v", after={"t": S.cnt["t"]})
                def _(v, tc=tc, dsl=dsl):
                    return v.tensor_copy(out=self.AF[:, tc * 2048 + dsl * 512: tc * 2048 + (dsl + 1) * 512],
                                         in_=self.P[tc % 2][:, :])
        for tc in range(NTC):
            tq = self.dma("q", self.T0[:, 0:D], self.xstream[tc * 128:(tc + 1) * 128, :],
                          after={"v": S.cnt["v"]})

            @self.step("v", after={"q": tq})
            def _(v, tc=tc):
                v.tensor_tensor(out=self.T1[:, 0:D], in0=self.AF[:, tc * 2048:(tc + 1) * 2048],
                                in1=self.T0[:, 0:D], op=AL.add)
                v.drain()
                return v.tensor_copy(out=self.AF[:, 0:D], in_=self.T1[:, 0:D])
            self.ln_write(self.AF[:, 0:D], self.xstream[tc * 128:(tc + 1) * 128, :],
                          after={"v": S.cnt["v"]})

    def _l0_head(self, hh, t_a0):
        S = self.S
        Sq = self.S
        Sq.cnt["q"] += 16
        semq = Sq.sems["q"]

        def stfn(eng, hh=hh, sem=semq):
            off = self._dyn_off(eng, 256)
            view = self.a0_out.rearrange("(b m r) t -> r b m t", m=3, r=2048)
            base = view[hh * 128: 2048, :, :, :]
            dst = self.hstage[:, :].rearrange("r (b m t) -> r b m t", m=3, t=TL)
            eng.dma_start(out=dst, in_=base[bass.ds(off, 128), :, :, :]).then_inc(sem, 16)
        Sq.steps.append(("sync", stfn,
                         {k: v for k, v in dict(cc=t_a0, t=S.cnt["t"], v=S.cnt["v"],
                                                s=S.cnt["s"], gd=S.cnt["gd"]).items() if v > 0}))
        t_hst = Sq.cnt["q"]
        hs = self.hstage
        tq = 0
        for b in range(NCORE):
            tq = self.dma("q", self.AB[:, 0 + b * 1024: 0 + (b + 1) * 1024],
                          hs[:, b * 3072 + 1024: b * 3072 + 2048],
                          after={"q": t_hst} if b == 0 else None)
            tq = self.dma("q", self.AB[:, 8192 + b * 1024: 8192 + (b + 1) * 1024],
                          hs[:, b * 3072 + 0: b * 3072 + 1024])
            for sub in range(8):
                jt = b * 8 + sub
                tq = self.dmaT("q", self.AB[:, 16384 + jt * 128: 16384 + (jt + 1) * 128],
                               hs[:, b * 3072 + 2048 + sub * 128: b * 3072 + 2048 + (sub + 1) * 128])
        t_hload = tq

        @self.step("v", after={"q": t_hload})
        def _(v):
            v.memset(self.AF[:, 0:128], 0.0)
            v.memset(self.SBB[:, 1024:1152], 0.0)
            return v.drain()

        kh = self.AB[:, 0:8192]
        qh = self.AB[:, 8192:16384]
        W1 = self.WS1
        KSF, QSF, KBF, BT, KST, VB, KBT = 0, 512, 1024, 1536, 2560, 3584, 4608
        MOFF, MTOFF, XOFF, ZOFF, WOFF = 5632, 6144, 6656, 7168, 7680
        UOFF, WTN, ATN = BT, KBF, MTOFF
        for b in range(16):
            t0c = b * 512
            @self.step("s", after={"v": S.cnt["v"], "t": S.cnt["t"]})
            def _(s, t0c=t0c):
                return s.activation(out=self.T0[:, 0:512], in_=kh[:, t0c:t0c + 512], func=AF.Exp)

            @self.step("t", after={"s": S.cnt["s"]})
            def _(t):
                return t.matmul(out=self.P[2][0:1, :], lhsT=self.ONEF[:, 0:1],
                                rhs=self.T0[:, 0:512], start=True, stop=True)

            @self.step("v", after={"t": S.cnt["t"]})
            def _(v):
                v.reciprocal(out=self.ROWV[0:1, 0:512], in_=self.P[2][0:1, :])
                return v.drain()

            @self.step("t", after={"v": S.cnt["v"]})
            def _(t):
                return t.matmul(out=self.P[3][:, :], lhsT=self.ONEF[0:1, :],
                                rhs=self.ROWV[0:1, 0:512], start=True, stop=True)

            @self.step("v", after={"t": S.cnt["t"]})
            def _(v):
                return v.tensor_tensor(out=W1[:, KSF:KSF + 512], in0=self.T0[:, 0:512],
                                       in1=self.P[3][:, :], op=AL.mult)
            @self.step("s", after={"v": S.cnt["v"]})
            def _(s, t0c=t0c):
                return s.activation(out=self.T0[:, 0:512], in_=qh[:, t0c:t0c + 512], func=AF.Exp)

            @self.step("t", after={"s": S.cnt["s"]})
            def _(t):
                return t.matmul(out=self.P[2][0:1, :], lhsT=self.ONEF[:, 0:1],
                                rhs=self.T0[:, 0:512], start=True, stop=True)

            @self.step("v", after={"t": S.cnt["t"]})
            def _(v):
                v.reciprocal(out=self.ROWV[0:1, 0:512], in_=self.P[2][0:1, :])
                v.drain()
                v.tensor_scalar(out=self.ROWV[0:1, 0:512], in0=self.ROWV[0:1, 0:512],
                                scalar1=1.0 / SQD, scalar2=None, op0=AL.mult)
                return v.drain()

            @self.step("t", after={"v": S.cnt["v"]})
            def _(t):
                return t.matmul(out=self.P[3][:, :], lhsT=self.ONEF[0:1, :],
                                rhs=self.ROWV[0:1, 0:512], start=True, stop=True)

            @self.step("v", after={"t": S.cnt["t"]})
            def _(v):
                return v.tensor_tensor(out=W1[:, QSF:QSF + 512], in0=self.T0[:, 0:512],
                                       in1=self.P[3][:, :], op=AL.mult)
            @self.step("s", after={"v": S.cnt["v"]})
            def _(s, t0c=t0c):
                return s.activation(out=self.T1[:, 0:512], in_=kh[:, t0c:t0c + 512],
                                    func=AF.Exp, scale=-1.0)

            @self.step("v", after={"s": S.cnt["s"]})
            def _(v):
                return v.tensor_scalar(out=self.T1[:, 0:512], in0=self.T1[:, 0:512],
                                       scalar1=1.0, scalar2=None, op0=AL.add)

            @self.step("s", after={"v": S.cnt["v"]})
            def _(s):
                return s.activation(out=self.T1[:, 0:512], in_=self.T1[:, 0:512],
                                    func=AF.Ln)

            @self.step("v", after={"s": S.cnt["s"]})
            def _(v):
                v.scalar_tensor_tensor(out=W1[:, KBF:KBF + 512], in0=self.T1[:, 0:512],
                                       scalar=-1.0 / GATE_NORM, in1=W1[:, KSF:KSF + 512],
                                       op0=AL.mult, op1=AL.mult)
                return v.tensor_scalar(out=self.SBB[:, 0:512], in0=self.T1[:, 0:512],
                                       scalar1=-1.0 / GATE_NORM, scalar2=None, op0=AL.mult)
            @self.step("t", after={"v": S.cnt["v"]})
            def _(t):
                last = None
                for i in range(8):
                    pp = self.PT if i < 4 else self.PT2
                    last = t.transpose(out=pp[0:64, (i % 4) * 128:(i % 4) * 128 + 128],
                                       in_=self.SBB[:, i * 64:(i + 1) * 64],
                                       identity=self.EYEB[:, :])
                return last

            @self.step("v", after={"t": S.cnt["t"]})
            def _(v):
                v.tensor_copy(out=W1[0:64, BT:BT + 512], in_=self.PT[0:64, :])
                return v.tensor_copy(out=W1[0:64, BT + 512:BT + 1024], in_=self.PT2[0:64, :])

            @self.step("t", after={"v": S.cnt["v"]})
            def _(t):
                last = None
                for i in range(8):
                    pp = self.PT if i < 4 else self.PT2
                    last = t.transpose(out=pp[0:64, (i % 4) * 128:(i % 4) * 128 + 128],
                                       in_=W1[:, KSF + i * 64: KSF + (i + 1) * 64],
                                       identity=self.EYEB[:, :])
                return last

            @self.step("v", after={"t": S.cnt["t"]})
            def _(v):
                v.tensor_copy(out=W1[0:64, KST:KST + 512], in_=self.PT[0:64, :])
                return v.tensor_copy(out=W1[0:64, KST + 512:KST + 1024], in_=self.PT2[0:64, :])
            blk0 = t0c // 1024
            coff = t0c % 1024
            tqv = self.dma("q", self.SBB[:, 1536:2048],
                           self.hstage[:, blk0 * 3072 + 2048 + coff:
                                       blk0 * 3072 + 2048 + coff + 512],
                           after={"v": S.cnt["v"]})

            @self.step("t", after={"q": tqv, "v": S.cnt["v"]})
            def _(t):
                last = None
                for i in range(8):
                    pp = self.PT if i < 4 else self.PT2
                    last = t.transpose(out=pp[0:64, (i % 4) * 128:(i % 4) * 128 + 128],
                                       in_=self.SBB[:, 1536 + i * 64: 1536 + (i + 1) * 64],
                                       identity=self.EYEB[:, :])
                return last

            @self.step("v", after={"t": S.cnt["t"]})
            def _(v):
                v.tensor_copy(out=W1[0:64, VB:VB + 512], in_=self.PT[0:64, :])
                v.tensor_copy(out=W1[0:64, VB + 512:VB + 1024], in_=self.PT2[0:64, :])
                v.drain()
                v.tensor_tensor(out=W1[0:64, VB:VB + 1024], in0=W1[0:64, VB:VB + 1024],
                                in1=W1[0:64, BT:BT + 1024], op=AL.mult)
                return v.tensor_tensor(out=W1[0:64, KBT:KBT + 1024], in0=W1[0:64, KST:KST + 1024],
                                       in1=W1[0:64, BT:BT + 1024], op=AL.mult)
            @self.step("t", after={"v": S.cnt["v"]})
            def _(t):
                last = None
                for i in range(8):
                    t.matmul(out=self.P[0][0:64, i * 64:(i + 1) * 64],
                             lhsT=W1[:, KBF + i * 64: KBF + (i + 1) * 64],
                             rhs=W1[:, KSF + i * 64: KSF + (i + 1) * 64],
                             start=(i == 0), stop=(i == 7), skip_group_check=True)
                    last = t.matmul(out=self.P[1][0:64, i * 64:(i + 1) * 64],
                                    lhsT=W1[:, KSF + i * 64: KSF + (i + 1) * 64],
                                    rhs=W1[:, KBF + i * 64: KBF + (i + 1) * 64],
                                    start=(i == 0), stop=(i == 7), skip_group_check=True)
                return last

            @self.step("v", after={"t": S.cnt["t"]})
            def _(v):
                v.tensor_tensor(out=self.T0[0:64, 0:512], in0=self.P[0][0:64, :],
                                in1=self.MSK[:, 0:512], op=AL.mult)
                v.tensor_tensor(out=self.T1[0:64, 0:512], in0=self.P[1][0:64, :],
                                in1=self.MSK[:, 512:1024], op=AL.mult)
                v.drain()
                v.tensor_tensor(out=W1[0:64, MOFF:MOFF + 512], in0=self.T0[0:64, 0:512],
                                in1=self.MSK[:, 3 * 512:4 * 512], op=AL.add)
                v.tensor_tensor(out=W1[0:64, MTOFF:MTOFF + 512], in0=self.T1[0:64, 0:512],
                                in1=self.MSK[:, 3 * 512:4 * 512], op=AL.add)
                v.drain()
                v.scalar_tensor_tensor(out=W1[0:64, XOFF:XOFF + 512], in0=W1[0:64, MTOFF:MTOFF + 512],
                                       scalar=-1.0, in1=self.MSK[:, 4 * 512:5 * 512],
                                       op0=AL.mult, op1=AL.add)
                v.scalar_tensor_tensor(out=W1[0:64, ZOFF:ZOFF + 512], in0=W1[0:64, MOFF:MOFF + 512],
                                       scalar=-1.0, in1=self.MSK[:, 4 * 512:5 * 512],
                                       op0=AL.mult, op1=AL.add)
                return v.drain()
            for it in range(5):
                @self.step("t", after={"v": S.cnt["v"]})
                def _(t):
                    last = None
                    for i in range(8):
                        sl = slice(i * 64, (i + 1) * 64)
                        t.matmul(out=self.P[0][0:64, sl],
                                 lhsT=W1[0:64, MOFF + i * 64: MOFF + (i + 1) * 64],
                                 rhs=W1[0:64, XOFF + i * 64: XOFF + (i + 1) * 64],
                                 start=(i == 0), stop=(i == 7), skip_group_check=True)
                        last = t.matmul(out=self.P[1][0:64, sl],
                                        lhsT=W1[0:64, XOFF + i * 64: XOFF + (i + 1) * 64],
                                        rhs=W1[0:64, MOFF + i * 64: MOFF + (i + 1) * 64],
                                        start=(i == 0), stop=(i == 7), skip_group_check=True)
                    return last

                @self.step("v", after={"t": S.cnt["t"]})
                def _(v):
                    v.scalar_tensor_tensor(out=W1[0:64, WOFF:WOFF + 512], in0=self.P[0][0:64, :],
                                           scalar=-1.0, in1=self.MSK[:, 4 * 512:5 * 512],
                                           op0=AL.mult, op1=AL.add)
                    v.scalar_tensor_tensor(out=self.SBB[0:64, 0:512], in0=self.P[1][0:64, :],
                                           scalar=-1.0, in1=self.MSK[:, 4 * 512:5 * 512],
                                           op0=AL.mult, op1=AL.add)
                    return v.drain()

                @self.step("t", after={"v": S.cnt["v"]})
                def _(t):
                    last = None
                    for i in range(8):
                        sl = slice(i * 64, (i + 1) * 64)
                        t.matmul(out=self.P[0][0:64, sl],
                                 lhsT=W1[0:64, ZOFF + i * 64: ZOFF + (i + 1) * 64],
                                 rhs=W1[0:64, WOFF + i * 64: WOFF + (i + 1) * 64],
                                 start=(i == 0), stop=(i == 7), skip_group_check=True)
                        last = t.matmul(out=self.P[1][0:64, sl],
                                        lhsT=W1[0:64, WOFF + i * 64: WOFF + (i + 1) * 64],
                                        rhs=self.SBB[0:64, sl] if False else W1[0:64, ZOFF + i * 64: ZOFF + (i + 1) * 64],
                                        start=(i == 0), stop=(i == 7), skip_group_check=True)
                    return last

                @self.step("v", after={"t": S.cnt["t"]})
                def _(v):
                    v.tensor_copy(out=W1[0:64, XOFF:XOFF + 512], in_=self.P[0][0:64, :])
                    v.tensor_copy(out=W1[0:64, ZOFF:ZOFF + 512], in_=self.P[1][0:64, :])
                    return v.drain()
            @self.step("t", after={"v": S.cnt["v"]})
            def _(t):
                last = None
                for i in range(8):
                    t.matmul(out=self.P[i // 4][0:64, (i % 4) * 128:(i % 4) * 128 + 128],
                             lhsT=W1[0:64, XOFF + i * 64: XOFF + (i + 1) * 64],
                             rhs=W1[0:64, VB + i * 128: VB + (i + 1) * 128],
                             start=(i % 4 == 0), stop=(i % 4 == 3), skip_group_check=True)
                    t.matmul(out=self.P[2][:, i * 64:(i + 1) * 64],
                             lhsT=W1[0:64, KBT + i * 128: KBT + (i + 1) * 128],
                             rhs=W1[0:64, XOFF + i * 64: XOFF + (i + 1) * 64],
                             start=(i == 0), stop=(i == 7), skip_group_check=True)
                    last = t.matmul(out=self.P[3][0:64, i * 64:(i + 1) * 64],
                                    lhsT=W1[:, KSF + i * 64: KSF + (i + 1) * 64],
                                    rhs=W1[:, QSF + i * 64: QSF + (i + 1) * 64],
                                    start=(i == 0), stop=(i == 7), skip_group_check=True)
                return last

            @self.step("v", after={"t": S.cnt["t"]})
            def _(v):
                v.tensor_copy(out=W1[0:64, UOFF:UOFF + 512], in_=self.P[0][0:64, :])
                v.tensor_copy(out=W1[0:64, UOFF + 512:UOFF + 1024], in_=self.P[1][0:64, :])
                v.tensor_scalar(out=W1[:, WTN:WTN + 512], in0=self.P[2][:, :],
                                scalar1=-1.0, scalar2=None, op0=AL.mult)
                v.tensor_tensor(out=W1[0:64, ATN:ATN + 512], in0=self.P[3][0:64, :],
                                in1=self.MSK[:, 2 * 512:3 * 512], op=AL.mult)
                return v.drain()
            for i in range(8):
                c = b * 8 + i

                @self.step("t", after={"v": S.cnt["v"]})
                def _(t, i=i):
                    return t.matmul(out=self.P[4][0:64, 0:128],
                                    lhsT=W1[:, WTN + i * 64: WTN + (i + 1) * 64],
                                    rhs=self.SBB[:, 1024:1152],
                                    start=True, stop=True, skip_group_check=True)

                @self.step("v", after={"t": S.cnt["t"]})
                def _(v, i=i):
                    return v.tensor_tensor(out=W1[0:64, UOFF + i * 128: UOFF + (i + 1) * 128],
                                           in0=self.P[4][0:64, 0:128],
                                           in1=W1[0:64, UOFF + i * 128: UOFF + (i + 1) * 128],
                                           op=AL.add)

                @self.step("t", after={"v": S.cnt["v"]})
                def _(t, i=i):
                    t.matmul(out=self.P[5][0:64, 0:128],
                             lhsT=W1[:, QSF + i * 64: QSF + (i + 1) * 64],
                             rhs=self.SBB[:, 1024:1152],
                             start=True, stop=False, skip_group_check=True)
                    t.matmul(out=self.P[5][0:64, 0:128],
                             lhsT=W1[0:64, ATN + i * 64: ATN + (i + 1) * 64],
                             rhs=W1[0:64, UOFF + i * 128: UOFF + (i + 1) * 128],
                             start=False, stop=True, skip_group_check=True)
                    return t.matmul(out=self.P[4][:, 0:128],
                                    lhsT=W1[0:64, KST + i * 64 * 2 - i * 64: KST + i * 128 + 128][0:64, 0:128]
                                    if False else W1[0:64, KST + i * 128: KST + (i + 1) * 128],
                                    rhs=W1[0:64, UOFF + i * 128: UOFF + (i + 1) * 128],
                                    start=True, stop=True, skip_group_check=True)

                @self.step("v", after={"t": S.cnt["t"]})
                def _(v, i=i):
                    v.tensor_tensor(out=self.AF[:, 0:128], in0=self.AF[:, 0:128],
                                    in1=self.P[4][:, 0:128], op=AL.add)
                    v.drain()
                    v.tensor_copy(out=self.SBB[:, 1024:1152], in_=self.AF[:, 0:128])
                    v.tensor_copy(out=self.WS0[0:64, i * 128:(i + 1) * 128],
                                  in_=self.P[5][0:64, 0:128])
                    return v.drain()
            @self.step("t", after={"v": S.cnt["v"]})
            def _(t):
                last = None
                for i in range(8):
                    last = t.transpose(out=self.PT[:, i * 64:(i + 1) * 64],
                                       in_=self.WS0[0:64, i * 128:(i + 1) * 128],
                                       identity=self.EYEB[0:64, 0:64])
                return last

            @self.step("v", after={"t": S.cnt["t"]})
            def _(v):
                return v.tensor_scalar(out=self.SBB[:, 1536:2048], in0=self.PT[:, :],
                                       scalar1=0.5, scalar2=None, op0=AL.mult)
            self.dma("gd", self.olin_sp[hh * 128:(hh + 1) * 128, b * 512:(b + 1) * 512],
                     self.SBB[:, 1536:2048], after={"v": S.cnt["v"]})

        @self.step("v", after={"v": S.cnt["v"], "gd": S.cnt["gd"]})
        def _(v):
            v.memset(self.ROWV[0:1, 520:521], 0.0)
            return v.drain()

        @self.step("s", after={"v": S.cnt["v"]})
        def _(s):
            return s.activation(out=self.WS0[:, :], in_=kh, func=AF.Square)
        for w in range(4):
            @self.step("t", after={"s": S.cnt["s"], "v": S.cnt["v"]})
            def _(t, w=w):
                last = None
                for j in range(4):
                    last = t.matmul(out=self.P[j][0:1, :], lhsT=self.ONEB[:, 0:1],
                                    rhs=self.WS0[:, (w * 4 + j) * 512:(w * 4 + j + 1) * 512],
                                    start=True, stop=True, skip_group_check=True)
                return last

            @self.step("v", after={"t": S.cnt["t"]})
            def _(v, w=w):
                for j in range(4):
                    v.tensor_copy(out=self.T0[0:1, j * 512:(j + 1) * 512], in_=self.P[j][0:1, :])
                v.drain()
                v.tensor_reduce(out=self.ROWV[0:1, 521:522], in_=self.T0[0:1, 0:2048],
                                axis=AX, op=AL.max)
                v.drain()
                v.tensor_tensor(out=self.ROWV[0:1, 520:521], in0=self.ROWV[0:1, 520:521],
                                in1=self.ROWV[0:1, 521:522], op=AL.max)
                return v.drain()

        @self.step("s", after={"v": S.cnt["v"]})
        def _(s):
            return s.activation(out=self.WS0[:, :], in_=qh, func=AF.Square)
        for w in range(4):
            @self.step("t", after={"s": S.cnt["s"], "v": S.cnt["v"]})
            def _(t, w=w):
                last = None
                for j in range(4):
                    last = t.matmul(out=self.P[j][0:1, :], lhsT=self.ONEB[:, 0:1],
                                    rhs=self.WS0[:, (w * 4 + j) * 512:(w * 4 + j + 1) * 512],
                                    start=True, stop=True, skip_group_check=True)
                return last

            @self.step("v", after={"t": S.cnt["t"]})
            def _(v, w=w):
                for j in range(4):
                    v.tensor_copy(out=self.T0[0:1, j * 512:(j + 1) * 512], in_=self.P[j][0:1, :])
                v.drain()
                return v.tensor_scalar(out=self.T1[0:1, 0:2048], in0=self.T0[0:1, 0:2048],
                                       scalar1=self.ROWV[0:1, 520:521], scalar2=None, op0=AL.mult)

            @self.step("s", after={"v": S.cnt["v"]})
            def _(s):
                return s.activation(out=self.T0[0:1, 0:2048], in_=self.T1[0:1, 0:2048],
                                    func=AF.Sqrt)

            @self.step("v", after={"s": S.cnt["s"]})
            def _(v, w=w):
                return v.tensor_scalar(out=self.SBB[0:1, 0:2048],
                                       in0=self.T0[0:1, 0:2048],
                                       scalar1=-1.0, scalar2=None, op0=AL.mult)
            self.dma("gd", self.mrow_sp[hh:hh + 1, w * 2048:(w + 1) * 2048],
                     self.SBB[0:1, 0:2048], after={"v": S.cnt["v"]})

        for qg in range(16):
            njt = 4 * (qg + 1)
            tbm = self.dma("q", self.BIASR[0:1, 1024:1536],
                           self.mrow_sp[hh:hh + 1, qg * 512:(qg + 1) * 512],
                           after={"t": S.cnt["t"], "gd": S.cnt["gd"]})
            shist = [S.cnt["s"], S.cnt["s"]]
            for jt in range(njt):
                pb = self.P[0] if jt % 2 == 0 else self.P[3]

                @self.step("t", after={"s": shist[jt % 2], "q": tbm})
                def _(t, jt=jt, qg=qg, pb=pb):
                    t.matmul(out=pb[:, :], lhsT=kh[:, jt * 128:(jt + 1) * 128],
                             rhs=qh[:, qg * 512:(qg + 1) * 512],
                             start=True, stop=False, skip_group_check=True)
                    return t.matmul(out=pb[:, :], lhsT=self.ONEB[0:1, :],
                                    rhs=self.BIASR[0:1, 1024:1536],
                                    start=False, stop=True, skip_group_check=True)

                @self.step("s", after={"t": S.cnt["t"]})
                def _(s, jt=jt, pb=pb):
                    return s.activation(out=self.SBB[:, (jt % 2) * 512:(jt % 2) * 512 + 512],
                                        in_=pb[:, :], func=AF.Exp, scale=1.0 / SQD)
                shist[jt % 2] = S.cnt["s"]
                if jt >= 4 * qg:
                    @self.step("v", after={"s": S.cnt["s"]})
                    def _(v, r=jt - 4 * qg, jt=jt):
                        sb = self.SBB[:, (jt % 2) * 512:(jt % 2) * 512 + 512]
                        return v.tensor_tensor(out=sb, in0=sb,
                                               in1=self.BMASK[:, r * 512:(r + 1) * 512],
                                               op=AL.mult)

                @self.step("t", after={"s": S.cnt["s"], "v": S.cnt["v"]})
                def _(t, jt=jt, njt=njt):
                    sb = self.SBB[:, (jt % 2) * 512:(jt % 2) * 512 + 512]
                    t.matmul(out=self.P[2][0:1, :], lhsT=self.ONEB[:, 0:1],
                             rhs=sb, start=(jt == 0), stop=(jt == njt - 1),
                             skip_group_check=True)
                    return t.matmul(out=self.P[1][:, :],
                                    lhsT=self.AB[:, 16384 + jt * 128: 16384 + (jt + 1) * 128],
                                    rhs=sb, start=(jt == 0), stop=(jt == njt - 1),
                                    skip_group_check=True)
            @self.step("v", after={"t": S.cnt["t"]})
            def _(v):
                v.reciprocal(out=self.ROWV[0:1, 0:512], in_=self.P[2][0:1, :])
                return v.drain()

            @self.step("t", after={"v": S.cnt["v"]})
            def _(t):
                return t.matmul(out=self.P[4][:, :], lhsT=self.ONEF[0:1, :],
                                rhs=self.ROWV[0:1, 0:512], start=True, stop=True)
            tq = self.dma("q", self.SBB[:, 1024:1536],
                          self.olin_sp[hh * 128:(hh + 1) * 128, qg * 512:(qg + 1) * 512],
                          after={"v": S.cnt["v"]})

            @self.step("v", after={"t": S.cnt["t"], "q": tq})
            def _(v):
                v.tensor_copy(out=self.T0[:, 0:512], in_=self.P[1][:, :])
                v.drain()
                v.scalar_tensor_tensor(out=self.T1[:, 0:512], in0=self.T0[:, 0:512],
                                       scalar=0.5, in1=self.P[4][:, :],
                                       op0=AL.mult, op1=AL.mult)
                v.drain()
                return v.tensor_tensor(out=self.SBB[:, 1536:2048], in0=self.T1[:, 0:512],
                                       in1=self.SBB[:, 1024:1536], op=AL.add)
            self.dma("gd", self.a1_in[hh * 128:(hh + 1) * 128, qg * 512:(qg + 1) * 512],
                     self.SBB[:, 1536:2048], after={"v": S.cnt["v"]})


def host_constants(core):
    c = {}
    c["c_eye"] = np.eye(128, dtype=np.float32)
    c["c_ones"] = np.ones((128, 128), dtype=np.float32)
    sl = np.tril(np.ones((64, 64), np.float32), -1)
    su = np.triu(np.ones((64, 64), np.float32), 1)
    iu = np.triu(np.ones((64, 64), np.float32), 0)
    e64 = np.eye(64, dtype=np.float32)
    c["c_msk"] = np.concatenate([np.tile(sl, (1, 8)), np.tile(su, (1, 8)),
                                 np.tile(iu, (1, 8)), np.tile(e64, (1, 8)),
                                 np.tile(2 * e64, (1, 8))], axis=1)
    j = np.arange(128)[:, None]
    q = np.arange(512)[None, :]
    bm = [(j + r * 128 <= q).astype(np.float32) for r in range(4)]
    c["c_bmask"] = np.concatenate(bm, axis=1)
    pos = np.arange(TL, dtype=np.float64) + core * TL
    inv = 1.0 / (10000.0 ** (np.arange(0, DH, 2, dtype=np.float64) / DH))
    ang = pos[:, None] * inv[None, :]
    c["c_rsin"] = np.sin(ang).astype(np.float32)
    c["c_rcos"] = np.cos(ang).astype(np.float32)
    sel = np.zeros((128, NCORE), np.float32)
    sel[:, :core] = 1.0
    c["c_sel"] = sel
    return c


def qk_perm():
    p = []
    for h in range(H):
        base = h * DH
        p.extend(range(base, base + DH, 2))
        p.extend(range(base + 1, base + DH, 2))
    return np.array(p, dtype=np.int64)


def host_inputs(core, inputs):
    m = {}
    m["x"] = inputs["L0_attn_x"][0, core * TL:(core + 1) * TL, :]
    m["l0_wq"] = inputs["L0_attn_Wq"]
    m["l0_wk"] = inputs["L0_attn_Wk"]
    m["l0_wv"] = inputs["L0_attn_Wv"]
    m["l0_wo"] = inputs["L0_attn_Wo"]
    m["l0_wg"] = inputs["L0_ffn_w_gate"]
    m["l0_wu"] = inputs["L0_ffn_w_up"]
    m["l0_wd"] = inputs["L0_ffn_w_down"]
    m["l1_wq"] = inputs["L1_attn_Wq"]
    m["l1_wk"] = inputs["L1_attn_Wk"]
    m["l1_wv"] = inputs["L1_attn_Wv"]
    m["l1_wg"] = inputs["L1_attn_Wg"]
    m["l1_wo"] = inputs["L1_attn_Wo"]
    perm = qk_perm()
    b = np.stack([np.asarray(inputs["L1_attn_bq"])[perm],
                  np.asarray(inputs["L1_attn_bk"])[perm],
                  np.asarray(inputs["L1_attn_bv"]),
                  np.asarray(inputs["L1_attn_bg"]),
                  np.asarray(inputs["L1_attn_bo"])], axis=0).astype(np.float32)
    m["l1_b"] = b
    m["l1_wnorm"] = np.asarray(inputs["L1_ffn_w_norm"]).reshape(1, D)
    m["l1_wgv"] = inputs["L1_ffn_w_gv"]
    m["l1_wout"] = inputs["L1_ffn_w_out"]
    m.update(host_constants(core))
    return {k: np.ascontiguousarray(np.asarray(v, dtype=np.float32)) for k, v in m.items()}


def build_full(cfg=None):
    cfg = cfg or CFG
    kb = KB(cfg)
    kb.plan_setup()
    ph = cfg.get("phases", ("l0a", "l0f", "l1a", "l1f"))
    need_prep = bool({"l0a", "l0f", "l1a", "l1f"} & set(ph))
    if need_prep:
        kb.plan_weight_prep()
    if "l0a" in ph:
        kb.plan_l0_attn()
    if "l0f" in ph:
        kb.plan_l0_ffn()
    if "l1a" in ph:
        kb.plan_l1_attn()
    if "l1f" in ph:
        kb.plan_l1_ffn(last_block=True)
    else:
        for tc in range(NTC):
            tq = kb.dma("q", kb.T0[:, 0:D], kb.xstream[tc * 128:(tc + 1) * 128, :],
                        after=kb.S.all() if tc == 0 else {"v": kb.S.cnt["v"]})

            @kb.step("v", after={"q": tq})
            def _(v):
                return v.tensor_copy(out=kb.T1[:, 0:D], in_=kb.T0[:, 0:D])
            kb.dma("q", kb.y_out[tc * 128:(tc + 1) * 128, :], kb.T1[:, 0:D],
                   after={"v": kb.S.cnt["v"]})
    kb.plan_output_wait()
    nc = kb.finalize()
    return nc


def kernel(**inputs):
    nc = build_full(CFG)
    in_maps = [host_inputs(c, inputs) for c in range(NCORE)]
    res = run_bass_kernel_spmd(nc, in_maps, core_ids=list(range(NCORE)))
    out = np.concatenate([np.asarray(r["y"]) for r in res.results], axis=0)
    return out.reshape(1, T, D).astype(np.float32)
